# version 08 (final)
# baseline (speedup 1.0000x reference)
import contextlib
import math
import numpy as np
import ml_dtypes
import concourse.bass as bass
import concourse.mybir as mybir
from concourse.bass_utils import run_bass_kernel_spmd

F32 = mybir.dt.float32
BF16 = mybir.dt.bfloat16
U8 = mybir.dt.uint8
AF = mybir.ActivationFunctionType
ALU = mybir.AluOpType
BF = ml_dtypes.bfloat16

T = 2048
D = 1024
DC = D // 128
NT = 512

N_SLOTS = 40

ENGINES = ("sync", "scalar", "vector", "tensor", "gpsimd")


class P:

    def __init__(self, nc):
        self.nc = nc
        self.queues = {e: [] for e in ENGINES}
        self.cnt = {}
        self.hwm = {e: {} for e in ENGINES}
        for e in ENGINES:
            self.cnt["C_" + e] = 0
            self.cnt["D_" + e] = 0

    def _waits(self, eng, deps):
        need = {}
        for d in deps:
            if d is None:
                continue
            sem, val, peng, is_dma = d
            if peng == eng and not is_dma and eng == "tensor":
                continue
            if val <= self.hwm[eng].get(sem, 0):
                continue
            need[sem] = max(need.get(sem, 0), val)
        for sem, val in need.items():
            self.hwm[eng][sem] = val
        return need

    def do(self, eng, fn, deps=()):
        need = self._waits(eng, deps)
        semn = "C_" + eng
        self.cnt[semn] += 1
        val = self.cnt[semn]
        self.queues[eng].append(("do", fn, need, semn, val))
        return (semn, val, eng, False)

    def barrier(self):
        snap = {s: v for s, v in self.cnt.items() if v > 0}
        for eng in ENGINES:
            need = {}
            for s, v in snap.items():
                if s == "C_" + eng:
                    continue
                if v <= self.hwm[eng].get(s, 0):
                    continue
                need[s] = v
                self.hwm[eng][s] = v
            if need:
                self.queues[eng].append(("waits", None, need, None, None))

    def dma(self, eng, out, in_, deps=(), sem="misc", **kw):
        need = self._waits(eng, deps)
        semn = "D_" + sem
        if semn not in self.cnt:
            self.cnt[semn] = 0
        self.cnt[semn] += 16
        val = self.cnt[semn]
        self.queues[eng].append(("dma", (out, in_, kw), need, semn, val))
        return (semn, val, eng, True)

    def finalize(self, block, out_tokens):
        sems = {}
        self._sem_stack = contextlib.ExitStack()
        for name in self.cnt:
            sems[name] = self._sem_stack.enter_context(self.nc.semaphore(name))
        for eng in ENGINES:
            q = self.queues[eng]
            if not q and eng != "gpsimd":
                continue

            def mk(eng=eng, q=q):
                def body(engobj):
                    for kind, payload, waits, semn, val in q:
                        for s, v in waits.items():
                            engobj.wait_ge(sems[s], v)
                        if kind == "waits":
                            continue
                        if kind == "do":
                            inst = payload(engobj)
                            inst.then_inc(sems[semn], 1)
                        else:
                            out, in_, kw = payload
                            engobj.dma_start(out=out, in_=in_, **kw).then_inc(
                                sems[semn], 16)
                    if eng == "gpsimd":
                        fin = {}
                        for d in out_tokens:
                            sem, v, _, _ = d
                            fin[sem] = max(fin.get(sem, 0), v)
                        for s, v in fin.items():
                            engobj.wait_ge(sems[s], v)
                return body
            getattr(block, eng)(mk())


def pack_lhsT(W):
    K, M = W.shape
    assert K % 128 == 0 and M % 128 == 0, (K, M)
    Kc, Mb = K // 128, M // 128
    return np.ascontiguousarray(
        W.reshape(Kc, 128, Mb, 128).transpose(1, 2, 0, 3).reshape(128, Mb * Kc * 128))


def pack_cols(v):
    M = v.shape[0]
    assert M % 128 == 0
    return np.ascontiguousarray(v.reshape(M // 128, 128).T)


def pad_to(a, *shape):
    out = np.zeros(shape, a.dtype)
    out[tuple(slice(0, s) for s in a.shape)] = a
    return out


class Bufs:
    def __init__(self, nc, stack):
        e = stack.enter_context
        self.arena = e(nc.sbuf_tensor("arena", [128, N_SLOTS * T], BF16))
        self.wb = [e(nc.sbuf_tensor(f"wb{i}", [128, 4096], BF16)) for i in range(2)]
        self.tmpf = e(nc.sbuf_tensor("tmpf", [128, 2 * T], F32))
        self.stg = e(nc.sbuf_tensor("stg", [128, 2048], BF16))
        self.cols = e(nc.sbuf_tensor("cols", [128, 320], F32))
        self.ident_bf = e(nc.sbuf_tensor("c_ident_bf_sb", [128, 128], BF16))
        self.ident_f32 = e(nc.sbuf_tensor("c_ident_f32_sb", [128, 128], F32))
        self.ones_row = e(nc.sbuf_tensor("c_ones_row_sb", [1, 128], F32))
        self.cpool = e(nc.sbuf_tensor("c_pool_sb", [128, len(CONST_LIST)], F32))
        self.onescol_bf = e(nc.sbuf_tensor("c_onescol_bf_sb", [128, 1], BF16))
        self.ident64 = e(nc.sbuf_tensor("c_ident64_sb", [128, 64], BF16))
        self.rot64 = e(nc.sbuf_tensor("c_rot64_sb", [128, 64], BF16))
        self.selrow = e(nc.sbuf_tensor("c_selrow_sb", [128, 128], F32))
        self.rot128 = e(nc.sbuf_tensor("c_rot128_sb", [128, 128], BF16))
        self.psA = e(nc.psum_tensor("psA", [128, 2048], F32))
        self.ps4 = e(nc.psum_tensor("ps4", [128, 512], F32))
        self.ps5 = e(nc.psum_tensor("ps5", [128, 512], F32))
        self.ps6 = e(nc.psum_tensor("ps6", [128, 512], F32))
        self.psT = e(nc.psum_tensor("psT", [128, 1024], BF16))

    def cp(self, value):
        i = CONST_LIST.index(value)
        return self.cpool.ap()[:, i:i + 1]

    @property
    def rowa(self):
        return self.tmpf.ap()[0:1, 0:T]

    @property
    def rowb(self):
        return self.tmpf.ap()[0:1, T:2 * T]


    def slot(self, i, n=1):
        return self.arena.ap()[:, i * T:(i + n) * T]

    def slot_f32(self, i, n=1):
        return self.arena.ap()[:, i * T:(i + 2 * n) * T].bitcast(F32)

    def bank(self, i):
        return self.psA.ap()[:, i * NT:(i + 1) * NT]


CONST_LIST = [1e-5, 1.0, 1e-12, 1e-24, 0.5, -1e30, 1e-30, 2.0, -6.0, 6.0, 1e-5 + 1e-10]


def host_consts():
    ident = np.eye(128, dtype=np.float32)
    stat = np.zeros((128, 64), np.float32)
    stat[:, 0] = 1.0
    stat[:, 32] = 1.0
    pool = np.tile(np.asarray(CONST_LIST, np.float32)[None, :], (128, 1))
    return {
        "c_ident_bf": ident.astype(BF),
        "c_ident_f32": ident,
        "c_ones_row": np.ones((1, 128), np.float32),
        "c_onescol_bf": np.ones((128, 1), BF),
        "c_stat_lhsT": stat.astype(BF),
        "c_pool": pool,
        "c_ident64": np.tile(np.eye(64, dtype=np.float32), (2, 1)).astype(BF),
        "c_rot64": _host_rot64(),
        "c_selrow": _host_selrow(),
        "c_rot128": _host_rot128(),
    }


def _host_selrow():
    m = np.zeros((128, 128), np.float32)
    m[0, 0:64] = 1.0
    m[32, 64:128] = 1.0
    return m


def _host_rot128():
    M = np.zeros((64, 64), np.float32)
    for i in range(32):
        M[i, i + 32] = -1.0
        M[i + 32, i] = 1.0
    out = np.zeros((128, 128), np.float32)
    out[0:64, 0:64] = M.T
    out[64:128, 64:128] = M.T
    return out.astype(BF)


def _host_rot64():
    M = np.zeros((64, 64), np.float32)
    for i in range(32):
        M[2 * i, 2 * i + 1] = -1.0
        M[2 * i + 1, 2 * i] = 1.0
    return np.tile(M.T, (2, 1)).astype(BF)


class Prog:

    def __init__(self):
        self.nc = bass.Bass("TRN2", target_bir_lowering=False)
        self.stack = contextlib.ExitStack()
        self.p = P(self.nc)
        self.B = Bufs(self.nc, self.stack)
        self.stat_lhsT = self.stack.enter_context(
            self.nc.sbuf_tensor("c_stat_lhsT_sb", [128, 64], BF16))
        self.dram_in = {}
        self.dram_out = {}
        self.wb_last_use = [None, None]
        self.out_tokens = []
        self.const_toks = []

    def din(self, name, shape, dtype=F32):
        if name not in self.dram_in:
            self.dram_in[name] = self.nc.dram_tensor(
                name, list(shape), dtype, kind="ExternalInput").ap()
        return self.dram_in[name]

    def dout(self, name, shape, dtype=F32):
        if name not in self.dram_out:
            self.dram_out[name] = self.nc.dram_tensor(
                name, list(shape), dtype, kind="ExternalOutput").ap()
        return self.dram_out[name]

    def dscratch(self, name, shape, dtype):
        key = "scr_" + name
        if key not in self.dram_in:
            self.dram_in[key] = self.nc.dram_tensor(
                key, list(shape), dtype, kind="Internal").ap()
        return self.dram_in[key]

    def load_consts(self):
        B, p = self.B, self.p
        for name, sb in [("c_ident_bf", B.ident_bf), ("c_ident_f32", B.ident_f32),
                         ("c_ones_row", B.ones_row), ("c_onescol_bf", B.onescol_bf),
                         ("c_stat_lhsT", self.stat_lhsT), ("c_pool", B.cpool),
                         ("c_ident64", B.ident64), ("c_rot64", B.rot64),
                         ("c_selrow", B.selrow),
                         ("c_rot128", B.rot128)]:
            d = self.din(name, host_consts()[name].shape,
                         BF16 if host_consts()[name].dtype == BF else F32)
            self.const_toks.append(p.dma("sync", sb.ap()[:, :], d[:, :],
                                          sem="cst"))

    def finish(self):
        block = self.stack.enter_context(self.nc.Block())
        self.p.finalize(block, self.out_tokens)

    def close(self):
        self.stack.close()


def emit_ln(pr, x_dram, xn_slot, xb_slot, x2_slot, deps=(), eps=1e-5,
            gcol=None, bcol=None, n_chunks=DC, d_norm=None, shift=False,
            var_scale=1.0, var_bias=None, src_sbuf=None, src_deps=None,
            cols_dep=()):
    CT = list(pr.const_toks)
    CD = list(cols_dep)
    """LN over features of the DRAM fp32 stream x_dram [n_chunks*128, T].

    Loads x as bf16 into arena slots xb_slot.., squares into x2_slot..,
    PE-stats into psA (S1 row0, S2 row32), applies into xn_slot.. (bf16).
    gcol/bcol: optional per-feature gain/bias fp32 cols [128,1] per chunk.
    shift=True: read tokens shifted by -1 (x[t-1], zero at t=0): only valid
    when x_dram has a zero guard column (we instead handle via partial DMA).
    Returns list of per-chunk done tokens (xn ready).
    """
    B, p = pr.B, pr.p
    d_norm = d_norm or (n_chunks * 128)
    if src_sbuf is not None:
        xb = src_sbuf
    else:
        xb = [B.slot(xb_slot + c) for c in range(n_chunks)]
    x2 = [B.slot(x2_slot + c) for c in range(n_chunks)]
    xn = [B.slot(xn_slot + c) for c in range(n_chunks)]
    t_load = []
    for c in range(n_chunks):
        if src_sbuf is not None:
            t_load.append((src_deps or (lambda c: None))(c))
        else:
            t_load.append(p.dma("gpsimd", xb[c], x_dram[c * 128:(c + 1) * 128, :],
                                deps=deps, sem=f"lnc{c}"))
    t_sq = []
    for c in range(n_chunks):
        t_sq.append(p.do("scalar", lambda s, c=c: s.activation(
            out=x2[c], in_=xb[c], func=AF.Square), deps=[t_load[c]]))
    stat_t = []
    for ntile in range(T // NT):
        sl = slice(ntile * NT, (ntile + 1) * NT)
        tt = None
        for c in range(n_chunks):
            tt = p.do("tensor", lambda t, c=c, sl=sl, first=(c == 0),
                      last=(c == n_chunks - 1): t.matmul(
                B.psA[0:1, sl], B.onescol_bf.ap()[:, :], xb[c][:, sl],
                start=first, stop=last), deps=[t_load[c]] + CT)
        for c in range(n_chunks):
            tt = p.do("tensor", lambda t, c=c, sl=sl, first=(c == 0),
                      last=(c == n_chunks - 1): t.matmul(
                B.psA[32:33, sl], B.onescol_bf.ap()[:, :], x2[c][:, sl],
                start=first, stop=last), deps=[t_sq[c]])
        stat_t.append(tt)
    t_mu = p.do("vector", lambda v: v.tensor_scalar(
        out=B.rowa, in0=B.psA[0:1, 0:T], scalar1=1.0 / d_norm,
        scalar2=None, op0=ALU.mult), deps=stat_t)
    t_m2 = p.do("vector", lambda v: v.tensor_scalar(
        out=B.rowb, in0=B.psA[32:33, 0:T], scalar1=1.0 / d_norm,
        scalar2=None, op0=ALU.mult), deps=stat_t)
    mu2row = B.arena.ap()[0:1, xn_slot * T:(xn_slot + 2) * T].bitcast(F32)
    t_mu2 = p.do("scalar", lambda s: s.activation(
        out=mu2row, in_=B.rowa, func=AF.Square), deps=[t_mu, t_m2])
    t_var = p.do("vector", lambda v: v.tensor_sub(
        out=B.rowb, in0=B.rowb, in1=mu2row),
        deps=[t_mu2])
    vb = eps if var_bias is None else var_bias
    t_ln = p.do("scalar", lambda s: s.activation(
        out=B.rowb, in_=B.rowb, func=AF.Ln, bias=B.cp(vb)[0:1, :],
        scale=float(var_scale)), deps=[t_var] + CT)
    t_rstd = p.do("scalar", lambda s: s.activation(
        out=B.rowb, in_=B.rowb, func=AF.Exp, scale=-0.5),
        deps=[t_ln])
    done = [[] for _ in range(n_chunks)]
    prev_half_applies = []
    for half in range(2):
        hs = slice(half * 1024, (half + 1) * 1024)
        bts = []
        for q in range(2):
            psl = slice(q * NT, (q + 1) * NT)
            ssl = slice(half * 1024 + q * NT, half * 1024 + (q + 1) * NT)
            bts.append(p.do("tensor", lambda t, psl=psl, ssl=ssl: t.matmul(
                B.psA[0:128, psl], B.ones_row.ap()[:, :], B.rowa[:, ssl],
                start=True, stop=True),
                deps=[t_mu, t_rstd] + prev_half_applies + CT))
            bts.append(p.do("tensor", lambda t, psl=psl, ssl=ssl: t.matmul(
                B.psA[0:128, 1024 + psl.start:1024 + psl.stop],
                B.ones_row.ap()[:, :], B.rowb[:, ssl],
                start=True, stop=True), deps=[t_rstd] + prev_half_applies))
        for c in range(n_chunks):
            t1 = p.do("vector", lambda v, c=c, hs=hs: v.tensor_sub(
                out=x2[c][:, hs], in0=xb[c][:, hs], in1=B.psA[0:128, 0:1024]),
                deps=bts + [t_sq[c]])
            if gcol is None:
                t2 = p.do("vector", lambda v, c=c, hs=hs: v.tensor_mul(
                    out=xn[c][:, hs], in0=x2[c][:, hs], in1=B.psA[0:128, 1024:2048]),
                    deps=[t1])
            else:
                t1b = p.do("vector", lambda v, c=c, hs=hs: v.tensor_mul(
                    out=x2[c][:, hs], in0=x2[c][:, hs], in1=B.psA[0:128, 1024:2048]),
                    deps=[t1])
                if bcol is not None:
                    t2 = p.do("vector", lambda v, c=c, hs=hs: v.scalar_tensor_tensor(
                        out=xn[c][:, hs], in0=x2[c][:, hs], scalar=gcol(c),
                        in1=bcol(c).rearrange("p one -> p one ()").broadcast_to(
                            [128, 1, 1024]),
                        op0=ALU.mult, op1=ALU.add), deps=[t1b] + CD)
                else:
                    t2 = p.do("vector", lambda v, c=c, hs=hs: v.tensor_scalar(
                        out=xn[c][:, hs], in0=x2[c][:, hs], scalar1=gcol(c),
                        scalar2=None, op0=ALU.mult), deps=[t1b] + CD)
            done[c].append(t2)
            prev_half_applies = prev_half_applies + [t2]
    return [d[-1] for d in done], done


class WStream:

    def __init__(self, pr, dram, Kc, name=""):
        self.pr = pr
        self.dram = dram
        self.Kc = Kc
        if not hasattr(pr, "wb_last_use"):
            pr.wb_last_use = [None, None]
        self.load_tok = {}

    def load(self, mb, deps=()):
        b = mb % 2
        B, p = self.pr.B, self.pr.p
        dep = list(deps)
        if self.pr.wb_last_use[b] is not None:
            dep.append(self.pr.wb_last_use[b])
        tok = p.dma("gpsimd", B.wb[b].ap()[:, 0:self.Kc * 128],
                    self.dram[:, mb * self.Kc * 128:(mb + 1) * self.Kc * 128],
                    deps=dep, sem=f"wb{b}")
        self.load_tok[mb] = tok
        return tok

    def tile(self, mb, kc):
        return self.pr.B.wb[mb % 2].ap()[:, kc * 128:(kc + 1) * 128]

    def used(self, mb, tok):
        self.pr.wb_last_use[mb % 2] = tok


def mm_evac_stream(pr, *, Mb, Kc, ws, rhs, evac, deps_rhs=(), n_total=T,
                   n_tile=1024, psum_banks=(0, 1, 2, 3), extra_mm_deps=None,
                   prefetch=2, region_init=()):
    B, p = pr.B, pr.p
    assert n_tile in (512, 1024)
    if deps_rhs is None:
        deps_rhs = ()
    deps_rhs = list(deps_rhs) if isinstance(deps_rhs, (list, tuple)) else [deps_rhs]
    if n_tile == 1024:
        regions = [B.psA[0:128, 0:1024], B.psA[0:128, 1024:2048]]
    else:
        regions = [B.bank(i) for i in psum_banks]
    region_busy = [list(region_init) or None for _ in regions]
    evac_toks = []
    ri = 0
    for mb in range(min(prefetch, Mb)):
        ws.load(mb, deps=deps_rhs)
    for mb in range(Mb):
        last_mm = None
        for n0 in range(0, n_total, n_tile):
            ps = regions[ri]
            dep0 = [ws.load_tok[mb]] + deps_rhs
            rb = region_busy[ri]
            if rb is not None:
                dep0 += rb if isinstance(rb, list) else [rb]
            for kc in range(Kc):
                dd = list(dep0) if kc == 0 else []
                if extra_mm_deps:
                    dd = dd + extra_mm_deps(mb, kc, n0)
                for q in range(n_tile // NT):
                    a, b_ = n0 + q * NT, n0 + (q + 1) * NT
                    last_mm = p.do("tensor", lambda t, mb=mb, kc=kc, a=a, b_=b_,
                                   ps=ps, n0=n0, kc0=(kc == 0): t.matmul(
                        ps[:, (a - n0):(b_ - n0)],
                        ws.tile(mb, kc), rhs(kc, a, b_),
                        start=kc0, stop=(kc == Kc - 1)), deps=dd)
            tok = evac(mb, n0, ps, last_mm)
            evac_toks.append(tok)
            region_busy[ri] = tok
            ri = (ri + 1) % len(regions)
        ws.used(mb, last_mm)
        if mb + prefetch < Mb:
            ws.load(mb + prefetch)
    return evac_toks


def emit_residual_add(pr, x_dram_in, x_dram_out, delta_src, scale, deps_per_chunk,
                      n_chunks=DC, first_deps=()):
    B, p = pr.B, pr.p
    store = []
    slot_busy = [None, None]
    for c in range(n_chunks):
        sbuf = B.tmpf.ap()[:, (c % 2) * T:(c % 2 + 1) * T]
        dep = [slot_busy[c % 2]] if slot_busy[c % 2] else list(first_deps)
        t_ld = p.dma("sync", sbuf, x_dram_in[c * 128:(c + 1) * 128, :], deps=dep,
                     sem=f"res{c % 2}")
        dap, ddeps = delta_src(c)
        t_add = p.do("vector", lambda v, sbuf=sbuf, dap=dap: v.scalar_tensor_tensor(
            out=sbuf, in0=dap, scalar=float(scale), in1=sbuf,
            op0=ALU.mult, op1=ALU.add), deps=[t_ld] + ddeps + deps_per_chunk(c))
        t_st = p.dma("sync", x_dram_out[c * 128:(c + 1) * 128, :], sbuf,
                     deps=[t_add], sem=f"xst{c % 2}")
        slot_busy[c % 2] = t_st
        store.append(t_st)
    return store


def emit_l3_ffn(pr, x_in, x_out, lname="L3"):
    B, p = pr.B, pr.p
    H4 = 4096
    Kc_d, Mb_h = DC, H4 // 128
    w1 = pr.din(f"{lname}_w1p", [128, 32 * 8 * 128])
    w2 = pr.din(f"{lname}_w2p", [128, 32 * 8 * 128])
    h1 = pr.din(f"{lname}_h1p", [128, 32 * 32 * 128])
    h2 = pr.din(f"{lname}_h2p", [128, 32 * 32 * 128])
    w3 = pr.din(f"{lname}_w3p", [128, 8 * 32 * 128])
    scol = pr.din(f"{lname}_scols", [128, 136])
    t_sc = p.dma("sync", B.cols.ap()[:, 0:136], scol[:, 0:136], sem="col")

    hb = pr.dscratch(lname + "_hb", [H4, T], BF16)

    t_ln, _ = emit_ln(pr, x_in, xn_slot=0, xb_slot=8, x2_slot=16, deps=())
    xn = [B.slot(c) for c in range(DC)]
    Hsl = [B.slot(8 + c) for c in range(32)]

    ws2 = WStream(pr, w2, Kc_d)

    def rhs_xn(kc, a, b_):
        return xn[kc][:, a:b_]

    def evac_g2(mb, n0, ps, dep):
        return p.do("scalar", lambda s, mb=mb, n0=n0, ps=ps: s.activation(
            out=Hsl[mb][:, n0:n0 + 1024], in_=ps[:, 0:1024], func=AF.Copy,
            scale=B.cols.ap()[:, 32 + mb:33 + mb]), deps=[dep, t_sc])

    ev2 = mm_evac_stream(pr, Mb=32, Kc=Kc_d, ws=ws2, rhs=rhs_xn, evac=evac_g2,
                         deps_rhs=t_ln)

    ws1 = WStream(pr, w1, Kc_d)
    prod_toks = []

    def evac_g1(mb, n0, ps, dep):
        dst = B.stg.ap()[:, (n0 // 1024) * 1024:(n0 // 1024) * 1024 + 1024]
        t_a = p.do("scalar", lambda s, ps=ps, dst=dst, mb=mb: s.activation(
            out=dst, in_=ps[:, 0:1024], func=AF.Silu,
            scale=B.cols.ap()[:, mb:mb + 1]), deps=[dep, t_sc])
        t_m = p.do("vector", lambda v, dst=dst, mb=mb, n0=n0: v.tensor_mul(
            out=Hsl[mb][:, n0:n0 + 1024], in0=Hsl[mb][:, n0:n0 + 1024], in1=dst),
            deps=[t_a, ev2[mb * 2 + n0 // 1024]])
        prod_toks.append(t_m)
        return t_m

    ev1 = mm_evac_stream(pr, Mb=32, Kc=Kc_d, ws=ws1, rhs=rhs_xn, evac=evac_g1,
                         deps_rhs=t_ln, region_init=ev2[-2:])

    h_ready = {}
    for i, tok in enumerate(ev1):
        h_ready.setdefault(i // 2, []).append(tok)

    ws_h1 = WStream(pr, h1, 32)
    hb_w = []

    def rhs_H(kc, a, b_):
        return Hsl[kc][:, a:b_]

    def evac_h1(mb, n0, ps, dep):
        dst = B.stg.ap()[:, (n0 // 1024) * 1024:(n0 // 1024) * 1024 + 1024]
        t_a = p.do("scalar", lambda s, ps=ps, dst=dst, mb=mb: s.activation(
            out=dst, in_=ps[:, 0:1024], func=AF.Gelu,
            scale=B.cols.ap()[:, 64 + mb:65 + mb]), deps=[dep])
        t_w = p.dma("sync", hb[mb * 128:(mb + 1) * 128, n0:n0 + 1024], dst,
                    deps=[t_a], sem="hbw")
        hb_w.append(t_w)
        return t_w

    def mm_deps_h1(mb, kc, n0):
        return h_ready[kc] if kc in h_ready else []

    ev_h1 = mm_evac_stream(pr, Mb=32, Kc=32, ws=ws_h1, rhs=rhs_H, evac=evac_h1,
                           extra_mm_deps=mm_deps_h1, region_init=ev1[-2:])

    t_h1_ld = []
    for c in range(32):
        dep = [ev_h1[c * 2], ev_h1[c * 2 + 1], *h_ready.get(c, [])]
        t_h1_ld.append(p.dma("scalar", Hsl[c], hb[c * 128:(c + 1) * 128, :],
                             deps=list(ev_h1), sem=f"hbr{c % 4}"))

    hb2 = pr.dscratch(lname + "_hb2", [H4, T], BF16)
    ws_h2 = WStream(pr, h2, 32)
    hb2_w = []

    def evac_h2(mb, n0, ps, dep):
        dst = B.stg.ap()[:, (n0 // 1024) * 1024:(n0 // 1024) * 1024 + 1024]
        t_a = p.do("scalar", lambda s, ps=ps, dst=dst, mb=mb: s.activation(
            out=dst, in_=ps[:, 0:1024], func=AF.Gelu,
            scale=B.cols.ap()[:, 96 + mb:97 + mb]), deps=[dep])
        t_w = p.dma("sync", hb2[mb * 128:(mb + 1) * 128, n0:n0 + 1024], dst,
                    deps=[t_a], sem="hb2w")
        hb2_w.append(t_w)
        return t_w

    def mm_deps_h2(mb, kc, n0):
        return [t_h1_ld[kc]]

    ev_h2 = mm_evac_stream(pr, Mb=32, Kc=32, ws=ws_h2, rhs=rhs_H, evac=evac_h2,
                           extra_mm_deps=mm_deps_h2, region_init=ev_h1[-2:])

    t_h2_ld = []
    for c in range(32):
        t_h2_ld.append(p.dma("scalar", Hsl[c], hb2[c * 128:(c + 1) * 128, :],
                             deps=list(ev_h2), sem=f"hb2r{c % 4}"))

    ws3 = WStream(pr, w3, 32)
    delta = [B.slot(c) for c in range(DC)]
    dtoks = {}

    def evac_o(mb, n0, ps, dep):
        tok = p.do("scalar", lambda s, ps=ps, mb=mb, n0=n0: s.activation(
            out=delta[mb][:, n0:n0 + 1024], in_=ps[:, 0:1024], func=AF.Copy,
            scale=B.cols.ap()[:, 128 + mb:129 + mb]), deps=[dep])
        dtoks.setdefault(mb, []).append(tok)
        return tok

    def mm_deps_o(mb, kc, n0):
        return [t_h2_ld[kc]]

    ev_o = mm_evac_stream(pr, Mb=DC, Kc=32, ws=ws3, rhs=rhs_H, evac=evac_o,
                          extra_mm_deps=mm_deps_o, region_init=ev_h2[-2:])

    def delta_src(c):
        return (delta[c], dtoks[c])

    st = emit_residual_add(pr, x_in, x_out, delta_src, 0.5,
                           lambda c: [], n_chunks=DC,
                           first_deps=[t_h2_ld[-1]])
    return st


def host_pack_l3_ffn(ins, pref="L3"):
    out = {}
    out[f"{pref}_w1p"] = pack_lhsT(np.ascontiguousarray(ins[f"{pref}_ffn_w1_w"].T))
    out[f"{pref}_w2p"] = pack_lhsT(np.ascontiguousarray(ins[f"{pref}_ffn_w2_w"].T))
    out[f"{pref}_h1p"] = pack_lhsT(np.ascontiguousarray(ins[f"{pref}_ffn_h1_w"].T))
    out[f"{pref}_h2p"] = pack_lhsT(np.ascontiguousarray(ins[f"{pref}_ffn_h2_w"].T))
    out[f"{pref}_w3p"] = pack_lhsT(np.ascontiguousarray(ins[f"{pref}_ffn_w3_w"].T))
    sc = np.zeros((128, 320), np.float32)
    sc[:, 0:32] = pack_cols(ins[f"{pref}_ffn_w1_s"])
    sc[:, 32:64] = pack_cols(ins[f"{pref}_ffn_w2_s"])
    sc[:, 64:96] = pack_cols(ins[f"{pref}_ffn_h1_s"])
    sc[:, 96:128] = pack_cols(ins[f"{pref}_ffn_h2_s"])
    sc[:, 128:136] = pack_cols(ins[f"{pref}_ffn_w3_s"])
    out[f"{pref}_scols"] = np.ascontiguousarray(sc[:, 0:136])
    return out




def emit_transpose8(pr, src_fn, dst_fn, n_tc, deps_fn, n_c=8, dtype_bf=True,
                    evac_eng="vector"):
    B, p = pr.B, pr.p
    toks = []
    prev_evac = None
    for tc in range(n_tc):
        last = None
        for c in range(n_c):
            dd = list(deps_fn(tc, c)) + list(pr.const_toks)
            if prev_evac is not None:
                dd.append(prev_evac)
            last = p.do("tensor", lambda t, tc=tc, c=c: t.transpose(
                B.psT[0:128, c * 128:(c + 1) * 128], src_fn(tc, c),
                B.ident_bf.ap()[:, :]), deps=dd)
        tok = p.do(evac_eng, lambda e, tc=tc: (
            e.tensor_copy(out=dst_fn(tc), in_=B.psT[0:128, 0:n_c * 128])
            if evac_eng == "vector" else
            e.activation(out=dst_fn(tc), in_=B.psT[0:128, 0:n_c * 128],
                         func=AF.Copy)), deps=[last])
        prev_evac = tok
        toks.append(tok)
    return toks


C_MAGIC = float(1.5 * 2 ** 23)


def emit_mx_quantize_tile(pr, xin, xout, scratch_v, u_buf, q_buf, m1, m2,
                          scales, rinv, deps, width=1024, out_deps=None,
                          a_buf=None):
    B, p = pr.B, pr.p
    nb = width // 32

    def r3(ap, w=width):
        return ap.rearrange("p (b j) -> p b j", b=nb)

    t0 = p.do("vector", lambda v: v.tensor_reduce(
        out=scales.rearrange("p (b one) -> p b one", b=nb),
        in_=r3(xin), axis=mybir.AxisListType.X, op=ALU.max,
        apply_absolute_value=True), deps=deps)
    t1 = p.do("vector", lambda v: v.tensor_scalar(
        out=scales, in0=scales, scalar1=1.0 / 7.5, scalar2=1e-30,
        op0=ALU.mult, op1=ALU.max), deps=[t0])
    t2 = p.do("vector", lambda v: v.reciprocal(out=rinv, in_=scales), deps=[t1])
    tu = p.do("vector", lambda v: v.tensor_tensor(
        out=r3(u_buf), in0=r3(xin),
        in1=rinv.rearrange("p b -> p b ()").broadcast_to([128, nb, 32]),
        op=ALU.mult), deps=[t2])
    t_abs = p.do("scalar", lambda s: s.activation(
        out=a_buf, in_=u_buf, func=AF.Abs), deps=[tu])
    tm1 = p.do("vector", lambda v: v.tensor_scalar(
        out=m1, in0=a_buf, scalar1=1.9375, scalar2=None,
        op0=ALU.is_ge), deps=[t_abs])
    tm2 = p.do("vector", lambda v: v.tensor_scalar(
        out=m2, in0=a_buf, scalar1=3.875, scalar2=None,
        op0=ALU.is_ge), deps=[t_abs])
    tq = p.do("vector", lambda v: v.tensor_scalar(
        out=q_buf, in0=u_buf, scalar1=8.0, scalar2=C_MAGIC,
        op0=ALU.mult, op1=ALU.add), deps=[tu])
    tq = p.do("vector", lambda v: v.tensor_scalar(
        out=q_buf, in0=q_buf, scalar1=C_MAGIC, scalar2=0.125,
        op0=ALU.subtract, op1=ALU.mult), deps=[tq])
    tv = p.do("vector", lambda v: v.tensor_scalar(
        out=scratch_v, in0=u_buf, scalar1=4.0, scalar2=C_MAGIC,
        op0=ALU.mult, op1=ALU.add), deps=[tu])
    tv = p.do("vector", lambda v: v.tensor_scalar(
        out=scratch_v, in0=scratch_v, scalar1=C_MAGIC, scalar2=0.25,
        op0=ALU.subtract, op1=ALU.mult), deps=[tv])
    tq = p.do("vector", lambda v: v.copy_predicated(
        out=q_buf, mask=m1, data=scratch_v), deps=[tq, tv, tm1])
    tv = p.do("vector", lambda v: v.tensor_scalar(
        out=scratch_v, in0=u_buf, scalar1=2.0, scalar2=C_MAGIC,
        op0=ALU.mult, op1=ALU.add), deps=[tv, tq])
    tv = p.do("vector", lambda v: v.tensor_scalar(
        out=scratch_v, in0=scratch_v, scalar1=C_MAGIC, scalar2=0.5,
        op0=ALU.subtract, op1=ALU.mult), deps=[tv])
    tq = p.do("vector", lambda v: v.copy_predicated(
        out=q_buf, mask=m2, data=scratch_v), deps=[tq, tv, tm2])
    tout = p.do("vector", lambda v: v.tensor_tensor(
        out=r3(xout), in0=r3(q_buf),
        in1=scales.rearrange("p b -> p b ()").broadcast_to([128, nb, 32]),
        op=ALU.mult), deps=[tq] + (out_deps or []))
    return tout


def emit_l1_ffn(pr, x_in, x_out, lname="L1f"):
    B, p = pr.B, pr.p
    wdram = pr.din(lname + "_w", [D, D])
    bias = pr.din(lname + "_bcols", [128, 8])
    t_b = p.dma("sync", B.cols.ap()[:, 0:8], bias[:, :], deps=pr.const_toks,
                sem="col")

    t_ln, _ = emit_ln(pr, x_in, xn_slot=0, xb_slot=8, x2_slot=16, deps=())

    def xn_tm(tc):
        return B.arena.ap()[:, 8 * T + tc * 1024: 8 * T + (tc + 1) * 1024]

    tr1 = emit_transpose8(
        pr, lambda tc, c: B.slot(c)[:, tc * 128:(tc + 1) * 128],
        xn_tm, 16, lambda tc, c: [t_ln[c]])

    u_buf = B.tmpf.ap()[:, 0:1024]
    q_buf = B.tmpf.ap()[:, 1024:2048]
    vscr = B.slot_f32(6, 1)[:, 0:1024]
    a_buf = B.slot_f32(6, 1)[:, 1024:2048]
    m1 = B.stg.ap().bitcast(U8)[:, 0:1024]
    m2 = B.stg.ap().bitcast(U8)[:, 1024:2048]
    scales = B.stg.ap()[:, 1024 + 0:1024 + 64].bitcast(F32)
    rinv = B.stg.ap()[:, 1024 + 64:1024 + 128].bitcast(F32)
    q_toks = []
    tq_prev = None
    for tc in range(16):
        dd = [tr1[tc]]
        if tc < 2:
            dd = dd + [tr1[-1]]
        if tq_prev is not None:
            dd.append(tq_prev)
        tq_prev = emit_mx_quantize_tile(
            pr, xn_tm(tc), xn_tm(tc), vscr, u_buf, q_buf, m1, m2, scales,
            rinv, dd, a_buf=a_buf)
        q_toks.append(tq_prev)

    def wq_tile(oc):
        return B.arena.ap()[:, (20 + oc // 2) * T + (oc % 2) * 1024:
                            (20 + oc // 2) * T + (oc % 2 + 1) * 1024]

    t_wld = []
    for oc in range(8):
        t_wld.append(p.dma("gpsimd", wq_tile(oc),
                           wdram[oc * 128:(oc + 1) * 128, :],
                           deps=[t_ln[4 + oc // 2]], sem=f"wq{oc}"))
    for oc in range(8):
        dd = [t_wld[oc], tq_prev]
        tq_prev = emit_mx_quantize_tile(
            pr, wq_tile(oc), wq_tile(oc), vscr, u_buf, q_buf, m1, m2, scales,
            rinv, dd, a_buf=a_buf)
        q_toks.append(tq_prev)

    def wdT_tile(mb, kc):
        off = 16 * T + (mb * 8 + kc) * 128
        return B.arena.ap()[:, off:off + 128]

    trw = []
    prev = None
    for mb in range(8):
        for kc in range(8):
            dd = [q_toks[16 + mb]]
            if prev is not None:
                dd.append(prev)
            tt = p.do("tensor", lambda t, mb=mb, kc=kc: t.transpose(
                B.psT[0:128, kc * 128:(kc + 1) * 128],
                wq_tile(mb)[:, kc * 128:(kc + 1) * 128],
                B.ident_bf.ap()[:, :]), deps=dd)
        tok = p.do("vector", lambda v, mb=mb: v.tensor_copy(
            out=B.arena.ap()[:, 16 * T + mb * 1024:16 * T + (mb + 1) * 1024],
            in_=B.psT[0:128, 0:1024]), deps=[tt])
        prev = tok
        trw.append(tok)

    xd_done = [[] for _ in range(8)]
    prev = trw[-1]
    for tc in range(16):
        tts = []
        for c in range(8):
            dd = [q_toks[tc]]
            if prev is not None:
                dd.append(prev)
            tts.append(p.do("tensor", lambda t, tc=tc, c=c: t.transpose(
                B.psT[0:128, c * 128:(c + 1) * 128],
                xn_tm(tc)[:, c * 128:(c + 1) * 128],
                B.ident_bf.ap()[:, :]), deps=dd))
        ev = p.do("vector", lambda v, tc=tc: v.tensor_copy(
            out=B.stg.ap()[:, 0:1024], in_=B.psT[0:128, 0:1024]),
            deps=[tts[-1], prev])
        for c in range(8):
            tok = p.do("vector", lambda v, tc=tc, c=c: v.tensor_copy(
                out=B.slot(c)[:, tc * 128:(tc + 1) * 128],
                in_=B.stg.ap()[:, c * 128:(c + 1) * 128]), deps=[ev])
            xd_done[c].append(tok)
        prev = tok

    class WS:
        def __init__(self):
            self.load_tok = {}

        def load(self, mb, deps=()):
            self.load_tok[mb] = None
            return None

        def tile(self, mb, kc):
            return wdT_tile(mb, kc)

        def used(self, mb, tok):
            pass

    ws = WS()
    delta = [B.slot(24 + c) for c in range(8)]
    dtoks = {}

    def rhs_xd(kc, a, b_):
        return B.slot(kc)[:, a:b_]

    def evac_o(mb, n0, ps, dep):
        tok = p.do("vector", lambda v, ps=ps, mb=mb, n0=n0: v.tensor_scalar(
            out=delta[mb][:, n0:n0 + 1024], in0=ps[:, 0:1024],
            scalar1=B.cols.ap()[:, mb:mb + 1], scalar2=None, op0=ALU.add),
            deps=[dep, t_b])
        dtoks.setdefault(mb, []).append(tok)
        return tok

    def mm_deps(mb, kc, n0):
        return xd_done[kc] + [trw[mb]]

    ws.load_tok = {mb: trw[mb] for mb in range(8)}
    ev_o = mm_evac_stream(pr, Mb=8, Kc=8, ws=ws, rhs=rhs_xd, evac=evac_o,
                          extra_mm_deps=mm_deps, region_init=t_ln, prefetch=0)

    def delta_src(c):
        return (delta[c], dtoks[c])

    st = emit_residual_add(pr, x_in, x_out, delta_src, 0.5, lambda c: [],
                           first_deps=[q_toks[-1]])
    return st


def host_pack_l1_ffn(ins):
    return {
        "L1f_w": np.ascontiguousarray(ins["L1_ffn_weight"].astype(np.float32)),
        "L1f_bcols": pack_cols(ins["L1_ffn_bias"].astype(np.float32)),
    }


def build_l3_ffn_program():
    pr = Prog()
    pr.load_consts()
    x_in = pr.din("x_cur", [D, T])
    x_out = pr.dout("x_next", [D, T])
    st = emit_l3_ffn(pr, x_in, x_out)
    pr.out_tokens = st
    pr.finish()
    return pr


def run_sublayer_l3ffn(x_full, ins):
    pr = build_l3_ffn_program()
    packed = host_pack_l3_ffn(ins)
    consts = host_consts()
    in_maps = []
    for core in range(8):
        m = dict(consts)
        m.update(packed)
        m["x_cur"] = np.ascontiguousarray(x_full[core].T)
        in_maps.append(m)
    res = run_bass_kernel_spmd(pr.nc, in_maps, core_ids=list(range(8)))
    out = np.stack([res.results[c]["x_next"].T for c in range(8)])
    pr.close()
    return out


def build_l1_ffn_program():
    pr = Prog()
    pr.load_consts()
    x_in = pr.din("x_cur", [D, T])
    x_out = pr.dout("x_next", [D, T])
    st = emit_l1_ffn(pr, x_in, x_out)
    pr.out_tokens = st
    pr.finish()
    return pr


def run_sublayer_l1ffn(x_full, ins):
    pr = build_l1_ffn_program()
    packed = host_pack_l1_ffn(ins)
    consts = host_consts()
    in_maps = []
    for core in range(8):
        m = dict(consts)
        m.update(packed)
        m["x_cur"] = np.ascontiguousarray(x_full[core].T)
        in_maps.append(m)
    res = run_bass_kernel_spmd(pr.nc, in_maps, core_ids=list(range(8)))
    out = np.stack([res.results[c]["x_next"].T for c in range(8)])
    pr.close()
    return out




def emit_l0_ffn(pr, x_in, x_out, lname="L0f"):
    B, p = pr.B, pr.p
    E, HD, I = 16, 128, 512
    NSUB = 16384
    NC = 128
    wm = pr.din(lname + "_wmp", [128, 8 * 8 * 128])
    wo = pr.din(lname + "_wop", [128, 8 * 8 * 128])
    w1 = pr.din(lname + "_w1", [128, E * I])
    w2 = pr.din(lname + "_w2", [128, E * 512])
    emb_d = pr.din(lname + "_emb", [128, E])
    b2_d = pr.din(lname + "_b2", [16, 128])
    colsd = pr.din(lname + "_cols", [128, 80])

    if not hasattr(pr, "_l0_small"):
        st = pr.stack.enter_context
        pr._l0_small = (
            st(pr.nc.sbuf_tensor(lname + "_embsb", [128, E], BF16)),
            st(pr.nc.sbuf_tensor(lname + "_b2sb", [16, 128], BF16)),
        )
    embsb, b2sb = pr._l0_small
    w1sb_v = B.arena.ap()[:, 16 * T:16 * T + E * I]
    w2sb_v = B.arena.ap()[:, 20 * T:20 * T + E * 512]

    t_cols = p.dma("sync", B.cols.ap()[:, 0:80], colsd[:, :], deps=pr.const_toks,
                   sem="col")
    t_emb = p.dma("gpsimd", embsb.ap()[:, :], emb_d[:, :], sem="l0emb")
    t_b2 = p.dma("gpsimd", b2sb.ap()[:, :], b2_d[:, :], sem="l0b2")

    t_ln, _ = emit_ln(pr, x_in, xn_slot=0, xb_slot=8, x2_slot=16)
    t_w1 = p.dma("gpsimd", w1sb_v, w1[:, :], sem="l0w1",
                 deps=[t_ln[0], t_ln[1], t_ln[2], t_ln[3]])
    t_w2 = p.dma("gpsimd", w2sb_v, w2[:, :], sem="l0w2",
                 deps=[t_ln[4], t_ln[5], t_ln[6], t_ln[7]])
    xn = [B.slot(c) for c in range(8)]
    hsl = [B.slot(8 + c) for c in range(8)]

    ws_m = WStream(pr, wm, 8)
    h_toks = {}

    def rhs_xn(kc, a, b_):
        return xn[kc][:, a:b_]

    def evac_h(mb, n0, ps, dep):
        tok = p.do("vector", lambda v, ps=ps, mb=mb, n0=n0: v.tensor_scalar(
            out=hsl[mb][:, n0:n0 + 1024], in0=ps[:, 0:1024],
            scalar1=B.cols.ap()[:, mb:mb + 1], scalar2=None, op0=ALU.add),
            deps=[dep, t_cols])
        h_toks.setdefault(mb, []).append(tok)
        return tok

    ev_h = mm_evac_stream(pr, Mb=8, Kc=8, ws=ws_m, rhs=rhs_xn, evac=evac_h,
                          deps_rhs=t_ln, region_init=t_ln)

    def tsub(nc_):
        return hsl[nc_ // 16][:, (nc_ % 16) * 128:(nc_ % 16 + 1) * 128]

    gf = B.slot_f32(24, 1)
    u_b = B.tmpf.ap()[:, 0:512]
    u2_b = B.tmpf.ap()[:, 512:1024]
    m_b = B.tmpf.ap()[:, 1024:1536]
    s_b = B.tmpf.ap()[:, 1536:2048]
    mx = B.cols.ap()[:, 80:112]
    Ssum = B.cols.ap()[:, 112:144]
    umx = B.cols.ap()[:, 144:176]
    umx2 = B.cols.ap()[:, 176:208]
    rS = B.cols.ap()[:, 208:240]

    def r3(ap):
        return ap.rearrange("p (b j) -> p b j", b=32)

    def bc(colap):
        return colap.rearrange("p b -> p b ()").broadcast_to([128, 32, 16])

    g_done = []
    prev_gate = None
    for bank in range(4):
        ps = B.ps6.ap()[:, :]
        mms = []
        for j in range(32):
            nc_ = bank * 32 + j
            dd = [h_toks[nc_ // 16][(nc_ % 16) // 8], t_emb]
            if prev_gate is not None and j == 0:
                dd.append(prev_gate)
            mms.append(p.do("tensor", lambda t, nc_=nc_, j=j, ps=ps: t.matmul(
                ps[:, j * 16:(j + 1) * 16], tsub(nc_), embsb.ap()[:, :],
                start=True, stop=True), deps=dd))
        t0 = p.do("vector", lambda v, ps=ps: v.tensor_reduce(
            out=mx.rearrange("p (b one) -> p b one", b=32), in_=r3(ps),
            axis=mybir.AxisListType.X, op=ALU.max), deps=[mms[-1]])
        t1 = p.do("vector", lambda v, ps=ps: v.tensor_tensor(
            out=r3(u_b), in0=r3(ps), in1=bc(mx), op=ALU.subtract), deps=[t0])
        t2 = p.do("scalar", lambda s: s.activation(
            out=u_b, in_=u_b, func=AF.Exp), deps=[t1])
        t3 = p.do("vector", lambda v: v.tensor_reduce(
            out=Ssum.rearrange("p (b one) -> p b one", b=32), in_=r3(u_b),
            axis=mybir.AxisListType.X, op=ALU.add), deps=[t2])
        t4 = p.do("vector", lambda v: v.tensor_reduce(
            out=umx.rearrange("p (b one) -> p b one", b=32), in_=r3(u_b),
            axis=mybir.AxisListType.X, op=ALU.max), deps=[t2])
        t5 = p.do("vector", lambda v: v.tensor_tensor(
            out=r3(m_b), in0=r3(u_b), in1=bc(umx), op=ALU.is_ge), deps=[t4])
        t6 = p.do("vector", lambda v: v.scalar_tensor_tensor(
            out=u2_b, in0=m_b, scalar=-1e30, in1=u_b, op0=ALU.mult,
            op1=ALU.add), deps=[t5])
        t7 = p.do("vector", lambda v: v.tensor_reduce(
            out=umx2.rearrange("p (b one) -> p b one", b=32), in_=r3(u2_b),
            axis=mybir.AxisListType.X, op=ALU.max), deps=[t6])
        t8 = p.do("vector", lambda v: v.tensor_tensor(
            out=r3(s_b), in0=r3(u2_b), in1=bc(umx2), op=ALU.is_ge), deps=[t7])
        t9 = p.do("vector", lambda v: v.tensor_add(
            out=m_b, in0=m_b, in1=s_b), deps=[t8])
        t10 = p.do("vector", lambda v: v.reciprocal(out=rS, in_=Ssum), deps=[t3])
        t11 = p.do("vector", lambda v: v.tensor_mul(
            out=u_b, in0=u_b, in1=m_b), deps=[t9])
        t12 = p.do("vector", lambda v, bank=bank: v.tensor_tensor(
            out=r3(gf[:, bank * 512:(bank + 1) * 512]), in0=r3(u_b),
            in1=bc(rS), op=ALU.mult), deps=[t10, t11])
        prev_gate = t12
        g_done.append(t12)

    g_fm = B.arena.ap()[0:16, 29 * T:37 * T]
    gfm_toks = []
    prev = None
    for grp in range(8):
        tts = []
        for j in range(16):
            nc_ = grp * 16 + j
            dd = [g_done[nc_ // 32]]
            if prev is not None:
                dd.append(prev)
            tts.append(p.do("tensor", lambda t, nc_=nc_, j=j: t.transpose(
                B.psA[0:16, j * 128:(j + 1) * 128],
                gf[:, nc_ * 16:(nc_ + 1) * 16],
                B.ident_f32.ap()[:, :]), deps=dd))
        tok = p.do("vector", lambda v, grp=grp: v.tensor_copy(
            out=g_fm[:, grp * T:(grp + 1) * T], in_=B.psA[0:16, 0:T]),
            deps=[tts[-1]])
        prev = tok
        gfm_toks.append(tok)

    comb = [B.slot(c) for c in range(8)]
    hm = [B.slot(s) for s in (26, 27, 28, 37)]
    comb_toks = {}
    init_dep = [ev_h[-1], t_b2, gfm_toks[-1]]
    prev_ev = None
    for grp in range(16):
        ps = B.psA[0:128, (grp % 2) * 1024:(grp % 2) * 1024 + 1024]
        tts = []
        for j in range(8):
            nc_ = grp * 8 + j
            dd = init_dep + [gfm_toks[nc_ // 16]]
            if prev_ev is not None and j == 0:
                dd.append(prev_ev)
            tts.append(p.do("tensor", lambda t, nc_=nc_, j=j, ps=ps: t.matmul(
                ps[:, j * 128:(j + 1) * 128],
                g_fm[:, nc_ * 128:(nc_ + 1) * 128], b2sb.ap()[:, :],
                start=True, stop=True), deps=dd))
        tok = p.do("scalar", lambda s, grp=grp, ps=ps: s.activation(
            out=comb[grp // 2][:, (grp % 2) * 1024:(grp % 2) * 1024 + 1024],
            in_=ps[:, 0:1024], func=AF.Copy), deps=[tts[-1]])
        prev_ev = tok
        for j in range(8):
            comb_toks[grp * 8 + j] = tok

    ph_busy = [prev_ev, prev_ev]
    eo_bank_busy = {0: None, 1: None}
    hm_busy = [None] * 4
    stg_busy = [None, None]

    for nblk in range(8):
        for e in range(E):
            hm_toks = []
            for mb in range(4):
                for half in range(2):
                    ps = B.psA[0:128, half * 1024:half * 1024 + 1024]
                    dd = [t_w1, ph_busy[half]]
                    last = None
                    for q in range(2):
                        nc_ = nblk * 16 + half * 8 + (q * 4)
                        a = half * 1024 + q * 512
                        last = p.do("tensor", lambda t, mb=mb, e=e, ps=ps,
                                    nblk=nblk, a=a, q=q: t.matmul(
                            ps[:, (a % 1024):(a % 1024) + 512],
                            w1sb_v[:, e * I + mb * 128:e * I + (mb + 1) * 128],
                            _l0_sub_rhs(B, hsl, nblk, a),
                            start=True, stop=True), deps=dd)
                    tok = p.do("scalar", lambda s, mb=mb, e=e, ps=ps, half=half:
                               s.activation(
                        out=hm[mb][:, half * 1024:half * 1024 + 1024],
                        in_=ps[:, 0:1024], func=AF.Gelu,
                        bias=B.cols.ap()[:, 8 + e * 4 + mb:9 + e * 4 + mb]),
                        deps=[last, t_cols, hm_busy[mb]])
                    ph_busy[half] = tok
                    hm_toks.append(tok)
                hm_busy[mb] = hm_toks[-1]
            for quad in range(4):
                bank = B.ps4.ap() if quad % 2 == 0 else B.ps5.ap()
                bsel = quad % 2
                last = None
                for j in range(4):
                    nc_ = nblk * 16 + quad * 4 + j
                    dd = [hm_toks[-1], t_w2]
                    if eo_bank_busy[bsel] is not None and j == 0:
                        dd.append(eo_bank_busy[bsel])
                    for kc in range(4):
                        last = p.do("tensor", lambda t, nc_=nc_, j=j, kc=kc,
                                    e=e, bank=bank, nblk=nblk, quad=quad: t.matmul(
                            bank[:, j * 128:(j + 1) * 128],
                            hm[kc][:, (quad * 4 + j) * 128:(quad * 4 + j + 1) * 128],
                            w2sb_v[:, e * 512 + kc * 128:e * 512 + (kc + 1) * 128],
                            start=(kc == 0), stop=(kc == 3)),
                            deps=dd if kc == 0 else [t_w2])
                sb = (nblk * 4 + quad) % 2
                tev = p.do("scalar", lambda s, bank=bank, sb=sb: s.activation(
                    out=B.stg.ap()[:, sb * 1024:sb * 1024 + 512],
                    in_=bank[:, 0:512], func=AF.Copy),
                    deps=[last, stg_busy[sb]])
                eo_bank_busy[bsel] = tev
                for j in range(4):
                    nc_ = nblk * 16 + quad * 4 + j
                    tok = p.do("vector", lambda v, nc_=nc_, j=j, e=e, sb=sb:
                               v.scalar_tensor_tensor(
                        out=_comb_sub(B, comb, nc_),
                        in0=B.stg.ap()[:, sb * 1024 + j * 128:sb * 1024 + (j + 1) * 128],
                        scalar=gf[:, nc_ * 16 + e:nc_ * 16 + e + 1],
                        in1=_comb_sub(B, comb, nc_),
                        op0=ALU.mult, op1=ALU.add),
                        deps=[tev, comb_toks[nc_]])
                    comb_toks[nc_] = tok
                stg_busy[sb] = tok

    def cfm(c):
        return B.slot(8 + c)

    prev = None
    cfm_toks = [[] for _ in range(8)]
    for nc_ in range(NC):
        tts = []
        for dummy in range(1):
            pass
        dd = [comb_toks[nc_]]
        if prev is not None:
            dd.append(prev)
        tt = p.do("tensor", lambda t, nc_=nc_: t.transpose(
            B.psT[0:128, (nc_ % 8) * 128:(nc_ % 8 + 1) * 128],
            _comb_sub(B, comb, nc_), B.ident_bf.ap()[:, :]), deps=dd)
        if nc_ % 8 == 7:
            tok = p.do("vector", lambda v, nc_=nc_: v.tensor_copy(
                out=_cfm_dst(B, nc_), in_=B.psT[0:128, 0:1024]), deps=[tt])
            prev = tok
            c = (nc_ // 16)
            cfm_toks[c].append(tok)
            if (nc_ - 7) // 16 != c:
                cfm_toks[(nc_ - 7) // 16].append(tok)

    ws_o = WStream(pr, wo, 8)
    delta = [B.slot(s) for s in (16, 17, 18, 19, 20, 21, 22, 23)]
    dtoks = {}

    def rhs_cfm(kc, a, b_):
        return cfm(kc)[:, a:b_]

    def evac_o(mb, n0, ps, dep):
        tok = p.do("vector", lambda v, ps=ps, mb=mb, n0=n0: v.tensor_scalar(
            out=delta[mb][:, n0:n0 + 1024], in0=ps[:, 0:1024],
            scalar1=B.cols.ap()[:, 72 + mb:73 + mb], scalar2=None, op0=ALU.add),
            deps=[dep, t_cols])
        dtoks.setdefault(mb, []).append(tok)
        return tok

    def mm_deps_o(mb, kc, n0):
        return cfm_toks[kc] + [t_w1, t_w2]

    ev_o = mm_evac_stream(pr, Mb=8, Kc=8, ws=ws_o, rhs=rhs_cfm, evac=evac_o,
                          extra_mm_deps=mm_deps_o,
                          region_init=[ph_busy[0], ph_busy[1]])

    def delta_src(c):
        return (delta[c], dtoks[c])

    st = emit_residual_add(pr, x_in, x_out, delta_src, 0.5, lambda c: [],
                           first_deps=[ev_o[0]])
    return st


def _l0_sub_rhs(B, hsl, nblk, a):
    base = 8 * T + nblk * 2048 + a
    return B.arena.ap()[:, base:base + 512]


def _comb_sub(B, comb, nc_):
    return B.arena.ap()[:, nc_ * 128:nc_ * 128 + 128]


def _cfm_dst(B, nc_):
    first = nc_ - 7
    c = first // 16
    off = (first % 16) * 128
    return B.slot(8 + c)[:, off:off + 1024]


def host_pack_l0_ffn(ins):
    E, HD, I = 16, 128, 512
    w1 = ins["L0_ffn_W1"].astype(np.float32)
    w2 = ins["L0_ffn_W2"].astype(np.float32)
    w1p = np.ascontiguousarray(w1.transpose(1, 0, 2).reshape(128, E * I))
    w2p = np.ascontiguousarray(
        w2.reshape(E, 4, 128, 128).transpose(2, 0, 1, 3).reshape(128, E * 512))
    cols = np.zeros((128, 80), np.float32)
    cols[:, 0:8] = pack_cols(ins["L0_ffn_bm"].astype(np.float32))
    cols[:, 8:72] = np.concatenate(
        [pack_cols(ins["L0_ffn_b1"][e].astype(np.float32)) for e in range(E)], 1)
    cols[:, 72:80] = pack_cols(ins["L0_ffn_bo"].astype(np.float32))
    return {
        "L0f_wmp": pack_lhsT(ins["L0_ffn_Wm"].astype(np.float32)),
        "L0f_wop": pack_lhsT(ins["L0_ffn_Wo"].astype(np.float32)),
        "L0f_w1": w1p,
        "L0f_w2": w2p,
        "L0f_emb": np.ascontiguousarray(ins["L0_ffn_expert_emb"].astype(np.float32)),
        "L0f_b2": np.ascontiguousarray(ins["L0_ffn_b2"].astype(np.float32)),
        "L0f_cols": cols,
    }


def build_l0_ffn_program():
    pr = Prog()
    pr.load_consts()
    x_in = pr.din("x_cur", [D, T])
    x_out = pr.dout("x_next", [D, T])
    st = emit_l0_ffn(pr, x_in, x_out)
    pr.out_tokens = st
    pr.finish()
    return pr


def run_sublayer_l0ffn(x_full, ins):
    pr = build_l0_ffn_program()
    packed = host_pack_l0_ffn(ins)
    consts = host_consts()
    in_maps = []
    for core in range(8):
        m = dict(consts)
        m.update(packed)
        m["x_cur"] = np.ascontiguousarray(x_full[core].T)
        in_maps.append(m)
    res = run_bass_kernel_spmd(pr.nc, in_maps, core_ids=list(range(8)))
    out = np.stack([res.results[c]["x_next"].T for c in range(8)])
    pr.close()
    return out




def emit_l1_attn(pr, x_in, x_out, lname="L1a"):
    B, p = pr.B, pr.p
    H, HD = 16, 64
    inw = pr.din(lname + "_inwp", [128, 24 * 8 * 128])
    outw = pr.din(lname + "_outwp", [128, 8 * 8 * 128])
    colsd = pr.din(lname + "_cols", [128, 32])
    vecs = pr.din(lname + "_vecs", [16, 2])
    srow = pr.din(lname + "_sloperow", [1, 16])
    selm = pr.din(lname + "_selmask", [16, 128])
    selr = pr.din(lname + "_selrhs", [16, 16])
    iota = pr.din(lname + "_iota", [128, T])
    caus = pr.din(lname + "_caus", [128, 128])

    if not hasattr(pr, "_l1a_small"):
        st = pr.stack.enter_context
        pr._l1a_small = (
            st(pr.nc.sbuf_tensor(lname + "_vec_sb", [16, 2], F32)),
            st(pr.nc.sbuf_tensor(lname + "_srow_sb", [1, 16], F32)),
            st(pr.nc.sbuf_tensor(lname + "_selm_sb", [16, 128], F32)),
            st(pr.nc.sbuf_tensor(lname + "_selr_sb", [16, 16], F32)),
            st(pr.nc.sbuf_tensor(lname + "_caus_sb", [128, 128], F32)),
        )
    vec_sb, srow_sb, selm_sb, selr_sb, caus_sb = pr._l1a_small

    t_cols = p.dma("sync", B.cols.ap()[:, 0:32], colsd[:, :], deps=pr.const_toks,
                   sem="col")
    t_v1 = p.dma("sync", vec_sb.ap()[:, :], vecs[:, :], sem="l1v1")
    t_v2 = p.dma("sync", srow_sb.ap()[:, :], srow[:, :], sem="l1v2")
    t_v3 = p.dma("sync", selm_sb.ap()[:, :], selm[:, :], sem="l1v3")
    t_v4 = p.dma("sync", selr_sb.ap()[:, :], selr[:, :], sem="l1v4")
    t_v5 = p.dma("sync", caus_sb.ap()[:, :], caus[:, :], sem="l1v5")
    tv = [t_v1, t_v2, t_v3, t_v4, t_v5]

    def gc(c):
        return B.cols.ap()[:, c:c + 1]

    def bc_(c):
        return B.cols.ap()[:, 8 + c:9 + c]

    t_ln, _ = emit_ln(pr, x_in, xn_slot=0, xb_slot=8, x2_slot=16,
                      gcol=gc, bcol=bc_, var_scale=1.0 + 1e-5,
                      var_bias=1e-5 + 1e-10, cols_dep=[t_cols])
    xn = [B.slot(c) for c in range(8)]

    t_sig = p.do("scalar", lambda s: s.activation(
        out=vec_sb.ap()[0:16, 1:2], in_=vec_sb.ap()[0:16, 1:2],
        func=AF.Sigmoid), deps=tv)
    t_sd = p.do("vector", lambda v: v.tensor_scalar(
        out=selm_sb.ap()[:, :], in0=selm_sb.ap()[:, :],
        scalar1=vec_sb.ap()[0:16, 1:2], scalar2=None, op0=ALU.mult),
        deps=[t_sig, t_v3])
    t_mm1 = p.do("tensor", lambda t: t.matmul(
        B.ps6[0:128, 0:16], selm_sb.ap()[:, :], selr_sb.ap()[:, :],
        start=True, stop=True), deps=[t_sd, t_v4] + t_ln)
    t_sc = p.do("vector", lambda v: v.tensor_copy(
        out=B.cols.ap()[:, 32:48], in_=B.ps6[0:128, 0:16]), deps=[t_mm1])
    t_1ms = p.do("vector", lambda v: v.tensor_scalar(
        out=B.cols.ap()[:, 48:64], in0=B.cols.ap()[:, 32:48],
        scalar1=-1.0, scalar2=1.0, op0=ALU.mult, op1=ALU.add), deps=[t_sc])
    t_mm2 = p.do("tensor", lambda t: t.matmul(
        B.ps6[0:128, 16:32], B.ones_row.ap()[:, :], srow_sb.ap()[:, :],
        start=True, stop=True), deps=[t_v2, t_sc] + list(pr.const_toks))
    t_slc = p.do("vector", lambda v: v.tensor_copy(
        out=B.cols.ap()[:, 64:80], in_=B.ps6[0:128, 16:32]), deps=[t_mm2])

    ws = WStream(pr, inw, 8)
    q_toks, k_toks, v_toks = {}, {}, {}

    def rhs_xn(kc, a, b_):
        return xn[kc][:, a:b_]

    def evac_qkv(mb, n0, ps, dep):
        if mb < 8:
            tok = p.do("vector", lambda v, ps=ps, mb=mb, n0=n0: v.tensor_scalar(
                out=B.slot(8 + mb)[:, n0:n0 + 1024], in0=ps[:, 0:1024],
                scalar1=0.125, scalar2=None, op0=ALU.mult), deps=[dep])
            q_toks.setdefault(mb, []).append(tok)
        elif mb < 16:
            tok = p.do("vector", lambda v, ps=ps, mb=mb, n0=n0: v.tensor_copy(
                out=B.slot(16 + mb - 8)[:, n0:n0 + 1024], in_=ps[:, 0:1024]),
                deps=[dep])
            k_toks.setdefault(mb - 8, []).append(tok)
        else:
            tok = p.do("scalar", lambda s, ps=ps, mb=mb, n0=n0: s.activation(
                out=B.slot(24 + mb - 16)[:, n0:n0 + 1024], in_=ps[:, 0:1024],
                func=AF.Copy), deps=[dep])
            v_toks.setdefault(mb - 16, []).append(tok)
        return tok

    ev_qkv = mm_evac_stream(pr, Mb=24, Kc=8, ws=ws, rhs=rhs_xn, evac=evac_qkv,
                            deps_rhs=t_ln, region_init=t_ln)

    ksm_toks = {}
    prev_sm = None
    for c in range(8):
        ksl = B.slot(16 + c)
        dd = k_toks[c] + [t_sc, t_1ms]
        if prev_sm is not None:
            dd.append(prev_sm)
        ta = p.do("vector", lambda v, ksl=ksl, c=c: v.tensor_scalar(
            out=B.stg.ap()[:, 0:T], in0=ksl, scalar1=B.cols.ap()[:, 48 + c:49 + c],
            scalar2=None, op0=ALU.mult), deps=dd)
        tb = p.do("vector", lambda v, ksl=ksl, c=c: v.tensor_scalar(
            out=B.tmpf.ap()[:, 0:T].bitcast(BF16)[:, 0:T], in0=ksl,
            scalar1=B.cols.ap()[:, 32 + c:33 + c],
            scalar2=None, op0=ALU.mult), deps=dd)
        t0 = p.do("vector", lambda v, ksl=ksl: v.tensor_copy(
            out=ksl[:, 0:1], in_=B.stg.ap()[:, 0:1]), deps=[ta, tb])
        t1 = p.do("vector", lambda v, ksl=ksl: v.tensor_add(
            out=ksl[:, 1:T], in0=B.stg.ap()[:, 1:T],
            in1=B.tmpf.ap()[:, 0:T].bitcast(BF16)[:, 0:T - 1]), deps=[t0])
        ksm_toks[c] = t1
        prev_sm = t1

    def vtm(h, tkc):
        off = (32 + tkc // 2) * T + (tkc % 2) * 1024 + h * 64
        return B.arena.ap()[:, off:off + 64]

    vt_toks = {}
    prev_t = None
    for h in range(H):
        for tkc in range(16):
            dd = v_toks[(2 * h) // 2 if False else h // 2] + [prev_t]
            tt = p.do("tensor", lambda t, h=h, tkc=tkc: t.transpose(
                B.psT[0:128, (tkc % 8) * 128:(tkc % 8) * 128 + 64],
                B.slot(24 + h // 2)[(h % 2) * 64:(h % 2) * 64 + 64,
                                    tkc * 128:(tkc + 1) * 128],
                B.ident64.ap()[(h % 2) * 64:(h % 2) * 64 + 64, :]),
                deps=[d for d in dd if d])
            if tkc % 8 == 7:
                toks = []
                for j in range(8):
                    tkc0 = tkc - 7 + j
                    toks.append(p.do("vector", lambda v, h=h, tkc0=tkc0, j=j:
                                     v.tensor_copy(
                        out=vtm(h, tkc0), in_=B.psT[0:128, j * 128:j * 128 + 64]),
                        deps=[tt]))
                prev_t = toks[-1]
                vt_toks.setdefault(h, []).append(toks[-1])

    t_iota = p.dma("sync", B.tmpf.ap()[:, T:2 * T], iota[:, :],
                   deps=[prev_sm], sem="l1io")

    Sf = B.tmpf.ap()[:, 0:T]
    y_toks = {}
    psA_busy = None
    psT_busy = prev_t
    pt_busy = [None, None]
    av_busy = [None, None]
    it = 0
    units1 = [(h, tc) for h in range(H) for tc in range(16)]
    states1 = {}

    def a1_stageA(i):
        nonlocal psA_busy
        h, tc = units1[i]
        qsl = B.slot(8 + h // 2)
        W = (tc + 1) * 128
        sb = i % 2
        dd = q_toks[h // 2] + [ksm_toks[h // 2]]
        if psA_busy is not None:
            dd.append(psA_busy)
        last = None
        for n0 in range(0, W, 512):
            nn = min(512, W - n0)
            last = p.do("tensor", lambda t, h=h, tc=tc, n0=n0, nn=nn,
                        qsl=qsl: t.matmul(
                B.psA[0:128, n0:n0 + nn],
                qsl[(h % 2) * 64:(h % 2) * 64 + 64, tc * 128:(tc + 1) * 128],
                B.slot(16 + h // 2)[(h % 2) * 64:(h % 2) * 64 + 64, n0:n0 + nn],
                start=True, stop=True), deps=dd)
        t_ai = p.do("vector", lambda v, h=h, W=W: v.scalar_tensor_tensor(
            out=B.psA[0:128, 0:W], in0=B.tmpf.ap()[:, T:T + W],
            scalar=B.cols.ap()[:, 64 + h:65 + h], in1=B.psA[0:128, 0:W],
            op0=ALU.mult, op1=ALU.add), deps=[last, t_iota, t_slc])
        t_cz = p.do("vector", lambda v, tc=tc: v.tensor_add(
            out=B.psA[0:128, tc * 128:tc * 128 + 128],
            in0=B.psA[0:128, tc * 128:tc * 128 + 128], in1=caus_sb.ap()[:, :]),
            deps=[t_ai, t_v5])
        t_mx = p.do("vector", lambda v, W=W: v.tensor_reduce(
            out=B.cols.ap()[:, 88:89].rearrange("p one -> p one ()"),
            in_=B.psA[0:128, 0:W].rearrange("p w -> p () w"),
            axis=mybir.AxisListType.X, op=ALU.max, negate=True),
            deps=[t_cz])
        t_ex = p.do("scalar", lambda s, W=W, sb=sb: s.activation(
            out=B.slot(26 + sb)[:, 0:W],
            in_=B.psA[0:128, 0:W], func=AF.Exp, bias=B.cols.ap()[:, 88:89],
            accum_out=B.cols.ap()[:, 89:90]), deps=[t_mx, pt_busy[sb]])
        psA_busy = t_ex
        t_rd = p.do("vector", lambda v: v.reciprocal(
            out=B.cols.ap()[:, 90:91], in_=B.cols.ap()[:, 89:90]),
            deps=[t_ex])
        t_pn = p.do("vector", lambda v, W=W, sb=sb: v.tensor_scalar(
            out=B.slot(26 + sb)[:, 0:W], in0=B.slot(26 + sb)[:, 0:W],
            scalar1=B.cols.ap()[:, 90:91], scalar2=None, op0=ALU.mult),
            deps=[t_rd])
        states1[i] = dict(h=h, tc=tc, sb=sb, t_pn=t_pn)

    def a1_stageC(i):
        nonlocal psT_busy
        st_ = states1[i]
        h, tc, sb, t_pn = st_["h"], st_["tc"], st_["sb"], st_["t_pn"]
        ptsl = 24 + sb
        tts = []
        tev = None
        for j in range(tc + 1):
            dd2 = [t_pn]
            if j == 0:
                dd2.append(psT_busy)
                if pt_busy[sb] is not None:
                    dd2.append(pt_busy[sb])
            if j % 8 == 0 and tev is not None:
                dd2.append(tev)
            tts.append(p.do("tensor", lambda t, j=j, sb=sb: t.transpose(
                B.psT[0:128, (j % 8) * 128:(j % 8 + 1) * 128],
                B.slot(26 + sb)[:, j * 128:(j + 1) * 128],
                B.ident_bf.ap()[:, :]), deps=dd2))
            if j % 8 == 7 or j == tc:
                lo = (j // 8) * 8
                tev = p.do("vector", lambda v, lo=lo, j=j, ptsl=ptsl: v.tensor_copy(
                    out=B.slot(ptsl)[:, lo * 128:(j + 1) * 128],
                    in_=B.psT[0:128, 0:((j % 8) + 1) * 128]),
                    deps=[tts[-1]])
        psT_busy = tev
        pt_busy[sb] = tev
        st_["tev"] = tev

    def a1_stageD(i):
        st_ = states1[i]
        h, tc, sb, tev = st_["h"], st_["tc"], st_["sb"], st_["tev"]
        avb = B.ps4.ap() if sb == 0 else B.ps5.ap()
        dd3 = [tev, vt_toks[h][0], vt_toks[h][1]]
        if av_busy[sb] is not None:
            dd3.append(av_busy[sb])
        lastav = None
        for j in range(tc + 1):
            lastav = p.do("tensor", lambda t, h=h, j=j, avb=avb, sb=sb,
                          tc=tc: t.matmul(
                avb[0:64, 0:128], vtm(h, j),
                B.slot(24 + sb)[:, j * 128:(j + 1) * 128],
                start=(j == 0), stop=(j == tc)),
                deps=dd3 if j == 0 else [])
        t_ye = p.do("scalar", lambda s, h=h, tc=tc, avb=avb: s.activation(
            out=B.slot(h // 2)[(h % 2) * 64:(h % 2) * 64 + 64,
                               tc * 128:(tc + 1) * 128],
            in_=avb[0:64, 0:128], func=AF.Copy),
            deps=[lastav, ev_qkv[-1]])
        av_busy[sb] = t_ye
        y_toks.setdefault(h // 2, []).append(t_ye)
        del states1[i]

    NU1 = len(units1)
    for i in range(NU1 + 1):
        if i < NU1:
            a1_stageA(i)
        if i >= 1:
            a1_stageC(i - 1)
            a1_stageD(i - 1)
    it += NU1

    ws_o = WStream(pr, outw, 8)
    o_toks = {}

    def rhs_y(kc, a, b_):
        return B.slot(kc)[:, a:b_]

    def evac_o(mb, n0, ps, dep):
        tok = p.do("scalar", lambda s, ps=ps, mb=mb, n0=n0: s.activation(
            out=B.slot(8 + mb)[:, n0:n0 + 1024], in_=ps[:, 0:1024],
            func=AF.Copy), deps=[dep])
        o_toks.setdefault(mb, []).append(tok)
        return tok

    def mm_deps_o(mb, kc, n0):
        return y_toks[kc]

    ev_o = mm_evac_stream(pr, Mb=8, Kc=8, ws=ws_o, rhs=rhs_y, evac=evac_o,
                          extra_mm_deps=mm_deps_o, region_init=[psA_busy])

    def gc2(c):
        return B.cols.ap()[:, 16 + c:17 + c]

    def bc2(c):
        return B.cols.ap()[:, 24 + c:25 + c]

    o_sl = [B.slot(8 + c) for c in range(8)]
    t_ln2, _ = emit_ln(pr, None, xn_slot=16, xb_slot=None, x2_slot=24,
                       gcol=gc2, bcol=bc2, src_sbuf=o_sl,
                       src_deps=lambda c: ev_o[-1], cols_dep=[t_cols])
    vout = [B.slot(16 + c) for c in range(8)]

    def delta_src(c):
        return (vout[c], [t_ln2[c]])

    st = emit_residual_add(pr, x_in, x_out, delta_src, 1.0, lambda c: [],
                           first_deps=[t_ln2[-1]])
    return st


def host_pack_l1_attn(ins):
    H = 16
    inw = ins["L1_attn_in_w"].astype(np.float32)
    outw = ins["L1_attn_out_w"].astype(np.float32)
    cols = np.zeros((128, 32), np.float32)
    cols[:, 0:8] = pack_cols(ins["L1_attn_ln1_g"].astype(np.float32))
    cols[:, 8:16] = pack_cols(ins["L1_attn_ln1_b"].astype(np.float32))
    cols[:, 16:24] = pack_cols(ins["L1_attn_ln2_g"].astype(np.float32))
    cols[:, 24:32] = pack_cols(ins["L1_attn_ln2_b"].astype(np.float32))
    vecs = np.zeros((16, 2), np.float32)
    vecs[:, 0] = ins["L1_attn_slopes"]
    vecs[:, 1] = ins["L1_attn_smear"]
    selm = np.zeros((16, 128), np.float32)
    for k in range(16):
        selm[k, (k % 2) * 64:(k % 2) * 64 + 64] = 1.0
    selr = np.zeros((16, 16), np.float32)
    for k in range(16):
        selr[k, k // 2] = 1.0
    iota = np.tile(np.arange(T, dtype=np.float32)[None, :], (128, 1))
    caus = np.triu(np.full((128, 128), -1e9, np.float32), 1)
    return {
        "L1a_inwp": pack_lhsT(np.ascontiguousarray(inw.T)),
        "L1a_outwp": pack_lhsT(np.ascontiguousarray(outw.T)),
        "L1a_cols": cols,
        "L1a_vecs": vecs,
        "L1a_sloperow": np.ascontiguousarray(ins["L1_attn_slopes"].astype(np.float32)[None, :]),
        "L1a_selmask": selm,
        "L1a_selrhs": selr,
        "L1a_iota": iota,
        "L1a_caus": caus,
    }


def build_l1_attn_program():
    pr = Prog()
    pr.load_consts()
    x_in = pr.din("x_cur", [D, T])
    x_out = pr.dout("x_next", [D, T])
    st = emit_l1_attn(pr, x_in, x_out)
    pr.out_tokens = st
    pr.finish()
    return pr


def run_sublayer_l1attn(x_full, ins):
    pr = build_l1_attn_program()
    packed = host_pack_l1_attn(ins)
    consts = host_consts()
    in_maps = []
    for core in range(8):
        m = dict(consts)
        m.update(packed)
        m["x_cur"] = np.ascontiguousarray(x_full[core].T)
        in_maps.append(m)
    res = run_bass_kernel_spmd(pr.nc, in_maps, core_ids=list(range(8)))
    out = np.stack([res.results[c]["x_next"].T for c in range(8)])
    pr.close()
    return out




def emit_l0_attn(pr, x_in, x_out, lname="L0a"):
    B, p = pr.B, pr.p
    wq = pr.din(lname + "_wqp", [128, 8 * 8 * 128])
    wkv = pr.din(lname + "_wkvp", [128, 1 * 8 * 128])
    wup = pr.din(lname + "_wup", [64, 256])
    wo = pr.din(lname + "_wop", [128, 8 * 8 * 128])
    caus = pr.din("L1a_caus", [128, 128])

    if not hasattr(pr, "_l0a_small"):
        st = pr.stack.enter_context
        pr._l0a_small = (
            st(pr.nc.sbuf_tensor(lname + "_wup_sb", [64, 256], BF16)),
            st(pr.nc.sbuf_tensor(lname + "_caus_sb", [128, 128], F32)),
        )
    wup_sb, caus_sb = pr._l0a_small
    t_wu = p.dma("gpsimd", wup_sb.ap()[:, :], wup[:, :], sem="l0wu")
    t_cz = p.dma("sync", caus_sb.ap()[:, :], caus[:, :], sem="l0cz")

    t_ln, _ = emit_ln(pr, x_in, xn_slot=0, xb_slot=8, x2_slot=16)
    xn = [B.slot(c) for c in range(8)]

    ws_q = WStream(pr, wq, 8)
    q_toks = {}
    QSC = float(128 ** -0.5)

    def rhs_xn(kc, a, b_):
        return xn[kc][:, a:b_]

    def evac_q(mb, n0, ps, dep):
        tok = p.do("vector", lambda v, ps=ps, mb=mb, n0=n0: v.tensor_scalar(
            out=B.slot(8 + mb)[:, n0:n0 + 1024], in0=ps[:, 0:1024],
            scalar1=QSC, scalar2=None, op0=ALU.mult), deps=[dep])
        q_toks.setdefault(mb, []).append(tok)
        return tok

    ev_q = mm_evac_stream(pr, Mb=8, Kc=8, ws=ws_q, rhs=rhs_xn, evac=evac_q,
                          deps_rhs=t_ln, region_init=t_ln)

    ws_kv = WStream(pr, wkv, 8)
    kv_toks = []

    def evac_kv(mb, n0, ps, dep):
        p.do("scalar", lambda s, ps=ps, n0=n0: s.activation(
            out=B.slot(22)[0:64, n0:n0 + 1024], in_=ps[0:64, 0:1024],
            func=AF.Copy), deps=[dep])
        tok = p.do("scalar", lambda s, ps=ps, n0=n0: s.activation(
            out=B.slot(23)[0:64, n0:n0 + 1024], in_=ps[64:128, 0:1024],
            func=AF.Copy), deps=[dep])
        kv_toks.append(tok)
        return tok

    ev_kv = mm_evac_stream(pr, Mb=1, Kc=8, ws=ws_kv, rhs=rhs_xn, evac=evac_kv,
                           region_init=[ev_q[-1]])

    kf_toks, vf_toks = {}, {}
    prev = ev_kv[-1]
    for kv in range(2):
        for what in range(2):
            dst = B.slot(16 + 2 * what + kv)
            for n0 in range(0, T, 512):
                tt = p.do("tensor", lambda t, kv=kv, what=what, n0=n0: t.matmul(
                    B.ps6[0:128, 0:512],
                    wup_sb.ap()[kv * 32:kv * 32 + 32,
                                what * 128:what * 128 + 128],
                    B.slot(22 + what)[kv * 32:kv * 32 + 32, n0:n0 + 512],
                    start=True, stop=True),
                    deps=[t_wu, kv_toks[-1], prev])
                tok = p.do("vector", lambda v, dst=dst, n0=n0: v.tensor_copy(
                    out=dst[:, n0:n0 + 512], in_=B.ps6[0:128, 0:512]),
                    deps=[tt])
                prev = tok
                (kf_toks if what == 0 else vf_toks).setdefault(kv, []).append(tok)

    vt_toks = {}
    prev_t = None
    for kv in range(2):
        for tkc in range(16):
            dd = vf_toks[kv] + ([prev_t] if prev_t else [])
            tt = p.do("tensor", lambda t, kv=kv, tkc=tkc: t.transpose(
                B.psT[0:128, (tkc % 8) * 128:(tkc % 8 + 1) * 128],
                B.slot(18 + kv)[:, tkc * 128:(tkc + 1) * 128],
                B.ident_bf.ap()[:, :]), deps=dd)
            if tkc % 8 == 7:
                tok = p.do("vector", lambda v, kv=kv, tkc=tkc: v.tensor_copy(
                    out=B.slot(20 + kv)[:, (tkc - 7) * 128:(tkc + 1) * 128],
                    in_=B.psT[0:128, 0:1024]), deps=[tt])
                prev_t = tok
                vt_toks.setdefault(kv, []).append(tok)

    y_toks = {}
    psA_busy = ev_q[-1]
    psT_busy = prev_t
    pt_busy = [None, None]
    av_busy = [None, None]
    it = 0
    units0 = [(c, tc) for c in range(8) for tc in range(16)]
    states0 = {}

    def a0_stageA(i):
        c, tc = units0[i]
        kv = c // 4
        W = (tc + 1) * 128
        nonlocal psA_busy
        dd = q_toks[c] + kf_toks[kv] + [psA_busy]
        last = None
        for n0 in range(0, W, 512):
            nn = min(512, W - n0)
            last = p.do("tensor", lambda t, c=c, kv=kv, tc=tc, n0=n0,
                        nn=nn: t.matmul(
                B.psA[0:128, n0:n0 + nn],
                B.slot(8 + c)[:, tc * 128:(tc + 1) * 128],
                B.slot(16 + kv)[:, n0:n0 + nn],
                start=True, stop=True), deps=dd)
        t_cm = p.do("vector", lambda v, tc=tc: v.tensor_add(
            out=B.psA[0:128, tc * 128:tc * 128 + 128],
            in0=B.psA[0:128, tc * 128:tc * 128 + 128],
            in1=caus_sb.ap()[:, :]), deps=[last, t_cz])
        t_mx = p.do("vector", lambda v, W=W: v.tensor_reduce(
            out=B.cols.ap()[:, 88:89].rearrange("p one -> p one ()"),
            in_=B.psA[0:128, 0:W].rearrange("p w -> p () w"),
            axis=mybir.AxisListType.X, op=ALU.max, negate=True),
            deps=[t_cm])
        states0[i] = dict(c=c, tc=tc, kv=kv, W=W, t_mx=t_mx)

    def a0_stageB(i):
        nonlocal psA_busy
        st_ = states0[i]
        W, tc = st_["W"], st_["tc"]
        sb = i % 2
        t_ex = p.do("scalar", lambda s, W=W, sb=sb: s.activation(
            out=B.slot(26 + sb)[:, 0:W], in_=B.psA[0:128, 0:W], func=AF.Exp,
            bias=B.cols.ap()[:, 88:89], accum_out=B.cols.ap()[:, 89:90]),
            deps=[st_["t_mx"], pt_busy[sb]])
        psA_busy = t_ex
        t_rd = p.do("vector", lambda v: v.reciprocal(
            out=B.cols.ap()[:, 90:91], in_=B.cols.ap()[:, 89:90]),
            deps=[t_ex])
        t_pn = p.do("vector", lambda v, W=W, sb=sb: v.tensor_scalar(
            out=B.slot(26 + sb)[:, 0:W], in0=B.slot(26 + sb)[:, 0:W],
            scalar1=B.cols.ap()[:, 90:91], scalar2=None, op0=ALU.mult),
            deps=[t_rd])
        st_["t_pn"] = t_pn
        st_["sb"] = sb

    def a0_stageC(i):
        nonlocal psT_busy
        st_ = states0[i]
        tc, sb, t_pn = st_["tc"], st_["sb"], st_["t_pn"]
        
        ptsl = 24 + sb
        tts = []
        tev = None
        for j in range(tc + 1):
            dd2 = [t_pn]
            if j == 0:
                dd2.append(psT_busy)
                if pt_busy[sb] is not None:
                    dd2.append(pt_busy[sb])
            if j % 8 == 0 and tev is not None:
                dd2.append(tev)
            tts.append(p.do("tensor", lambda t, j=j, sb=sb: t.transpose(
                B.psT[0:128, (j % 8) * 128:(j % 8 + 1) * 128],
                B.slot(26 + sb)[:, j * 128:(j + 1) * 128],
                B.ident_bf.ap()[:, :]), deps=dd2))
            if j % 8 == 7 or j == tc:
                lo = (j // 8) * 8
                tev = p.do("scalar", lambda s, lo=lo, j=j, ptsl=ptsl:
                           s.activation(
                    out=B.slot(ptsl)[:, lo * 128:(j + 1) * 128],
                    in_=B.psT[0:128, 0:((j % 8) + 1) * 128], func=AF.Copy),
                    deps=[tts[-1]])
        psT_busy = tev
        pt_busy[sb] = tev
        st_["tev"] = tev

    def a0_stageD(i):
        st_ = states0[i]
        c, tc, kv, sb, tev = st_["c"], st_["tc"], st_["kv"], st_["sb"], st_["tev"]
        avb = B.ps4.ap() if sb == 0 else B.ps5.ap()
        dd3 = [tev] + vt_toks[kv]
        if av_busy[sb] is not None:
            dd3.append(av_busy[sb])
        lastav = None
        for j in range(tc + 1):
            lastav = p.do("tensor", lambda t, kv=kv, j=j, avb=avb, sb=sb,
                          tc=tc: t.matmul(
                avb[0:128, 0:128],
                B.slot(20 + kv)[:, j * 128:(j + 1) * 128],
                B.slot(24 + sb)[:, j * 128:(j + 1) * 128],
                start=(j == 0), stop=(j == tc)),
                deps=dd3 if j == 0 else [])
        t_ye = p.do("scalar", lambda s, c=c, tc=tc, avb=avb: s.activation(
            out=B.slot(c)[:, tc * 128:(tc + 1) * 128],
            in_=avb[0:128, 0:128], func=AF.Copy),
            deps=[lastav, ev_q[-1], ev_kv[-1]])
        av_busy[sb] = t_ye
        y_toks.setdefault(c, []).append(t_ye)
        del states0[i]

    NU0 = len(units0)
    for i in range(NU0 + 1):
        if i < NU0:
            a0_stageA(i)
            a0_stageB(i)
        if i >= 1:
            a0_stageC(i - 1)
            a0_stageD(i - 1)
    it += NU0

    ws_o = WStream(pr, wo, 8)
    dtoks = {}

    def rhs_y(kc, a, b_):
        return B.slot(kc)[:, a:b_]

    def evac_o(mb, n0, ps, dep):
        tok = p.do("scalar", lambda s, ps=ps, mb=mb, n0=n0: s.activation(
            out=B.slot(8 + mb)[:, n0:n0 + 1024], in_=ps[:, 0:1024],
            func=AF.Copy), deps=[dep])
        dtoks.setdefault(mb, []).append(tok)
        return tok

    def mm_deps_o(mb, kc, n0):
        return y_toks[kc]

    ev_o = mm_evac_stream(pr, Mb=8, Kc=8, ws=ws_o, rhs=rhs_y, evac=evac_o,
                          extra_mm_deps=mm_deps_o, region_init=[psA_busy])

    def delta_src(c):
        return (B.slot(8 + c), dtoks[c])

    st = emit_residual_add(pr, x_in, x_out, delta_src, 1.0, lambda c: [],
                           first_deps=[ev_o[-1]])
    return st


def host_pack_l0_attn(ins):
    wkv = np.concatenate([ins["L0_attn_wk_down"], ins["L0_attn_wv_down"]],
                         1).astype(np.float32)
    wup = np.zeros((64, 256), np.float32)
    wup[0:32, 0:128] = ins["L0_attn_wk_up"][:, :]
    wup[32:64, 0:128] = ins["L0_attn_wk_up"][:, :]
    wup[0:32, 128:256] = ins["L0_attn_wv_up"][:, :]
    wup[32:64, 128:256] = ins["L0_attn_wv_up"][:, :]
    return {
        "L0a_wqp": pack_lhsT(ins["L0_attn_wq"].astype(np.float32)),
        "L0a_wkvp": pack_lhsT(wkv),
        "L0a_wup": wup,
        "L0a_wop": pack_lhsT(ins["L0_attn_wo"].astype(np.float32)),
    }


def build_l0_attn_program():
    pr = Prog()
    pr.load_consts()
    x_in = pr.din("x_cur", [D, T])
    x_out = pr.dout("x_next", [D, T])
    st = emit_l0_attn(pr, x_in, x_out)
    pr.out_tokens = st
    pr.finish()
    return pr


def run_sublayer_l0attn(x_full, ins):
    pr = build_l0_attn_program()
    packed = host_pack_l0_attn(ins)
    packed["L1a_caus"] = np.triu(np.full((128, 128), -1e9, np.float32), 1)
    consts = host_consts()
    in_maps = []
    for core in range(8):
        m = dict(consts)
        m.update(packed)
        m["x_cur"] = np.ascontiguousarray(x_full[core].T)
        in_maps.append(m)
    res = run_bass_kernel_spmd(pr.nc, in_maps, core_ids=list(range(8)))
    out = np.stack([res.results[c]["x_next"].T for c in range(8)])
    pr.close()
    return out




def emit_l3_attn(pr, x_in, x_out, lname="L3a", stage=99):
    B, p = pr.B, pr.p
    H, HD = 16, 64
    wqkv = pr.din(lname + "_wqkvp", [128, 24 * 8 * 128])
    wg = pr.din(lname + "_wgp", [128, 8 * 8 * 128])
    wpr = pr.din(lname + "_wprp", [128, 8 * 8 * 128])
    ropes = pr.din(lname + "_rope", [128, 4 * T], BF16)
    dec = pr.din(lname + "_dec", [128, 16 * 256], BF16)
    colsd = pr.din(lname + "_cols", [128, 16])
    gsc = host_gamma_scalars()

    t_cols = p.dma("sync", B.cols.ap()[:, 0:16], colsd[:, :], deps=pr.const_toks,
                   sem="col")

    t_ln, _ = emit_ln(pr, x_in, xn_slot=0, xb_slot=8, x2_slot=16)
    xn = [B.slot(c) for c in range(8)]

    t_rope = p.dma("gpsimd", B.slot(32, 4), ropes[:, :], sem="l3rope",
                   deps=t_ln)

    ws = WStream(pr, wqkv, 8)
    q_toks, k_toks, v_toks = {}, {}, {}

    def rhs_xn(kc, a, b_):
        return xn[kc][:, a:b_]

    def evac_qkv(mb, n0, ps, dep):
        if mb < 8:
            tok = p.do("vector", lambda v, ps=ps, mb=mb, n0=n0: v.tensor_scalar(
                out=B.slot(8 + mb)[:, n0:n0 + 1024], in0=ps[:, 0:1024],
                scalar1=1.0, scalar2=None, op0=ALU.mult), deps=[dep])
            q_toks.setdefault(mb, []).append(tok)
        elif mb < 16:
            tok = p.do("vector", lambda v, ps=ps, mb=mb, n0=n0: v.tensor_copy(
                out=B.slot(16 + mb - 8)[:, n0:n0 + 1024], in_=ps[:, 0:1024]),
                deps=[dep])
            k_toks.setdefault(mb - 8, []).append(tok)
        else:
            tok = p.do("scalar", lambda s, ps=ps, mb=mb, n0=n0: s.activation(
                out=B.slot(24 + mb - 16)[:, n0:n0 + 1024], in_=ps[:, 0:1024],
                func=AF.Copy), deps=[dep])
            v_toks.setdefault(mb - 16, []).append(tok)
        return tok

    ev_qkv = mm_evac_stream(pr, Mb=24, Kc=8, ws=ws, rhs=rhs_xn, evac=evac_qkv,
                            deps_rhs=t_ln, region_init=t_ln)

    if stage <= 1:
        def delta_src(c):
            return (B.slot(8 + c), q_toks[c])
        return emit_residual_add(pr, x_in, x_out, delta_src, 1.0,
                                 lambda c: [], first_deps=[ev_qkv[-1]])

    rope_done = {}
    ps_busy = ev_qkv[-1]
    for what in range(2):
        base_slot = 8 if what == 0 else 16
        cs = B.slot(32 + 2 * what)
        ss = B.slot(33 + 2 * what)
        for h in range(H):
            r0 = (h % 2) * 64
            sl = B.slot(base_slot + h // 2)
            for half in range(2):
                hs = slice(half * 1024, (half + 1) * 1024)
                dd0 = (q_toks if what == 0 else k_toks)[h // 2] + [t_rope,
                                                                   ps_busy]
                mm2 = []
                for qq in range(2):
                    n0 = half * 1024 + qq * 512
                    mm2.append(p.do("tensor", lambda t, h=h, r0=r0, sl=sl,
                                    n0=n0, qq=qq: t.matmul(
                        (B.ps6 if qq == 0 else B.ps5)[r0:r0 + 64, 0:512],
                        B.rot64.ap()[r0:r0 + 64, :],
                        sl[r0:r0 + 64, n0:n0 + 512],
                        start=True, stop=True), deps=dd0))
                t1 = p.do("vector", lambda v, sl=sl, r0=r0, hs=hs, cs=cs:
                          v.tensor_mul(
                    out=B.stg.ap()[r0:r0 + 64, 0:1024],
                    in0=sl[r0:r0 + 64, hs], in1=cs[r0:r0 + 64, hs]),
                    deps=[t_rope] + dd0)
                t2 = p.do("vector", lambda v, sl=sl, r0=r0, half=half, ss=ss:
                          v.tensor_mul(
                    out=sl[r0:r0 + 64, half * 1024:half * 1024 + 512],
                    in0=B.ps6[r0:r0 + 64, 0:512],
                    in1=ss[r0:r0 + 64, half * 1024:half * 1024 + 512]),
                    deps=[mm2[0], t1])
                t2b = p.do("vector", lambda v, sl=sl, r0=r0, half=half, ss=ss:
                           v.tensor_mul(
                    out=sl[r0:r0 + 64, half * 1024 + 512:half * 1024 + 1024],
                    in0=B.ps5[r0:r0 + 64, 0:512],
                    in1=ss[r0:r0 + 64, half * 1024 + 512:half * 1024 + 1024]),
                    deps=[mm2[1], t2])
                t3 = p.do("vector", lambda v, sl=sl, r0=r0, hs=hs: v.tensor_add(
                    out=sl[r0:r0 + 64, hs], in0=sl[r0:r0 + 64, hs],
                    in1=B.stg.ap()[r0:r0 + 64, 0:1024]), deps=[t2b])
                ps_busy = t3
                rope_done.setdefault((what, h), []).append(t3)

    if stage <= 2:
        def delta_src(c):
            return (B.slot(8 + c), [ps_busy])
        return emit_residual_add(pr, x_in, x_out, delta_src, 1.0,
                                 lambda c: [], first_deps=[ps_busy])

    def vtm(h, tkc):
        off = (32 + tkc // 2) * T + (tkc % 2) * 1024 + h * 64
        return B.arena.ap()[:, off:off + 64]

    vt_toks = {}
    prev_t = None
    for h in range(H):
        for tkc in range(16):
            dd = v_toks[h // 2] + [ps_busy] + ([prev_t] if prev_t else [])
            tt = p.do("tensor", lambda t, h=h, tkc=tkc: t.transpose(
                B.psT[0:128, (tkc % 8) * 128:(tkc % 8) * 128 + 64],
                B.slot(24 + h // 2)[(h % 2) * 64:(h % 2) * 64 + 64,
                                    tkc * 128:(tkc + 1) * 128],
                B.ident64.ap()[(h % 2) * 64:(h % 2) * 64 + 64, :]),
                deps=[d for d in dd if d])
            if tkc % 8 == 7:
                toks = []
                for j in range(8):
                    tkc0 = tkc - 7 + j
                    toks.append(p.do("vector", lambda v, h=h, tkc0=tkc0, j=j:
                                     v.tensor_copy(
                        out=vtm(h, tkc0), in_=B.psT[0:128, j * 128:j * 128 + 64]),
                        deps=[tt]))
                prev_t = toks[-1]
                vt_toks.setdefault(h, []).append(toks[-1])

    t_dec = p.dma("gpsimd", B.slot(26, 2), dec[:, :], sem="l3dec",
                  deps=[prev_t])

    y_toks = {}
    psA_busy = ps_busy
    pt_busy = [None, None]
    av_busy = [None, None]
    it = 0
    for h in range(H):
        ksl = B.slot(16 + h // 2)
        qsl = B.slot(8 + h // 2)
        r0 = (h % 2) * 64
        for tc in range(16):
            sb = it % 2
            stt_toks = []
            last = None
            for kc in range(tc + 1):
                dd = rope_done[(0, h)] + rope_done[(1, h)]
                if kc == 0:
                    dd = dd + [psA_busy]
                last = p.do("tensor", lambda t, h=h, r0=r0, tc=tc, kc=kc,
                            ksl=ksl, qsl=qsl: t.matmul(
                    B.psA[0:128, kc * 128:(kc + 1) * 128],
                    ksl[r0:r0 + 64, kc * 128:(kc + 1) * 128],
                    qsl[r0:r0 + 64, tc * 128:(tc + 1) * 128],
                    start=True, stop=True), deps=dd)
                stt_toks.append(last)
            ev_toks = []
            for kc in range(tc + 1):
                dtile = B.slot(26, 2)[:, h * 256 + (128 if tc != kc else 0):
                                      h * 256 + (256 if tc != kc else 128)]
                imm = float(gsc[h] ** (128 * (tc - kc)) * 0.125)
                dd2 = [stt_toks[-1], t_dec]
                if pt_busy[sb] is not None and kc == 0:
                    dd2.append(pt_busy[sb])
                ev_toks.append(p.do("vector", lambda v, kc=kc, sb=sb,
                                    dtile=dtile, imm=imm: v.scalar_tensor_tensor(
                    out=B.slot(24 + sb)[:, kc * 128:(kc + 1) * 128],
                    in0=B.psA[0:128, kc * 128:(kc + 1) * 128], scalar=imm,
                    in1=dtile, op0=ALU.mult, op1=ALU.mult), deps=dd2))
            psA_busy = ev_toks[-1]
            pt_busy[sb] = ev_toks[-1]
            avb = B.ps4.ap() if sb == 0 else B.ps5.ap()
            dd3 = [ev_toks[-1]] + vt_toks[h]
            if av_busy[sb] is not None:
                dd3.append(av_busy[sb])
            lastav = None
            for j in range(tc + 1):
                lastav = p.do("tensor", lambda t, h=h, j=j, avb=avb, sb=sb,
                              tc=tc: t.matmul(
                    avb[0:64, 0:128], vtm(h, j),
                    B.slot(24 + sb)[:, j * 128:(j + 1) * 128],
                    start=(j == 0), stop=(j == tc)),
                    deps=dd3 if j == 0 else [])
            t_ye = p.do("scalar", lambda s, h=h, tc=tc, avb=avb: s.activation(
                out=B.slot(h // 2)[(h % 2) * 64:(h % 2) * 64 + 64,
                                   tc * 128:(tc + 1) * 128],
                in_=avb[0:64, 0:128], func=AF.Copy),
                deps=[lastav, ev_qkv[-1]])
            av_busy[sb] = t_ye
            y_toks.setdefault(h, []).append(t_ye)
            it += 1

    if stage <= 4:
        def delta_src(c):
            return (B.slot(c), y_toks[2 * c] + y_toks[2 * c + 1])
        return emit_residual_add(pr, x_in, x_out, delta_src, 1.0,
                                 lambda c: [], first_deps=[psA_busy])

    gn_ab = {}
    prev_gn = None
    for h in range(H):
        r0 = (h % 2) * 64
        ysl = B.slot(h // 2)
        dd = y_toks[h] + ([prev_gn] if prev_gn else [])
        t_sq = p.do("scalar", lambda s, ysl=ysl, r0=r0: s.activation(
            out=B.stg.ap()[r0:r0 + 64, 0:T], in_=ysl[r0:r0 + 64, :],
            func=AF.Square), deps=dd)
        mm = None
        for nt in range(4):
            mm = p.do("tensor", lambda t, ysl=ysl, r0=r0, nt=nt: t.matmul(
                B.ps6[0:1, 0:512], B.onescol_bf.ap()[r0:r0 + 64, :],
                ysl[r0:r0 + 64, nt * 512:(nt + 1) * 512],
                start=(nt == 0), stop=(nt == 3)), deps=dd + list(pr.const_toks))
        for nt in range(4):
            mm = p.do("tensor", lambda t, r0=r0, nt=nt: t.matmul(
                B.ps6[32:33, 0:512], B.onescol_bf.ap()[r0:r0 + 64, :],
                B.stg.ap()[r0:r0 + 64, nt * 512:(nt + 1) * 512],
                start=(nt == 0), stop=(nt == 3)), deps=[t_sq])
        t_r1 = p.do("vector", lambda v: v.tensor_reduce(
            out=B.cols.ap()[0:1, 96:97].rearrange("p one -> p one ()"),
            in_=B.ps6[0:1, 0:512].rearrange("p w -> p () w"),
            axis=mybir.AxisListType.X, op=ALU.add), deps=[mm])
        t_r2 = p.do("vector", lambda v: v.tensor_reduce(
            out=B.cols.ap()[0:1, 97:98].rearrange("p one -> p one ()"),
            in_=B.ps6[32:33, 0:512].rearrange("p w -> p () w"),
            axis=mybir.AxisListType.X, op=ALU.add), deps=[mm])
        NN = 64.0 * T
        t_mu = p.do("vector", lambda v: v.tensor_scalar(
            out=B.cols.ap()[0:1, 98:99], in0=B.cols.ap()[0:1, 96:97],
            scalar1=1.0 / NN, scalar2=None, op0=ALU.mult), deps=[t_r1])
        t_m2 = p.do("vector", lambda v: v.tensor_scalar(
            out=B.cols.ap()[0:1, 99:100], in0=B.cols.ap()[0:1, 97:98],
            scalar1=1.0 / NN, scalar2=None, op0=ALU.mult), deps=[t_r2])
        t_mu2 = p.do("scalar", lambda s: s.activation(
            out=B.cols.ap()[0:1, 100:101], in_=B.cols.ap()[0:1, 98:99],
            func=AF.Square), deps=[t_mu])
        t_var = p.do("vector", lambda v: v.tensor_sub(
            out=B.cols.ap()[0:1, 99:100], in0=B.cols.ap()[0:1, 99:100],
            in1=B.cols.ap()[0:1, 100:101]), deps=[t_mu2, t_m2])
        t_lnv = p.do("scalar", lambda s: s.activation(
            out=B.cols.ap()[0:1, 100:101], in_=B.cols.ap()[0:1, 99:100],
            func=AF.Ln, bias=B.cp(1e-5)[0:1, :]), deps=[t_var])
        t_rs = p.do("scalar", lambda s: s.activation(
            out=B.cols.ap()[0:1, 101:102], in_=B.cols.ap()[0:1, 100:101],
            func=AF.Exp, scale=-0.5), deps=[t_lnv])
        t_bc = p.do("tensor", lambda t: t.matmul(
            B.ps6[0:128, 510:511], B.ones_row.ap()[:, :],
            B.cols.ap()[0:1, 98:99], start=True, stop=True),
            deps=[t_rs] + list(pr.const_toks))
        t_bc2 = p.do("tensor", lambda t: t.matmul(
            B.ps6[0:128, 508:509], B.ones_row.ap()[:, :],
            B.cols.ap()[0:1, 101:102], start=True, stop=True), deps=[t_rs, t_bc])
        ci = 104 + (h % 2) * 2
        t_a = p.do("vector", lambda v, h=h, ci=ci: v.tensor_mul(
            out=B.cols.ap()[:, ci:ci + 1], in0=B.ps6[0:128, 508:509],
            in1=B.cols.ap()[:, h // 2:h // 2 + 1]), deps=[t_bc2, t_cols,
                                                          prev_gn or t_bc2])
        t_b = p.do("vector", lambda v, h=h, ci=ci: v.scalar_tensor_tensor(
            out=B.cols.ap()[:, ci + 1:ci + 2], in0=B.ps6[0:128, 510:511],
            scalar=B.cols.ap()[:, ci:ci + 1], in1=B.cols.ap()[:, 8 + h // 2:
                                                              9 + h // 2],
            op0=ALU.mult, op1=ALU.subtract), deps=[t_a])
        t_bf = p.do("vector", lambda v, ci=ci: v.tensor_scalar(
            out=B.cols.ap()[:, ci + 1:ci + 2], in0=B.cols.ap()[:, ci + 1:ci + 2],
            scalar1=-1.0, scalar2=None, op0=ALU.mult), deps=[t_b])
        t_ap = p.do("vector", lambda v, ysl=ysl, r0=r0, ci=ci: v.tensor_scalar(
            out=ysl[r0:r0 + 64, :], in0=ysl[r0:r0 + 64, :],
            scalar1=B.cols.ap()[r0:r0 + 64, ci:ci + 1] if False else
            B.cols.ap()[:, ci:ci + 1][r0:r0 + 64, :],
            scalar2=B.cols.ap()[:, ci + 1:ci + 2][r0:r0 + 64, :],
            op0=ALU.mult, op1=ALU.add), deps=[t_bf])
        prev_gn = t_ap
        gn_ab[h] = t_ap

    if stage <= 5:
        def delta_src(c):
            return (B.slot(c), [gn_ab[2 * c], gn_ab[2 * c + 1]])
        return emit_residual_add(pr, x_in, x_out, delta_src, 1.0,
                                 lambda c: [], first_deps=[prev_gn])

    t_ln2, _ = emit_ln(pr, x_in, xn_slot=24, xb_slot=8, x2_slot=16,
                       deps=[prev_gn, psA_busy])
    xn2 = [B.slot(24 + c) for c in range(8)]

    ws_g = WStream(pr, wg, 8)
    g_toks = {}

    def rhs_xn2(kc, a, b_):
        return xn2[kc][:, a:b_]

    def evac_g(mb, n0, ps, dep):
        tok = p.do("scalar", lambda s, ps=ps, mb=mb, n0=n0: s.activation(
            out=B.slot(8 + mb)[:, n0:n0 + 1024], in_=ps[:, 0:1024],
            func=AF.Silu), deps=[dep])
        g_toks.setdefault(mb, []).append(tok)
        return tok

    ev_g = mm_evac_stream(pr, Mb=8, Kc=8, ws=ws_g, rhs=rhs_xn2, evac=evac_g,
                          deps_rhs=t_ln2, region_init=t_ln2)
    yg_toks = {}
    for c in range(8):
        tok = p.do("vector", lambda v, c=c: v.tensor_mul(
            out=B.slot(c), in0=B.slot(c), in1=B.slot(8 + c)),
            deps=g_toks[c] + [gn_ab[2 * c], gn_ab[2 * c + 1]])
        yg_toks[c] = tok

    ws_o = WStream(pr, wpr, 8)
    dtoks = {}

    def rhs_yg(kc, a, b_):
        return B.slot(kc)[:, a:b_]

    def evac_o(mb, n0, ps, dep):
        tok = p.do("scalar", lambda s, ps=ps, mb=mb, n0=n0: s.activation(
            out=B.slot(16 + mb)[:, n0:n0 + 1024], in_=ps[:, 0:1024],
            func=AF.Copy), deps=[dep])
        dtoks.setdefault(mb, []).append(tok)
        return tok

    def mm_deps_o(mb, kc, n0):
        return [yg_toks[kc]]

    ev_o = mm_evac_stream(pr, Mb=8, Kc=8, ws=ws_o, rhs=rhs_yg, evac=evac_o,
                          extra_mm_deps=mm_deps_o, region_init=ev_g[-2:])

    def delta_src(c):
        return (B.slot(16 + c), dtoks[c])

    st = emit_residual_add(pr, x_in, x_out, delta_src, 1.0, lambda c: [],
                           first_deps=[ev_o[-1]])
    return st


def _cat_ps(B):
    return None


def host_gamma_scalars():
    return [float(1.0 - 2.0 ** (-5.0 - h)) for h in range(16)]


def host_pack_l3_attn(ins):
    H, DH, TT_ = 16, 64, T
    half = np.arange(0, DH, 2, dtype=np.float64)
    inv_freq = 1.0 / (10000.0 ** (half / DH))
    t = np.arange(TT_, dtype=np.float64)
    freqs = np.repeat(t[:, None] * inv_freq[None, :], 2, axis=-1)
    scale_vec = (half + 0.4 * DH) / (1.4 * DH)
    power = (t - TT_ // 2) / 512.0
    scale = np.repeat(scale_vec[None, :] ** power[:, None], 2, axis=-1)
    cos, sin = np.cos(freqs), np.sin(freqs)
    CS = (cos * scale).T
    SS = (sin * scale).T
    CSi = (cos / scale).T
    SSi = (sin / scale).T
    rope = np.concatenate(
        [np.tile(a, (2, 1))[None] for a in (CS, SS, CSi, SSi)], 0)
    rope = rope.transpose(1, 0, 2).reshape(128, 4 * TT_).astype(BF)
    qq = np.arange(128)
    diff = qq[None, :] - qq[:, None]
    dec = np.zeros((128, 16 * 256), np.float64)
    g = host_gamma_scalars()
    for h in range(16):
        d = g[h] ** diff
        dec[:, h * 256:h * 256 + 128] = np.where(diff >= 0, d, 0.0)
        dec[:, h * 256 + 128:h * 256 + 256] = d
    cols = np.zeros((128, 16), np.float32)
    cols[:, 0:8] = pack_cols(ins["L3_attn_gn_weight"].astype(np.float32))
    cols[:, 8:16] = pack_cols(ins["L3_attn_gn_bias"].astype(np.float32))
    return {
        "L3a_wqkvp": pack_lhsT(np.ascontiguousarray(
            ins["L3_attn_w_qkv"].astype(np.float32).T)),
        "L3a_wgp": pack_lhsT(np.ascontiguousarray(
            ins["L3_attn_w_gated"].astype(np.float32).T)),
        "L3a_wprp": pack_lhsT(np.ascontiguousarray(
            ins["L3_attn_w_proj"].astype(np.float32).T)),
        "L3a_rope": np.ascontiguousarray(rope),
        "L3a_dec": dec.astype(BF),
        "L3a_cols": cols,
    }


def build_l3_attn_program(stage=99):
    pr = Prog()
    pr.load_consts()
    x_in = pr.din("x_cur", [D, T])
    x_out = pr.dout("x_next", [D, T])
    st = emit_l3_attn(pr, x_in, x_out, stage=stage)
    pr.out_tokens = st
    pr.finish()
    return pr


def run_sublayer_l3attn(x_full, ins, stage=99):
    pr = build_l3_attn_program(stage)
    packed = host_pack_l3_attn(ins)
    consts = host_consts()
    in_maps = []
    for core in range(8):
        m = dict(consts)
        m.update(packed)
        m["x_cur"] = np.ascontiguousarray(x_full[core].T)
        in_maps.append(m)
    res = run_bass_kernel_spmd(pr.nc, in_maps, core_ids=list(range(8)))
    out = np.stack([res.results[c]["x_next"].T for c in range(8)])
    pr.close()
    return out




def emit_l2_attn(pr, x_in, x_out, lname="L2a"):
    B, p = pr.B, pr.p
    win = pr.din(lname + "_winp", [128, 32 * 8 * 128])
    wx = pr.din(lname + "_wxp", [128, 9 * 16 * 128])
    wdt = pr.din(lname + "_wdtp", [128, 16 * 8 * 128])
    wout = pr.din(lname + "_woutp", [128, 8 * 16 * 128])
    ascan = pr.din(lname + "_ascan", [128, 256])
    reps = pr.din(lname + "_reps", [128, 4096], BF16)
    colsd = pr.din(lname + "_cols", [128, 96])
    selbc = pr.din(lname + "_selbc", [32, 256], BF16)

    if not hasattr(pr, "_l2a_small"):
        st = pr.stack.enter_context
        pr._l2a_small = (
            st(pr.nc.sbuf_tensor(lname + "_asc_sb", [128, 256], F32)),
            st(pr.nc.sbuf_tensor(lname + "_selbc_sb", [32, 256], BF16)),
        )
    asc_sb, selbc_sb = pr._l2a_small

    zdram = pr.dscratch(lname + "_z", [2048, T], BF16)
    ydram = pr.dscratch(lname + "_y", [2048, T], BF16)

    t_cols = p.dma("sync", B.cols.ap()[:, 0:96], colsd[:, :], deps=pr.const_toks,
                   sem="col")
    t_asc = p.dma("sync", asc_sb.ap()[:, :], ascan[:, :], sem="l2asc")
    t_sbc = p.dma("sync", selbc_sb.ap()[:, :], selbc[:, :], sem="l2sbc")

    t_ln, _ = emit_ln(pr, x_in, xn_slot=0, xb_slot=8, x2_slot=16)
    xn = [B.slot(c) for c in range(8)]

    t_a1 = p.do("scalar", lambda s: s.activation(
        out=asc_sb.ap()[:, :], in_=asc_sb.ap()[:, :], func=AF.Exp),
        deps=[t_asc])
    t_a2 = p.do("vector", lambda v: v.tensor_scalar(
        out=asc_sb.ap()[:, :], in0=asc_sb.ap()[:, :], scalar1=-1.0,
        scalar2=None, op0=ALU.mult), deps=[t_a1])

    def rep8(gg):
        return B.slot(6, 2)[:, gg * 128:(gg + 1) * 128]

    def selp(gg):
        return B.slot(6, 2)[:, 2048 + gg * 128:2048 + (gg + 1) * 128]

    ws_in = WStream(pr, win, 8)
    upre_toks = {}
    z_w_toks = []
    stg_busy = {5: None, 5.5: None}

    def evac_in(mb, n0, ps, dep):
        if mb < 16:
            tok = p.do("scalar", lambda s, ps=ps, mb=mb, n0=n0: s.activation(
                out=B.slot(8 + mb)[:, n0:n0 + 1024], in_=ps[:, 0:1024],
                func=AF.Copy), deps=[dep])
            upre_toks.setdefault(mb, []).append(tok)
            return tok
        key = 5 if (n0 == 0) else 5.5
        tok = p.do("scalar", lambda s, ps=ps, n0=n0: s.activation(
            out=B.slot(5)[:, n0:n0 + 1024], in_=ps[:, 0:1024],
            func=AF.Copy), deps=[dep, stg_busy[key]])
        if n0 == 1024:
            tok = p.dma("sync", zdram[(mb - 16) * 128:(mb - 15) * 128, :],
                        B.slot(5), deps=[tok], sem="l2zw")
            z_w_toks.append(tok)
            stg_busy[5] = tok
            stg_busy[5.5] = tok
        return tok

    ev_in = mm_evac_stream(pr, Mb=32, Kc=8, ws=ws_in, rhs=lambda kc, a, b_:
                           xn[kc][:, a:b_], evac=evac_in, deps_rhs=t_ln,
                           region_init=t_ln)

    conv_toks = {}
    prev_cv = None
    for c in range(16):
        usl = B.slot(24 + c)
        psl = B.slot(8 + c)
        wc = [B.cols.ap()[:, j * 16 + c:j * 16 + c + 1] for j in range(4)]
        bcc = B.cols.ap()[:, 64 + c:65 + c]
        dd = upre_toks[c] + [t_cols] + ([prev_cv] if prev_cv else [])
        acc = B.stg.ap().bitcast(F32)[:, 0:1024]
        half_prev = None
        for half in range(2):
            h0 = half * 1024
            lo = 3 if half == 0 else 0
            width = 1024 - lo
            t1 = p.do("vector", lambda v, psl=psl, wc=wc, h0=h0, lo=lo,
                      width=width, acc=acc: v.tensor_scalar(
                out=acc[:, lo:lo + width], in0=psl[:, h0 + lo - 3:h0 + lo - 3 + width],
                scalar1=wc[0], scalar2=None, op0=ALU.mult),
                deps=dd + ([half_prev] if half_prev else []))
            for j in range(1, 4):
                t1 = p.do("vector", lambda v, psl=psl, wc=wc, h0=h0, lo=lo,
                          width=width, j=j, acc=acc: v.scalar_tensor_tensor(
                    out=acc[:, lo:lo + width],
                    in0=psl[:, h0 + lo - 3 + j:h0 + lo - 3 + j + width],
                    scalar=wc[j], in1=acc[:, lo:lo + width],
                    op0=ALU.mult, op1=ALU.add), deps=[t1])
            if half == 0:
                for tcol in range(3):
                    te = p.do("vector", lambda v, psl=psl, wc=wc, tcol=tcol,
                              acc=acc: v.tensor_scalar(
                        out=acc[:, tcol:tcol + 1], in0=psl[:, 0:1],
                        scalar1=wc[3 - tcol], scalar2=None, op0=ALU.mult),
                        deps=[t1])
                    for j in range(4 - tcol, 4):
                        te = p.do("vector", lambda v, psl=psl, wc=wc, tcol=tcol,
                                  j=j, acc=acc: v.scalar_tensor_tensor(
                            out=acc[:, tcol:tcol + 1],
                            in0=psl[:, tcol - 3 + j:tcol - 2 + j],
                            scalar=wc[j], in1=acc[:, tcol:tcol + 1],
                            op0=ALU.mult, op1=ALU.add), deps=[te])
                    t1 = te
            tok = p.do("scalar", lambda s, usl=usl, h0=h0, bcc=bcc, acc=acc:
                       s.activation(
                out=usl[:, h0:h0 + 1024], in_=acc, func=AF.Silu, bias=bcc),
                deps=[t1])
            conv_toks.setdefault(c, []).append(tok)
            prev_cv = tok
            half_prev = tok

    ws_x = WStream(pr, wx, 16)
    xdbl_toks = {}

    def evac_x(mb, n0, ps, dep):
        tok = p.do("scalar", lambda s, ps=ps, mb=mb, n0=n0: s.activation(
            out=B.slot(mb)[:, n0:n0 + 1024], in_=ps[:, 0:1024],
            func=AF.Copy), deps=[dep, t_ln[mb] if mb < 8 else t_ln[-1]])
        xdbl_toks.setdefault(mb, []).append(tok)
        return tok

    def mm_deps_x(mb, kc, n0):
        return conv_toks[kc]

    ev_x = mm_evac_stream(pr, Mb=9, Kc=16, ws=ws_x, rhs=lambda kc, a, b_:
                          B.slot(24 + kc)[:, a:b_], evac=evac_x,
                          extra_mm_deps=mm_deps_x, region_init=[ev_in[-1]])

    ws_dt = WStream(pr, wdt, 8)
    d_toks = {}
    sfp_prev = [None]

    def dslot(mb):
        return B.slot(9 + mb) if mb < 15 else B.slot(0)

    def evac_dt(mb, n0, ps, dep):
        sf = B.stg.ap().bitcast(F32)[:, 0:1024]
        tf = B.tmpf.ap()[:, 0:1024]
        dd = [dep, t_cols, sfp_prev[0]]
        t0 = p.do("vector", lambda v, ps=ps, mb=mb, sf=sf: v.tensor_scalar(
            out=sf, in0=ps[:, 0:1024],
            scalar1=B.cols.ap()[:, 80 + mb:81 + mb], scalar2=None, op0=ALU.add),
            deps=dd)
        ta = p.do("scalar", lambda s, sf=sf, tf=tf: s.activation(
            out=tf, in_=sf, func=AF.Abs), deps=[t0])
        te = p.do("scalar", lambda s, tf=tf: s.activation(
            out=tf, in_=tf, func=AF.Exp, scale=-1.0), deps=[ta])
        tl = p.do("scalar", lambda s, tf=tf: s.activation(
            out=tf, in_=tf, func=AF.Ln, bias=B.cp(1.0)), deps=[te])
        tr = p.do("vector", lambda v, sf=sf: v.tensor_scalar(
            out=sf, in0=sf, scalar1=0.0, scalar2=None, op0=ALU.max), deps=[tl])
        tok = p.do("vector", lambda v, mb=mb, n0=n0, sf=sf, tf=tf: v.tensor_add(
            out=dslot(mb)[:, n0:n0 + 1024], in0=sf, in1=tf), deps=[tr])
        d_toks.setdefault(mb, []).append(tok)
        sfp_prev[0] = tok
        return tok

    def mm_deps_dt(mb, kc, n0):
        return xdbl_toks[kc]

    ev_dt = mm_evac_stream(pr, Mb=16, Kc=8, ws=ws_dt, rhs=lambda kc, a, b_:
                           B.slot(kc)[:, a:b_], evac=evac_dt,
                           extra_mm_deps=mm_deps_dt, region_init=[ev_x[-1]])


    bc_toks = []
    prev_bc = ev_dt[-1]
    for what in range(2):
        for nt in range(4):
            tt = p.do("tensor", lambda t, what=what, nt=nt: t.matmul(
                B.ps6[0:128, 0:512],
                selbc_sb.ap()[:, what * 128:what * 128 + 128],
                B.slot(8)[0:32, nt * 512:(nt + 1) * 512],
                start=True, stop=True),
                deps=[t_sbc, xdbl_toks[8][nt // 2], prev_bc, ev_dt[-1]])
            tok = p.do("vector", lambda v, what=what, nt=nt: v.tensor_copy(
                out=B.slot(2 + what)[:, nt * 512:(nt + 1) * 512],
                in_=B.ps6[0:128, 0:512]), deps=[tt])
            prev_bc = tok
            bc_toks.append(tok)

    t_rep = p.dma("gpsimd", B.slot(6, 2), reps[:, :], sem="l2rep",
                  deps=[ev_dt[-1]])

    yq_banks = [B.ps4.ap()[:, 0:512], B.ps5.ap()[:, 0:512],
                B.ps6.ap()[:, 0:512], B.psA[0:128, 1024:1536]]

    def scr(par):
        slot_i = 4 if par == 0 else 8
        base = slot_i * T
        return (B.arena.ap()[:, base:base + 512],
                B.arena.ap()[:, base + 512:base + 1024],
                B.arena.ap()[:, base + 1024:base + 1536],
                B.arena.ap()[:, base + 1536:base + 2048],
                B.arena.ap()[:, base + 1535:base + 1536])

    y_ev_toks = []
    tW = None
    m2_last = None
    rb_busy = [ev_dt[-1], ev_dt[-1]]
    yb_busy = [None] * 4
    scr_busy = [{"dA": None, "dB": None, "X": None, "Xc": None},
                {"dA": bc_toks[-1], "dB": bc_toks[-1], "X": bc_toks[-1],
                 "Xc": bc_toks[-1]}]
    prev_yev = None
    quarters = [(c, gg, q) for c in range(16) for gg in range(16)
                for q in range(4)]
    tW_by_c = {}
    states = {}

    def q_stageA(i):
        c, gg, q = quarters[i]
        par = i % 2
        dAq, dBq, Xq, Xcq, scol = scr(par)
        if c not in tW_by_c:
            prevW = tW_by_c.get(c - 1)
            tok = p.do("vector", lambda v, c=c: v.tensor_mul(
                out=B.slot(1), in0=dslot(c), in1=B.slot(24 + c)),
                deps=[d_toks[c][-1], conv_toks[c][-1]] +
                ([prevW] if prevW else []) +
                ([m2_by_c[c - 1]] if (c - 1) in m2_by_c else []))
            tW_by_c[c] = tok
        sl = slice(q * 512, (q + 1) * 512)
        m1 = p.do("tensor", lambda t, c=c, gg=gg, sl=sl: t.matmul(
            B.psA[0:128, 0:512], rep8(gg), dslot(c)[:, sl],
            start=True, stop=True),
            deps=[rb_busy[0], t_rep, d_toks[c][-1]])
        m2 = p.do("tensor", lambda t, c=c, gg=gg, sl=sl: t.matmul(
            B.psA[0:128, 512:1024], rep8(gg), B.slot(1)[:, sl],
            start=True, stop=True), deps=[rb_busy[1], tW_by_c[c]])
        m2_by_c[c] = m2
        g = c * 16 + gg
        tdA = p.do("scalar", lambda s, g=g, dAq=dAq: s.activation(
            out=dAq, in_=B.psA[0:128, 0:512], func=AF.Exp,
            scale=asc_sb.ap()[:, g:g + 1]),
            deps=[m1, t_a2, scr_busy[par]["dA"]])
        rb_busy[0] = tdA
        tdB = p.do("vector", lambda v, q=q, dBq=dBq: v.tensor_tensor(
            out=dBq, in0=B.psA[0:128, 512:1024],
            in1=B.slot(2)[:, q * 512:(q + 1) * 512], op=ALU.mult),
            deps=[m2, bc_toks[3], scr_busy[par]["dB"]])
        rb_busy[1] = tdB
        states[i] = dict(par=par, c=c, gg=gg, q=q, tdA=tdA, tdB=tdB)

    def q_stageB(i):
        st_ = states[i]
        par, c, gg, q = st_["par"], st_["c"], st_["gg"], st_["q"]
        dAq, dBq, Xq, Xcq, scol = scr(par)
        prev_scol = scr(1 - par)[4]
        tX = p.do("vector", lambda v, q=q, Xq=Xq, dAq=dAq, dBq=dBq,
                  prev_scol=prev_scol: v.tensor_tensor_scan(
            out=Xq, data0=dAq, data1=dBq,
            initial=(0.0 if q == 0 else prev_scol),
            op0=ALU.mult, op1=ALU.add),
            deps=[st_["tdA"], st_["tdB"], scr_busy[par]["X"],
                  scr_busy[par]["Xc"],
                  states[i - 1]["tX"] if q != 0 and (i - 1) in states else None])
        st_["tX"] = tX
        scr_busy[par]["dA"] = tX
        scr_busy[par]["dB"] = tX
        tXc = p.do("gpsimd", lambda gp, q=q, Xq=Xq, Xcq=Xcq: gp.tensor_tensor(
            out=Xcq, in0=Xq, in1=B.slot(3)[:, q * 512:(q + 1) * 512],
            op=ALU.mult), deps=[tX, bc_toks[7], scr_busy[par]["Xc"]])
        scr_busy[par]["X"] = tXc
        st_["tXc"] = tXc

    def q_stageC(i):
        nonlocal prev_yev
        st_ = states[i]
        par, c, gg, q = st_["par"], st_["c"], st_["gg"], st_["q"]
        Xcq = scr(par)[3]
        my = p.do("tensor", lambda t, gg=gg, q=q, Xcq=Xcq: t.matmul(
            yq_banks[q], selp(gg), Xcq,
            start=(gg == 0), stop=(gg == 15)),
            deps=[st_["tXc"], yb_busy[q] if gg == 0 else None, t_rep])
        scr_busy[par]["Xc"] = my
        if gg == 15:
            tok = p.do("scalar", lambda s, q=q, c=c: s.activation(
                out=B.slot(5)[:, q * 512:(q + 1) * 512],
                in_=yq_banks[q], func=AF.Copy),
                deps=[my, prev_yev])
            yb_busy[q] = tok
            prev_yev = tok
            if q == 3:
                tok = p.dma("sync", ydram[c * 128:(c + 1) * 128, :], B.slot(5),
                            deps=[prev_yev], sem="l2yw")
                prev_yev = tok
                y_ev_toks.append(tok)
        del states[i]

    m2_by_c = {}
    NQ = len(quarters)
    for i in range(NQ + 2):
        if i < NQ:
            q_stageA(i)
        if 1 <= i <= NQ:
            q_stageB(i - 1)
        if i >= 2:
            q_stageC(i - 2)

    g_toks = {}
    prev_g = y_ev_toks[-1]
    for c in range(16):
        gsl = dslot(c)
        t_zl = p.dma("sync", B.slot(4), zdram[c * 128:(c + 1) * 128, :],
                     deps=[prev_g, z_w_toks[c]], sem="l2zr")
        t_yl = p.dma("scalar", B.slot(5), ydram[c * 128:(c + 1) * 128, :],
                     deps=[prev_g, y_ev_toks[c]], sem="l2yr")
        t_sz = p.do("scalar", lambda s: s.activation(
            out=B.slot(4), in_=B.slot(4), func=AF.Silu), deps=[t_zl])
        t_ad = p.do("vector", lambda v, c=c: v.tensor_add(
            out=B.slot(5), in0=B.slot(5), in1=B.slot(24 + c)),
            deps=[t_yl, conv_toks[c][-1]])
        tok = p.do("vector", lambda v, gsl=gsl: v.tensor_mul(
            out=gsl, in0=B.slot(5), in1=B.slot(4)), deps=[t_sz, t_ad])
        g_toks[c] = tok
        prev_g = tok

    out_sl = [B.slot(24 + c) for c in range(8)]
    ws_o = WStream(pr, wout, 16)
    dtoks = {}

    def evac_o(mb, n0, ps, dep):
        tok = p.do("scalar", lambda s, ps=ps, mb=mb, n0=n0: s.activation(
            out=out_sl[mb][:, n0:n0 + 1024], in_=ps[:, 0:1024],
            func=AF.Copy), deps=[dep, g_toks[mb] if mb < 16 else None])
        dtoks.setdefault(mb, []).append(tok)
        return tok

    def mm_deps_o(mb, kc, n0):
        return [g_toks[kc]]

    ev_o = mm_evac_stream(pr, Mb=8, Kc=16, ws=ws_o, rhs=lambda kc, a, b_:
                          dslot(kc)[:, a:b_], evac=evac_o,
                          extra_mm_deps=mm_deps_o,
                          region_init=[prev_g, scr_busy[0]["Xc"],
                                       scr_busy[1]["Xc"]])

    def delta_src(c):
        return (out_sl[c], dtoks[c])

    st = emit_residual_add(pr, x_in, x_out, delta_src, 1.0, lambda c: [],
                           first_deps=[ev_o[-1]])
    return st


def host_pack_l2_attn(ins):
    DI = 2048
    wx = pad_to(ins["L2_attn_W_x"].astype(np.float32), DI, 1152)
    A_log = ins["L2_attn_A_log"].astype(np.float32)
    A_scan = np.ascontiguousarray(
        A_log.reshape(256, 8, 16).transpose(2, 1, 0).reshape(128, 256))
    rep = np.zeros((128, 2048), np.float32)
    selp = np.zeros((128, 2048), np.float32)
    for gg in range(16):
        for pp in range(128):
            rep[gg * 8 + (pp % 8), gg * 128 + pp] = 1.0
            selp[pp, gg * 128 + gg * 8 + (pp % 8)] = 1.0
    reps = np.concatenate([rep, selp], 1).astype(BF)
    selbc = np.zeros((32, 256), np.float32)
    for pp in range(128):
        selbc[pp // 8, pp] = 1.0
        selbc[16 + pp // 8, 128 + pp] = 1.0
    cols = np.zeros((128, 96), np.float32)
    convw = ins["L2_attn_conv_w"].astype(np.float32)[:, 0, :]
    for j in range(4):
        cols[:, j * 16:(j + 1) * 16] = np.ascontiguousarray(
            convw[j].reshape(16, 128).T)
    cols[:, 64:80] = pack_cols(ins["L2_attn_conv_b"].astype(np.float32))
    cols[:, 80:96] = pack_cols(ins["L2_attn_b_dt"].astype(np.float32))
    return {
        "L2a_winp": pack_lhsT(ins["L2_attn_W_in"].astype(np.float32)),
        "L2a_wxp": pack_lhsT(wx),
        "L2a_wdtp": pack_lhsT(ins["L2_attn_W_dt"].astype(np.float32)),
        "L2a_woutp": pack_lhsT(ins["L2_attn_W_out"].astype(np.float32)),
        "L2a_ascan": A_scan,
        "L2a_reps": reps,
        "L2a_cols": cols,
        "L2a_selbc": selbc.astype(BF),
    }


def build_l2_attn_program():
    pr = Prog()
    pr.load_consts()
    x_in = pr.din("x_cur", [D, T])
    x_out = pr.dout("x_next", [D, T])
    st = emit_l2_attn(pr, x_in, x_out)
    pr.out_tokens = st
    pr.finish()
    return pr


def run_sublayer_l2attn(x_full, ins):
    pr = build_l2_attn_program()
    packed = host_pack_l2_attn(ins)
    consts = host_consts()
    in_maps = []
    for core in range(8):
        m = dict(consts)
        m.update(packed)
        m["x_cur"] = np.ascontiguousarray(x_full[core].T)
        in_maps.append(m)
    res = run_bass_kernel_spmd(pr.nc, in_maps, core_ids=list(range(8)))
    out = np.stack([res.results[c]["x_next"].T for c in range(8)])
    pr.close()
    return out




def emit_l2_ffn(pr, x_in, x_out, lname="L2f", n_sub=6, macaron_scale=0.5,
                halves=2):
    B, p = pr.B, pr.p
    NSUB = n_sub
    wqkv = pr.din(lname + "_wqkvp", [128, 6 * 24 * 8 * 128])
    wo = pr.din(lname + "_wop", [128, 6 * 8 * 8 * 128])
    w1 = pr.din(lname + "_w1p", [128, 6 * 44 * 8 * 128])
    w2 = pr.din(lname + "_w2p", [128, 6 * 8 * 22 * 128])
    ropes = pr.din(lname + "_rope", [128, 4 * T], BF16)
    colsd = pr.din(lname + "_cols", [128, 6 * 112])
    lms = pr.din(lname + "_lm", [128, 2 * 512 + 2 * 256], BF16)

    if not hasattr(pr, "_l2f_small"):
        st = pr.stack.enter_context
        pr._l2f_small = (
            st(pr.nc.sbuf_tensor(lname + "_lm_sb", [128, 1536], BF16)),
            st(pr.nc.sbuf_tensor(lname + "_cols2", [128, 112], F32)),
        )
    lm_sb, cols2 = pr._l2f_small
    t_lm = p.dma("sync", lm_sb.ap()[:, :], lms[:, :], sem="l2flm")

    x2dram = pr.dscratch(lname + "_x2", [D, T], F32)
    ydram2 = pr.dscratch(lname + "_yat", [D, T], BF16)

    CT = list(pr.const_toks)
    x2 = [B.slot_f32(2 * c, 1) for c in range(8)]

    t_ln, _ = emit_ln(pr, x_in, xn_slot=16, xb_slot=24, x2_slot=32)
    x2_tok = {}
    for c in range(8):
        tok = p.do("vector", lambda v, c=c: v.tensor_copy(
            out=x2[c], in_=B.slot(16 + c)), deps=[t_ln[c]])
        x2_tok[c] = [tok]


    def ccol(i, j):
        return cols2.ap()[:, j:j + 1]

    prev_cols = None
    psA_busy = [t_ln[-1], t_ln[-1]]

    rope_war = [t_ln[-1]]
    for i in range(NSUB):
        t_cols = p.dma("sync", cols2.ap()[:, :], colsd[:, i * 112:(i + 1) * 112],
                       deps=[prev_cols] if prev_cols else CT, sem="col")
        t_rope = p.dma("gpsimd", B.slot(32, 4), ropes[:, :], sem="l2frope",
                       deps=rope_war)

        ax_toks = {}
        for c in range(8):
            if c < 4:
                tok = p.do("vector", lambda v, c=c: v.tensor_copy(
                    out=B.slot(16 + c), in_=x2[c]), deps=x2_tok[c])
            else:
                t0 = p.do("vector", lambda v, c=c: v.memset(
                    B.slot(16 + c)[:, 0:1], 0.0), deps=x2_tok[c])
                tok = p.do("vector", lambda v, c=c: v.tensor_copy(
                    out=B.slot(16 + c)[:, 1:T], in_=x2[c][:, 0:T - 1]),
                    deps=[t0])
            ax_toks[c] = tok
        ax = [B.slot(16 + c) for c in range(8)]
        psT_busy = [None, None]

        y_pair_toks = []
        pt_busy = [None, None]
        av_busy = [None, None]
        vt_busy = [None, None]
        qkv_busy = [None, None]
        it = 0
        for pj in range(8):
            pb = pj % 2
            qsl, ksl, vsl = (B.slot(24 + 3 * pb), B.slot(25 + 3 * pb),
                             B.slot(26 + 3 * pb))
            ws_dep = [t for t in [qkv_busy[pb]] if t]
            ld = p.dma("gpsimd", B.wb[pb].ap()[:, 0:3 * 8 * 128],
                       wqkv[:, (i * 24 + 3 * pj) * 8 * 128:
                            (i * 24 + 3 * pj + 3) * 8 * 128],
                       deps=[pr.wb_last_use[pb]] if pr.wb_last_use[pb] else [],
                       sem=f"wb{pb}")
            pair_toks = []
            last_mm = None
            for mi, dst in enumerate((qsl, ksl, vsl)):
                for n0 in (0, 1024):
                    last_mm = None
                    for kc in range(8):
                        dd = ([ld, psA_busy[0], psA_busy[1]] + ws_dep +
                              [ax_toks[kc]]) if kc == 0 else [ax_toks[kc]]
                        for qq in range(2):
                            a = n0 + qq * 512
                            last_mm = p.do("tensor", lambda t, mi=mi, kc=kc,
                                           a=a, n0=n0, pb=pb: t.matmul(
                                B.psA[0:128, (a - n0):(a - n0) + 512],
                                B.wb[pb].ap()[:, (mi * 8 + kc) * 128:
                                              (mi * 8 + kc + 1) * 128],
                                ax[kc][:, a:a + 512],
                                start=(kc == 0), stop=(kc == 7)), deps=dd)
                    tok = p.do("scalar", lambda s, dst=dst, n0=n0: s.activation(
                        out=dst[:, n0:n0 + 1024], in_=B.psA[0:128, 0:1024],
                        func=AF.Copy), deps=[last_mm])
                    psA_busy[0] = tok
                    psA_busy[1] = tok
                    pair_toks.append(tok)
            pr.wb_last_use[pb] = last_mm
            qkv_done = pair_toks[-1]

            rope_done = []
            sq_prev = None
            for what in range(2):
                sl = (qsl, ksl)[what]
                t_sq = p.do("scalar", lambda s, sl=sl: s.activation(
                    out=B.slot(39), in_=sl, func=AF.Square),
                    deps=[qkv_done, vt_busy[pb] or qkv_done,
                          sq_prev])
                mm = None
                for hh in range(2):
                    r0 = hh * 64
                    ro = hh * 32
                    for nt in range(4):
                        mm = p.do("tensor", lambda t, r0=r0, ro=ro, nt=nt:
                                  t.matmul(
                            B.psA[ro:ro + 1, nt * 512:(nt + 1) * 512],
                            B.onescol_bf.ap()[r0:r0 + 64, :],
                            B.slot(39)[r0:r0 + 64, nt * 512:(nt + 1) * 512],
                            start=True, stop=True),
                            deps=[t_sq, psA_busy[0], psA_busy[1]])
                sq_prev = mm
                t_ns = p.do("vector", lambda v: v.tensor_scalar(
                    out=B.tmpf.ap()[0:64, 0:T], in0=B.psA[0:64, 0:T],
                    scalar1=1e-24, scalar2=None, op0=ALU.max), deps=[mm])
                psA_busy[0] = t_ns
                psA_busy[1] = t_ns
                t_l = p.do("scalar", lambda s: s.activation(
                    out=B.tmpf.ap()[0:64, 0:T], in_=B.tmpf.ap()[0:64, 0:T],
                    func=AF.Ln), deps=[t_ns])
                t_r = p.do("scalar", lambda s: s.activation(
                    out=B.tmpf.ap()[0:64, 0:T], in_=B.tmpf.ap()[0:64, 0:T],
                    func=AF.Exp, scale=-0.5), deps=[t_l])
                bc_mm = []
                for nt in range(4):
                    bc_mm.append(p.do("tensor", lambda t, nt=nt: t.matmul(
                        B.psA[0:128, nt * 512:(nt + 1) * 512],
                        B.selrow.ap()[0:64, :],
                        B.tmpf.ap()[0:64, nt * 512:(nt + 1) * 512],
                        start=True, stop=True), deps=[t_r] + CT))
                t_ap = p.do("vector", lambda v, sl=sl, i=i: v.scalar_tensor_tensor(
                    out=sl, in0=sl, scalar=ccol(i, 0), in1=B.psA[0:128, 0:T],
                    op0=ALU.mult, op1=ALU.mult), deps=bc_mm + [t_cols])
                psA_busy[0] = t_ap
                psA_busy[1] = t_ap
                cs = B.slot(32 + 2 * what)
                ss = B.slot(33 + 2 * what)
                rp = t_ap
                for half in range(2):
                    h0 = half * 1024
                    m1 = p.do("tensor", lambda t, sl=sl, h0=h0: t.matmul(
                        B.ps6[0:128, 0:512], B.rot128.ap()[:, :],
                        sl[:, h0:h0 + 512], start=True, stop=True),
                        deps=[rp] + CT)
                    m2 = p.do("tensor", lambda t, sl=sl, h0=h0: t.matmul(
                        B.ps5[0:128, 0:512], B.rot128.ap()[:, :],
                        sl[:, h0 + 512:h0 + 1024], start=True, stop=True),
                        deps=[rp, av_busy[1], av_busy[0]] + CT)
                    t1 = p.do("vector", lambda v, sl=sl, h0=h0, cs=cs:
                              v.tensor_mul(
                        out=B.stg.ap()[:, 0:1024], in0=sl[:, h0:h0 + 1024],
                        in1=cs[:, h0:h0 + 1024]), deps=[rp, t_rope, m2])
                    t2 = p.do("vector", lambda v, sl=sl, h0=h0, ss=ss:
                              v.tensor_mul(
                        out=sl[:, h0:h0 + 512], in0=B.ps6[0:128, 0:512],
                        in1=ss[:, h0:h0 + 512]), deps=[m1, t1])
                    t2b = p.do("vector", lambda v, sl=sl, h0=h0, ss=ss:
                               v.tensor_mul(
                        out=sl[:, h0 + 512:h0 + 1024], in0=B.ps5[0:128, 0:512],
                        in1=ss[:, h0 + 512:h0 + 1024]), deps=[m2, t2])
                    t3 = p.do("vector", lambda v, sl=sl, h0=h0: v.tensor_add(
                        out=sl[:, h0:h0 + 1024], in0=sl[:, h0:h0 + 1024],
                        in1=B.stg.ap()[:, 0:1024]), deps=[t2b])
                    rp = t3
                rope_done.append(rp)

            def vtm2(pb, hh, tkc):
                off = (30 + pb) * T + tkc * 128 + hh * 64
                return B.arena.ap()[:, off:off + 64]

            vt_prev = psT_busy[0] if isinstance(psT_busy, list) else psT_busy
            vts = []
            for hh in range(2):
                for tkc in range(16):
                    dd = [qkv_done] + ([vt_prev] if vt_prev else []) +                          ([vt_busy[pb]] if vt_busy[pb] and tkc == 0 else [])
                    tt = p.do("tensor", lambda t, hh=hh, tkc=tkc, vsl=vsl:
                              t.transpose(
                        B.psT[0:128, (tkc % 8) * 128:(tkc % 8) * 128 + 64],
                        vsl[hh * 64:hh * 64 + 64, tkc * 128:(tkc + 1) * 128],
                        B.ident64.ap()[hh * 64:hh * 64 + 64, :]),
                        deps=[d for d in dd if d])
                    if tkc % 8 == 7:
                        toks = []
                        for j in range(8):
                            toks.append(p.do("vector", lambda v, hh=hh, j=j,
                                             tkc=tkc, pb=pb: v.tensor_copy(
                                out=vtm2(pb, hh, tkc - 7 + j),
                                in_=B.psT[0:128, j * 128:j * 128 + 64]),
                                deps=[tt]))
                        vt_prev = toks[-1]
                        vts.append(toks[-1])
            psT_busy = [vt_prev, vt_prev]
            vt_busy[pb] = vt_prev

            units = [(hh, w, r) for hh in range(2) for w in range(8)
                     for r in range(2)]

            def stage_A(idx):
                hh, w, r = units[idx]
                sb = (it + idx) % 2
                sA = sb * 512
                c0_ = 240 + sb * 3
                r0 = hh * 64
                tq0 = w * 256 + r * 128
                if w == 0:
                    k0 = 0
                    Wk = 256 if r == 1 else 128
                    moff = 1024 + r * 256
                else:
                    k0 = (w - 1) * 256 + (0 if r == 0 else 128)
                    Wk = 384
                    moff = r * 512 + r * 128
                dd = rope_done + [psA_busy[sb]]
                last = p.do("tensor", lambda t, r0=r0, tq0=tq0, k0=k0, Wk=Wk,
                            sA=sA, qsl=qsl, ksl=ksl: t.matmul(
                    B.psA[0:128, sA:sA + Wk],
                    qsl[r0:r0 + 64, tq0:tq0 + 128],
                    ksl[r0:r0 + 64, k0:k0 + Wk],
                    start=True, stop=True), deps=dd)
                t_mk = p.do("vector", lambda v, Wk=Wk, moff=moff, sA=sA:
                            v.tensor_tensor(
                    out=B.psA[0:128, sA:sA + Wk],
                    in0=B.psA[0:128, sA:sA + Wk],
                    in1=lm_sb.ap()[:, moff:moff + Wk], op=ALU.add),
                    deps=[last, t_lm])
                t_mx = p.do("vector", lambda v, Wk=Wk, sA=sA, c0_=c0_:
                            v.tensor_reduce(
                    out=B.cols.ap()[:, c0_:c0_ + 1].rearrange(
                        "p one -> p one ()"),
                    in_=B.psA[0:128, sA:sA + Wk].rearrange("p w -> p () w"),
                    axis=mybir.AxisListType.X, op=ALU.max, negate=True),
                    deps=[t_mk])
                return dict(sb=sb, sA=sA, c0_=c0_, hh=hh, w=w, r=r, r0=r0,
                            tq0=tq0, k0=k0, Wk=Wk, t_mx=t_mx)

            def stage_B(st_):
                sb, sA, c0_, Wk = st_["sb"], st_["sA"], st_["c0_"], st_["Wk"]
                sG = sb * 1024
                t_ex = p.do("scalar", lambda s, Wk=Wk, sA=sA, sG=sG, c0_=c0_:
                            s.activation(
                    out=B.stg.ap()[:, sG:sG + Wk],
                    in_=B.psA[0:128, sA:sA + Wk],
                    func=AF.Exp, bias=B.cols.ap()[:, c0_:c0_ + 1],
                    accum_out=B.cols.ap()[:, c0_ + 1:c0_ + 2]),
                    deps=[st_["t_mx"], pt_busy[sb]])
                psA_busy[sb] = t_ex
                t_rd = p.do("vector", lambda v, c0_=c0_: v.reciprocal(
                    out=B.cols.ap()[:, c0_ + 2:c0_ + 3],
                    in_=B.cols.ap()[:, c0_ + 1:c0_ + 2]), deps=[t_ex])
                t_pn = p.do("vector", lambda v, Wk=Wk, sG=sG, c0_=c0_:
                            v.tensor_scalar(
                    out=B.stg.ap()[:, sG:sG + Wk],
                    in0=B.stg.ap()[:, sG:sG + Wk],
                    scalar1=B.cols.ap()[:, c0_ + 2:c0_ + 3],
                    scalar2=None, op0=ALU.mult), deps=[t_rd])
                st_["t_pn"] = t_pn
                st_["sG"] = sG
                return st_

            def stage_C(st_):
                sb, sG, Wk = st_["sb"], st_["sG"], st_["Wk"]
                sP = sb * 512
                nkc = (Wk + 127) // 128
                tts = []
                for j in range(nkc):
                    dd2 = [st_["t_pn"]]
                    if j == 0 and psT_busy[sb] is not None:
                        dd2.append(psT_busy[sb])
                    tts.append(p.do("tensor", lambda t, j=j, sP=sP, sG=sG:
                                    t.transpose(
                        B.psT[0:128, sP + j * 128:sP + (j + 1) * 128],
                        B.stg.ap()[:, sG + j * 128:sG + (j + 1) * 128],
                        B.ident_bf.ap()[:, :]), deps=dd2))
                tev = p.do("scalar", lambda s, nkc=nkc, sb=sb, sP=sP:
                           s.activation(
                    out=B.slot(36 + sb)[:, 0:nkc * 128],
                    in_=B.psT[0:128, sP:sP + nkc * 128], func=AF.Copy),
                    deps=[tts[-1]])
                psT_busy[sb] = tev
                pt_busy[sb] = tev
                st_["tev"] = tev
                st_["nkc"] = nkc
                return st_

            def stage_D(st_, first_unit):
                sb, nkc, hh = st_["sb"], st_["nkc"], st_["hh"]
                k0, tq0 = st_["k0"], st_["tq0"]
                avb = B.ps4.ap() if sb == 0 else B.ps5.ap()
                dd3 = [st_["tev"], vts[-1]]
                if av_busy[sb] is not None:
                    dd3.append(av_busy[sb])
                lastav = None
                for j in range(nkc):
                    tkc = k0 // 128 + j
                    lastav = p.do("tensor", lambda t, hh=hh, j=j, avb=avb,
                                  nkc=nkc, tkc=tkc, pb=pb, sb=sb: t.matmul(
                        avb[0:64, 0:128], vtm2(pb, hh, tkc),
                        B.slot(36 + sb)[:, j * 128:(j + 1) * 128],
                        start=(j == 0), stop=(j == nkc - 1)),
                        deps=dd3 if j == 0 else [])
                t_ye = p.do("scalar", lambda s, hh=hh, tq0=tq0, avb=avb:
                            s.activation(
                    out=B.slot(38)[hh * 64:hh * 64 + 64, tq0:tq0 + 128],
                    in_=avb[0:64, 0:128], func=AF.Copy),
                    deps=[lastav, y_pair_toks[-1] if first_unit else None])
                av_busy[sb] = t_ye

            states = {}
            NU = len(units)
            for idx in range(NU + 2):
                if idx < NU:
                    states[idx] = stage_A(idx)
                if idx >= 1 and idx - 1 < NU:
                    stage_B(states[idx - 1])
                    stage_C(states[idx - 1])
                if idx >= 2:
                    stage_D(states[idx - 2],
                            first_unit=(idx - 2 == 0 and it >= 1))
                    del states[idx - 2]
            it += NU
            tok = p.dma("sync", ydram2[pj * 128:(pj + 1) * 128, :], B.slot(38),
                        deps=[av_busy[0], av_busy[1]], sem="l2fy")
            y_pair_toks.append(tok)
            qkv_busy[pb] = av_busy[0]

        y_ld = []
        for c in range(8):
            y_ld.append(p.dma("scalar", B.slot(16 + c),
                              ydram2[c * 128:(c + 1) * 128, :],
                              deps=[y_pair_toks[c], psA_busy[0], psA_busy[1]],
                              sem=f"lnc{c}"))

        ws_o = WStream(pr, wo[:, i * 8 * 8 * 128:(i + 1) * 8 * 8 * 128], 8)
        ao_toks = {}

        def evac_ao(mb, n0, ps, dep, _x2t=x2_tok):
            tok = p.do("vector", lambda v, ps=ps, mb=mb, n0=n0: v.tensor_add(
                out=x2[mb][:, n0:n0 + 1024], in0=ps[:, 0:1024],
                in1=x2[mb][:, n0:n0 + 1024]), deps=[dep] + _x2t[mb])
            ao_toks.setdefault(mb, []).append(tok)
            return tok

        def mm_deps_ao(mb, kc, n0, _yld=y_ld):
            return [_yld[kc]]

        ev_ao = mm_evac_stream(pr, Mb=8, Kc=8, ws=ws_o,
                               rhs=lambda kc, a, b_: B.slot(16 + kc)[:, a:b_],
                               evac=evac_ao, extra_mm_deps=mm_deps_ao,
                               region_init=[psA_busy[0], psA_busy[1]])
        for c in range(8):
            x2_tok[c] = [ao_toks[c][-1]]

        if halves == 1:
            break
        sh_toks = {}
        for c in range(8):
            if c < 4:
                tok = p.do("vector", lambda v, c=c: v.tensor_copy(
                    out=B.slot(24 + c), in_=x2[c]), deps=x2_tok[c] +
                    [vt_busy[0], vt_busy[1]])
            else:
                t0 = p.do("vector", lambda v, c=c: v.memset(
                    B.slot(24 + c)[:, 0:1], 0.0), deps=x2_tok[c] +
                    [vt_busy[0], vt_busy[1]])
                tok = p.do("vector", lambda v, c=c: v.tensor_copy(
                    out=B.slot(24 + c)[:, 1:T], in_=x2[c][:, 0:T - 1]),
                    deps=[t0])
            sh_toks[c] = tok
        vt_busy = [None, None]

        t_ln3, _ = emit_ln(pr, None, xn_slot=16, xb_slot=None, x2_slot=32,
                           gcol=lambda c: ccol(i, 1 + c),
                           bcol=lambda c: ccol(i, 9 + c),
                           src_sbuf=[B.slot(24 + c) for c in range(8)],
                           src_deps=lambda c: sh_toks[c], cols_dep=[t_cols])
        xn3 = [B.slot(16 + c) for c in range(8)]

        x2_sp = []
        for c in range(8):
            x2_sp.append(p.dma("sync", x2dram[c * 128:(c + 1) * 128, :],
                               x2[c], deps=x2_tok[c] + [t_ln3[c]],
                               sem=f"sp{c}"))

        def hslot(mb):
            return B.slot(24 + mb) if mb < 16 else B.slot(mb - 16)

        ws_1 = WStream(pr, w1[:, i * 44 * 8 * 128:(i + 1) * 44 * 8 * 128], 8)
        h_toks = {}
        stg_tok = [None, None]

        def evac_w1(mb, n0, ps, dep, _i=i):
            hmb = mb // 2
            sbi = (n0 // 1024)
            if mb % 2 == 0:
                tok = p.do("vector", lambda v, ps=ps, hmb=hmb, n0=n0:
                           v.tensor_scalar(
                    out=hslot(hmb)[:, n0:n0 + 1024], in0=ps[:, 0:1024],
                    scalar1=ccol(_i, 17 + hmb), scalar2=None, op0=ALU.add),
                    deps=[dep, t_cols] + ([h_toks[hmb][-1]] if hmb in h_toks
                                          else []) +
                    (x2_sp if hmb >= 16 else []))
                h_toks.setdefault(hmb, []).append(tok)
                return tok
            t_g = p.do("scalar", lambda s, ps=ps, hmb=hmb, n0=n0, sbi=sbi:
                       s.activation(
                out=B.stg.ap()[:, sbi * 1024:sbi * 1024 + 1024],
                in_=ps[:, 0:1024], func=AF.Gelu, bias=ccol(_i, 39 + hmb)),
                deps=[dep, t_cols, stg_tok[sbi]])
            tok = p.do("vector", lambda v, hmb=hmb, n0=n0, sbi=sbi: v.tensor_mul(
                out=hslot(hmb)[:, n0:n0 + 1024],
                in0=hslot(hmb)[:, n0:n0 + 1024],
                in1=B.stg.ap()[:, sbi * 1024:sbi * 1024 + 1024]),
                deps=[t_g, h_toks[hmb][-1]])
            h_toks[hmb].append(tok)
            stg_tok[sbi] = tok
            return tok

        def mm_deps_w1(mb, kc, n0):
            return [t_ln3[kc]]

        ev_w1 = mm_evac_stream(pr, Mb=44, Kc=8, ws=ws_1,
                               rhs=lambda kc, a, b_: xn3[kc][:, a:b_],
                               evac=evac_w1, extra_mm_deps=mm_deps_w1,
                               region_init=[t_ln3[-1], ev_ao[-1]])

        ws_2 = WStream(pr, w2[:, i * 8 * 22 * 128:(i + 1) * 8 * 22 * 128], 22)
        d_ev = {}

        def evac_w2(mb, n0, ps, dep):
            tok = p.do("scalar", lambda s, ps=ps, mb=mb, n0=n0: s.activation(
                out=B.slot(16 + mb)[:, n0:n0 + 1024], in_=ps[:, 0:1024],
                func=AF.Copy), deps=[dep, t_ln3[-1]])
            d_ev.setdefault(mb, []).append(tok)
            return tok

        def mm_deps_w2b(mb, kc, n0):
            return [h_toks[kc][-1]]

        ev_w2 = mm_evac_stream(pr, Mb=8, Kc=22, ws=ws_2,
                               rhs=lambda kc, a, b_: hslot(kc)[:, a:b_],
                               evac=evac_w2, extra_mm_deps=mm_deps_w2b,
                               region_init=[ev_w1[-1]])
        f_toks = {}
        res_ld = [None, None]
        for mb in range(8):
            for n0 in (0, 1024):
                sbi = (mb * 2 + n0 // 1024) % 2
                lt = p.dma("sync", B.tmpf.ap()[:, sbi * 1024:sbi * 1024 + 1024],
                           x2dram[mb * 128:(mb + 1) * 128, n0:n0 + 1024],
                           deps=[x2_sp[mb], ev_w2[-1]] +
                           ([res_ld[sbi]] if res_ld[sbi] else []),
                           sem=f"res{sbi}")
                tok = p.do("vector", lambda v, mb=mb, n0=n0, sbi=sbi, _i=i:
                           v.scalar_tensor_tensor(
                    out=x2[mb][:, n0:n0 + 1024],
                    in0=B.slot(16 + mb)[:, n0:n0 + 1024],
                    scalar=ccol(_i, 61 + mb),
                    in1=B.tmpf.ap()[:, sbi * 1024:sbi * 1024 + 1024],
                    op0=ALU.add, op1=ALU.add),
                    deps=[lt, d_ev[mb][-1], t_cols])
                f_toks.setdefault(mb, []).append(tok)
                res_ld[sbi] = tok
        for c in range(8):
            x2_tok[c] = [f_toks[c][-1]]
        psA_busy = [ev_w2[-1], ev_w2[-1]]
        prev_cols = f_toks[7][-1]
        rope_war = [f_toks[7][-1]]

    def delta_src(c):
        return (x2[c], x2_tok[c])

    st = emit_residual_add(pr, x_in, x_out, delta_src, macaron_scale,
                           lambda c: [],
                           first_deps=psA_busy +
                           [x2_tok[c][-1] for c in range(8)])
    return st


def host_pack_l2_ffn(ins):
    DEP, DIM, INNER = 6, 1024, 1024
    DI2 = 2816
    wqkv = ins["L2_ffn_Wqkv"].astype(np.float32)
    perm = []
    for j in range(8):
        for part in range(3):
            base = part * 1024
            perm += list(range(base + j * 128, base + j * 128 + 64 + 0))
    perm = []
    for j in range(8):
        for part in range(3):
            base = part * 1024
            perm += list(range(base + (2 * j) * 64, base + (2 * j) * 64 + 64))
            perm += list(range(base + (2 * j + 1) * 64, base + (2 * j + 1) * 64 + 64))
    wq_all = np.concatenate([pack_lhsT(np.ascontiguousarray(wqkv[d][:, perm]))
                             for d in range(DEP)], 1)
    wo_all = np.concatenate([pack_lhsT(ins["L2_ffn_Wo"][d].astype(np.float32))
                             for d in range(DEP)], 1)
    w1l, w2l = [], []
    for d in range(DEP):
        W1 = ins["L2_ffn_W1"][d].astype(np.float32)
        a = pad_to(W1[:, 0:2730], 1024, DI2)
        g = pad_to(W1[:, 2730:5460], 1024, DI2)
        cols = []
        for mb in range(22):
            cols.append(a[:, mb * 128:(mb + 1) * 128])
            cols.append(g[:, mb * 128:(mb + 1) * 128])
        w1l.append(pack_lhsT(np.concatenate(cols, 1)))
        w2l.append(pack_lhsT(pad_to(ins["L2_ffn_W2"][d].astype(np.float32),
                                    DI2, 1024)))
    colsd = np.zeros((128, DEP * 112), np.float32)
    for d in range(DEP):
        o = d * 112
        qs = ins["L2_ffn_q_scale"][d].astype(np.float32)
        ks = ins["L2_ffn_k_scale"][d].astype(np.float32)
        colsd[:, o:o + 1] = np.tile(qs * ks, 2)[:, None]
        colsd[:, o + 1:o + 9] = pack_cols(ins["L2_ffn_ln_g"][d].astype(np.float32))
        colsd[:, o + 9:o + 17] = pack_cols(ins["L2_ffn_ln_b"][d].astype(np.float32))
        b1 = ins["L2_ffn_b1"][d].astype(np.float32)
        b1a = pad_to(b1[0:2730], DI2)
        b1g = pad_to(b1[2730:5460], DI2)
        colsd[:, o + 17:o + 39] = pack_cols(b1a)
        colsd[:, o + 39:o + 61] = pack_cols(b1g)
        colsd[:, o + 61:o + 69] = pack_cols(ins["L2_ffn_b2"][d].astype(np.float32))
    ii = np.arange(128)
    lm = np.zeros((128, 1536), np.float32)
    NEG = -1e9
    for r in range(2):
        m = np.zeros((128, 512), np.float32)
        for c in range(4):
            k = r - c
            if k == 1 or k == -3:
                m[:, c * 128:(c + 1) * 128] = NEG
            elif k == 0:
                m[:, c * 128:(c + 1) * 128] = np.where(
                    ii[:, None] > ii[None, :], NEG, 0.0)
            elif k == -2:
                m[:, c * 128:(c + 1) * 128] = np.where(
                    ii[:, None] < ii[None, :], NEG, 0.0)
        lm[:, r * 512:(r + 1) * 512] = m
    lm[:, 1024:1152] = np.where(ii[:, None] < ii[None, :], NEG, 0.0)
    lm[:, 1280:1408] = 0.0
    lm[:, 1408:1536] = np.where(ii[:, None] < ii[None, :], NEG, 0.0)
    DH = 64
    half = np.arange(0, DH, 2, dtype=np.float64)
    inv_freq = 1.0 / (10000.0 ** (half / DH))
    t = np.arange(T, dtype=np.float64)
    freqs = np.concatenate([t[:, None] * inv_freq[None, :]] * 2, 1)
    base_scale = (half + 0.4 * DH) / (1.4 * DH)
    power = (t - T // 2) / 128.0
    scale = np.concatenate([base_scale[None, :] ** power[:, None]] * 2, 1)
    cos, sin = np.cos(freqs), np.sin(freqs)
    CS = (cos * scale).T
    SS = (sin * scale).T
    CSi = (cos / scale).T
    SSi = (sin / scale).T
    rope = np.concatenate([np.tile(a, (2, 1))[None] for a in
                           (CS, SS, CSi, SSi)], 0)
    rope = rope.transpose(1, 0, 2).reshape(128, 4 * T).astype(BF)
    return {
        "L2f_wqkvp": wq_all,
        "L2f_wop": wo_all,
        "L2f_w1p": np.concatenate(w1l, 1),
        "L2f_w2p": np.concatenate(w2l, 1),
        "L2f_rope": np.ascontiguousarray(rope),
        "L2f_cols": colsd,
        "L2f_lm": lm.astype(BF),
    }


def build_l2_ffn_program(n_sub=6, halves=2):
    pr = Prog()
    pr.load_consts()
    x_in = pr.din("x_cur", [D, T])
    x_out = pr.dout("x_next", [D, T])
    st = emit_l2_ffn(pr, x_in, x_out, n_sub=n_sub, halves=halves)
    pr.out_tokens = st
    pr.finish()
    return pr


def run_sublayer_l2ffn(x_full, ins, n_sub=6):
    pr = build_l2_ffn_program(n_sub)
    packed = host_pack_l2_ffn(ins)
    consts = host_consts()
    in_maps = []
    for core in range(8):
        m = dict(consts)
        m.update(packed)
        m["x_cur"] = np.ascontiguousarray(x_full[core].T)
        in_maps.append(m)
    res = run_bass_kernel_spmd(pr.nc, in_maps, core_ids=list(range(8)))
    out = np.stack([res.results[c]["x_next"].T for c in range(8)])
    pr.close()
    return out


def build_megakernel(n_sublayers=12):
    pr = Prog()
    pr.load_consts()
    x_in = pr.din("x_cur", [D, T])
    x_out = pr.dout("x_next", [D, T])
    xA = pr.dscratch("xA", [D, T], F32)
    xB = pr.dscratch("xB", [D, T], F32)
    seq = [
        (emit_l0_ffn, {}), (emit_l0_attn, {}), (emit_l0_ffn, {}),
        (emit_l1_ffn, {}), (emit_l1_attn, {}), (emit_l1_ffn, {}),
        (emit_l2_ffn, {}), (emit_l2_attn, {}), (emit_l2_ffn, {}),
        (emit_l3_ffn, {}), (emit_l3_attn, {}), (emit_l3_ffn, {}),
    ][:n_sublayers]
    cur = x_in
    nxt = xA
    st = None
    for idx, (fn, kw) in enumerate(seq):
        last = (idx == len(seq) - 1)
        dst = x_out if last else nxt
        st = fn(pr, cur, dst, **kw)
        pr.p.barrier()
        cur = dst
        nxt = xB if nxt is xA else xA
    pr.out_tokens = st
    pr.finish()
    return pr


_PROGS = {}


def _get_prog(name, builder):
    if name not in _PROGS:
        _PROGS[name] = builder()
    return _PROGS[name]


def _host_all_packs(ins):
    packs = {}
    packs.update(host_pack_l0_ffn(ins))
    packs.update(host_pack_l0_attn(ins))
    packs.update(host_pack_l1_ffn(ins))
    packs.update(host_pack_l1_attn(ins))
    packs.update(host_pack_l2_ffn(ins))
    packs.update(host_pack_l2_attn(ins))
    packs.update(host_pack_l3_ffn(ins))
    packs.update(host_pack_l3_attn(ins))
    packs["L1a_caus"] = np.triu(np.full((128, 128), -1e9, np.float32), 1)
    return packs


def kernel(**inputs):
    consts = host_consts()
    packs = _host_all_packs(inputs)
    pr = _get_prog("mega", build_megakernel)
    x = inputs["L0_attn_x"].astype(np.float32)
    base = dict(consts)
    base.update(packs)
    in_maps = []
    for c in range(8):
        m = dict(base)
        m["x_cur"] = np.ascontiguousarray(x[c].T)
        in_maps.append(m)
    res = run_bass_kernel_spmd(pr.nc, in_maps, core_ids=list(range(8)))
    return np.stack([res.results[c]["x_next"].T for c in range(8)])


def kernel_unfused(**inputs):
    ins = inputs
    consts = host_consts()
    packs = {}
    packs.update(host_pack_l0_ffn(ins))
    packs.update(host_pack_l0_attn(ins))
    packs.update(host_pack_l1_ffn(ins))
    packs.update(host_pack_l1_attn(ins))
    packs.update(host_pack_l2_ffn(ins))
    packs.update(host_pack_l2_attn(ins))
    packs.update(host_pack_l3_ffn(ins))
    packs.update(host_pack_l3_attn(ins))
    packs["L1a_caus"] = np.triu(np.full((128, 128), -1e9, np.float32), 1)

    progs = [
        ("l0f", build_l0_ffn_program),
        ("l0a", build_l0_attn_program),
        ("l1f", build_l1_ffn_program),
        ("l1a", build_l1_attn_program),
        ("l2f", build_l2_ffn_program),
        ("l2a", build_l2_attn_program),
        ("l3f", build_l3_ffn_program),
        ("l3a", build_l3_attn_program),
    ]
    seq = ["l0f", "l0a", "l0f", "l1f", "l1a", "l1f",
           "l2f", "l2a", "l2f", "l3f", "l3a", "l3f"]
    builders = dict(progs)

    x = inputs["L0_attn_x"].astype(np.float32)
    xs = [np.ascontiguousarray(x[c].T) for c in range(8)]
    base = dict(consts)
    base.update(packs)
    for name in seq:
        pr = _get_prog(name, builders[name])
        in_maps = []
        for c in range(8):
            m = dict(base)
            m["x_cur"] = xs[c]
            in_maps.append(m)
        res = run_bass_kernel_spmd(pr.nc, in_maps, core_ids=list(range(8)))
        xs = [np.ascontiguousarray(res.results[c]["x_next"]) for c in range(8)]
    out = np.stack([xs[c].T for c in range(8)])
    return out


# version 09
# speedup vs baseline: 1.0050x; 1.0050x over previous
import contextlib
import math
import numpy as np
import ml_dtypes
import concourse.bass as bass
import concourse.mybir as mybir
from concourse.bass_utils import run_bass_kernel_spmd

F32 = mybir.dt.float32
BF16 = mybir.dt.bfloat16
U8 = mybir.dt.uint8
AF = mybir.ActivationFunctionType
ALU = mybir.AluOpType
BF = ml_dtypes.bfloat16

T = 2048
D = 1024
DC = D // 128
NT = 512

N_SLOTS = 40

ENGINES = ("sync", "scalar", "vector", "tensor", "gpsimd")


class P:

    def __init__(self, nc):
        self.nc = nc
        self.queues = {e: [] for e in ENGINES}
        self.cnt = {}
        self.hwm = {e: {} for e in ENGINES}
        for e in ENGINES:
            self.cnt["C_" + e] = 0
            self.cnt["D_" + e] = 0

    def _waits(self, eng, deps):
        need = {}
        for d in deps:
            if d is None:
                continue
            sem, val, peng, is_dma = d
            if peng == eng and not is_dma and eng == "tensor":
                continue
            if val <= self.hwm[eng].get(sem, 0):
                continue
            need[sem] = max(need.get(sem, 0), val)
        for sem, val in need.items():
            self.hwm[eng][sem] = val
        return need

    def do(self, eng, fn, deps=()):
        need = self._waits(eng, deps)
        semn = "C_" + eng
        self.cnt[semn] += 1
        val = self.cnt[semn]
        self.queues[eng].append(("do", fn, need, semn, val))
        return (semn, val, eng, False)

    def barrier(self):
        snap = {s: v for s, v in self.cnt.items() if v > 0}
        for eng in ENGINES:
            need = {}
            for s, v in snap.items():
                if s == "C_" + eng:
                    continue
                if v <= self.hwm[eng].get(s, 0):
                    continue
                need[s] = v
                self.hwm[eng][s] = v
            if need:
                self.queues[eng].append(("waits", None, need, None, None))

    def dma(self, eng, out, in_, deps=(), sem="misc", **kw):
        need = self._waits(eng, deps)
        semn = "D_" + sem
        if semn not in self.cnt:
            self.cnt[semn] = 0
        self.cnt[semn] += 16
        val = self.cnt[semn]
        self.queues[eng].append(("dma", (out, in_, kw), need, semn, val))
        return (semn, val, eng, True)

    def finalize(self, block, out_tokens):
        sems = {}
        self._sem_stack = contextlib.ExitStack()
        for name in self.cnt:
            sems[name] = self._sem_stack.enter_context(self.nc.semaphore(name))
        for eng in ENGINES:
            q = self.queues[eng]
            if not q and eng != "gpsimd":
                continue

            def mk(eng=eng, q=q):
                def body(engobj):
                    for kind, payload, waits, semn, val in q:
                        for s, v in waits.items():
                            engobj.wait_ge(sems[s], v)
                        if kind == "waits":
                            continue
                        if kind == "do":
                            inst = payload(engobj)
                            inst.then_inc(sems[semn], 1)
                        else:
                            out, in_, kw = payload
                            engobj.dma_start(out=out, in_=in_, **kw).then_inc(
                                sems[semn], 16)
                    if eng == "gpsimd":
                        fin = {}
                        for d in out_tokens:
                            sem, v, _, _ = d
                            fin[sem] = max(fin.get(sem, 0), v)
                        for s, v in fin.items():
                            engobj.wait_ge(sems[s], v)
                return body
            getattr(block, eng)(mk())


def pack_lhsT(W):
    K, M = W.shape
    assert K % 128 == 0 and M % 128 == 0, (K, M)
    Kc, Mb = K // 128, M // 128
    return np.ascontiguousarray(
        W.reshape(Kc, 128, Mb, 128).transpose(1, 2, 0, 3).reshape(128, Mb * Kc * 128))


def pack_cols(v):
    M = v.shape[0]
    assert M % 128 == 0
    return np.ascontiguousarray(v.reshape(M // 128, 128).T)


def pad_to(a, *shape):
    out = np.zeros(shape, a.dtype)
    out[tuple(slice(0, s) for s in a.shape)] = a
    return out


class Bufs:
    def __init__(self, nc, stack):
        e = stack.enter_context
        self.arena = e(nc.sbuf_tensor("arena", [128, N_SLOTS * T], BF16))
        self.wb = [e(nc.sbuf_tensor(f"wb{i}", [128, 4096], BF16)) for i in range(2)]
        self.tmpf = e(nc.sbuf_tensor("tmpf", [128, 2 * T], F32))
        self.stg = e(nc.sbuf_tensor("stg", [128, 2048], BF16))
        self.cols = e(nc.sbuf_tensor("cols", [128, 320], F32))
        self.ident_bf = e(nc.sbuf_tensor("c_ident_bf_sb", [128, 128], BF16))
        self.ident_f32 = e(nc.sbuf_tensor("c_ident_f32_sb", [128, 128], F32))
        self.ones_row = e(nc.sbuf_tensor("c_ones_row_sb", [1, 128], F32))
        self.cpool = e(nc.sbuf_tensor("c_pool_sb", [128, len(CONST_LIST)], F32))
        self.onescol_bf = e(nc.sbuf_tensor("c_onescol_bf_sb", [128, 1], BF16))
        self.ident64 = e(nc.sbuf_tensor("c_ident64_sb", [128, 64], BF16))
        self.rot64 = e(nc.sbuf_tensor("c_rot64_sb", [128, 64], BF16))
        self.selrow = e(nc.sbuf_tensor("c_selrow_sb", [128, 128], F32))
        self.rot128 = e(nc.sbuf_tensor("c_rot128_sb", [128, 128], BF16))
        self.psA = e(nc.psum_tensor("psA", [128, 2048], F32))
        self.ps4 = e(nc.psum_tensor("ps4", [128, 512], F32))
        self.ps5 = e(nc.psum_tensor("ps5", [128, 512], F32))
        self.ps6 = e(nc.psum_tensor("ps6", [128, 512], F32))
        self.psT = e(nc.psum_tensor("psT", [128, 1024], BF16))

    def cp(self, value):
        i = CONST_LIST.index(value)
        return self.cpool.ap()[:, i:i + 1]

    @property
    def rowa(self):
        return self.tmpf.ap()[0:1, 0:T]

    @property
    def rowb(self):
        return self.tmpf.ap()[0:1, T:2 * T]


    def slot(self, i, n=1):
        return self.arena.ap()[:, i * T:(i + n) * T]

    def slot_f32(self, i, n=1):
        return self.arena.ap()[:, i * T:(i + 2 * n) * T].bitcast(F32)

    def bank(self, i):
        return self.psA.ap()[:, i * NT:(i + 1) * NT]


CONST_LIST = [1e-5, 1.0, 1e-12, 1e-24, 0.5, -1e30, 1e-30, 2.0, -6.0, 6.0, 1e-5 + 1e-10]


def host_consts():
    ident = np.eye(128, dtype=np.float32)
    stat = np.zeros((128, 64), np.float32)
    stat[:, 0] = 1.0
    stat[:, 32] = 1.0
    pool = np.tile(np.asarray(CONST_LIST, np.float32)[None, :], (128, 1))
    return {
        "c_ident_bf": ident.astype(BF),
        "c_ident_f32": ident,
        "c_ones_row": np.ones((1, 128), np.float32),
        "c_onescol_bf": np.ones((128, 1), BF),
        "c_stat_lhsT": stat.astype(BF),
        "c_pool": pool,
        "c_ident64": np.tile(np.eye(64, dtype=np.float32), (2, 1)).astype(BF),
        "c_rot64": _host_rot64(),
        "c_selrow": _host_selrow(),
        "c_rot128": _host_rot128(),
    }


def _host_selrow():
    m = np.zeros((128, 128), np.float32)
    m[0, 0:64] = 1.0
    m[32, 64:128] = 1.0
    return m


def _host_rot128():
    M = np.zeros((64, 64), np.float32)
    for i in range(32):
        M[i, i + 32] = -1.0
        M[i + 32, i] = 1.0
    out = np.zeros((128, 128), np.float32)
    out[0:64, 0:64] = M.T
    out[64:128, 64:128] = M.T
    return out.astype(BF)


def _host_rot64():
    M = np.zeros((64, 64), np.float32)
    for i in range(32):
        M[2 * i, 2 * i + 1] = -1.0
        M[2 * i + 1, 2 * i] = 1.0
    return np.tile(M.T, (2, 1)).astype(BF)


class Prog:

    def __init__(self):
        self.nc = bass.Bass("TRN2", target_bir_lowering=False)
        self.stack = contextlib.ExitStack()
        self.p = P(self.nc)
        self.B = Bufs(self.nc, self.stack)
        self.stat_lhsT = self.stack.enter_context(
            self.nc.sbuf_tensor("c_stat_lhsT_sb", [128, 64], BF16))
        self.dram_in = {}
        self.dram_out = {}
        self.wb_last_use = [None, None]
        self.out_tokens = []
        self.const_toks = []

    def din(self, name, shape, dtype=F32):
        if name not in self.dram_in:
            self.dram_in[name] = self.nc.dram_tensor(
                name, list(shape), dtype, kind="ExternalInput").ap()
        return self.dram_in[name]

    def dout(self, name, shape, dtype=F32):
        if name not in self.dram_out:
            self.dram_out[name] = self.nc.dram_tensor(
                name, list(shape), dtype, kind="ExternalOutput").ap()
        return self.dram_out[name]

    def dscratch(self, name, shape, dtype):
        key = "scr_" + name
        if key not in self.dram_in:
            self.dram_in[key] = self.nc.dram_tensor(
                key, list(shape), dtype, kind="Internal").ap()
        return self.dram_in[key]

    def load_consts(self):
        B, p = self.B, self.p
        for name, sb in [("c_ident_bf", B.ident_bf), ("c_ident_f32", B.ident_f32),
                         ("c_ones_row", B.ones_row), ("c_onescol_bf", B.onescol_bf),
                         ("c_stat_lhsT", self.stat_lhsT), ("c_pool", B.cpool),
                         ("c_ident64", B.ident64), ("c_rot64", B.rot64),
                         ("c_selrow", B.selrow),
                         ("c_rot128", B.rot128)]:
            d = self.din(name, host_consts()[name].shape,
                         BF16 if host_consts()[name].dtype == BF else F32)
            self.const_toks.append(p.dma("sync", sb.ap()[:, :], d[:, :],
                                          sem="cst"))

    def finish(self):
        block = self.stack.enter_context(self.nc.Block())
        self.p.finalize(block, self.out_tokens)

    def close(self):
        self.stack.close()


def emit_ln(pr, x_dram, xn_slot, xb_slot, x2_slot, deps=(), eps=1e-5,
            gcol=None, bcol=None, n_chunks=DC, d_norm=None, shift=False,
            var_scale=1.0, var_bias=None, src_sbuf=None, src_deps=None,
            cols_dep=()):
    CT = list(pr.const_toks)
    CD = list(cols_dep)
    """LN over features of the DRAM fp32 stream x_dram [n_chunks*128, T].

    Loads x as bf16 into arena slots xb_slot.., squares into x2_slot..,
    PE-stats into psA (S1 row0, S2 row32), applies into xn_slot.. (bf16).
    gcol/bcol: optional per-feature gain/bias fp32 cols [128,1] per chunk.
    shift=True: read tokens shifted by -1 (x[t-1], zero at t=0): only valid
    when x_dram has a zero guard column (we instead handle via partial DMA).
    Returns list of per-chunk done tokens (xn ready).
    """
    B, p = pr.B, pr.p
    d_norm = d_norm or (n_chunks * 128)
    if src_sbuf is not None:
        xb = src_sbuf
    else:
        xb = [B.slot(xb_slot + c) for c in range(n_chunks)]
    x2 = [B.slot(x2_slot + c) for c in range(n_chunks)]
    xn = [B.slot(xn_slot + c) for c in range(n_chunks)]
    t_load = []
    for c in range(n_chunks):
        if src_sbuf is not None:
            t_load.append((src_deps or (lambda c: None))(c))
        else:
            t_load.append(p.dma("gpsimd", xb[c], x_dram[c * 128:(c + 1) * 128, :],
                                deps=deps, sem=f"lnc{c}"))
    t_sq = []
    for c in range(n_chunks):
        t_sq.append(p.do("scalar", lambda s, c=c: s.activation(
            out=x2[c], in_=xb[c], func=AF.Square), deps=[t_load[c]]))
    stat_t = []
    for ntile in range(T // NT):
        sl = slice(ntile * NT, (ntile + 1) * NT)
        tt = None
        for c in range(n_chunks):
            tt = p.do("tensor", lambda t, c=c, sl=sl, first=(c == 0),
                      last=(c == n_chunks - 1): t.matmul(
                B.psA[0:1, sl], B.onescol_bf.ap()[:, :], xb[c][:, sl],
                start=first, stop=last), deps=[t_load[c]] + CT)
        for c in range(n_chunks):
            tt = p.do("tensor", lambda t, c=c, sl=sl, first=(c == 0),
                      last=(c == n_chunks - 1): t.matmul(
                B.psA[32:33, sl], B.onescol_bf.ap()[:, :], x2[c][:, sl],
                start=first, stop=last), deps=[t_sq[c]])
        stat_t.append(tt)
    t_mu = p.do("vector", lambda v: v.tensor_scalar(
        out=B.rowa, in0=B.psA[0:1, 0:T], scalar1=1.0 / d_norm,
        scalar2=None, op0=ALU.mult), deps=stat_t)
    t_m2 = p.do("vector", lambda v: v.tensor_scalar(
        out=B.rowb, in0=B.psA[32:33, 0:T], scalar1=1.0 / d_norm,
        scalar2=None, op0=ALU.mult), deps=stat_t)
    mu2row = B.arena.ap()[0:1, xn_slot * T:(xn_slot + 2) * T].bitcast(F32)
    t_mu2 = p.do("scalar", lambda s: s.activation(
        out=mu2row, in_=B.rowa, func=AF.Square), deps=[t_mu, t_m2])
    t_var = p.do("vector", lambda v: v.tensor_sub(
        out=B.rowb, in0=B.rowb, in1=mu2row),
        deps=[t_mu2])
    vb = eps if var_bias is None else var_bias
    t_ln = p.do("scalar", lambda s: s.activation(
        out=B.rowb, in_=B.rowb, func=AF.Ln, bias=B.cp(vb)[0:1, :],
        scale=float(var_scale)), deps=[t_var] + CT)
    t_rstd = p.do("scalar", lambda s: s.activation(
        out=B.rowb, in_=B.rowb, func=AF.Exp, scale=-0.5),
        deps=[t_ln])
    done = [[] for _ in range(n_chunks)]
    prev_half_applies = []
    for half in range(2):
        hs = slice(half * 1024, (half + 1) * 1024)
        bts = []
        for q in range(2):
            psl = slice(q * NT, (q + 1) * NT)
            ssl = slice(half * 1024 + q * NT, half * 1024 + (q + 1) * NT)
            bts.append(p.do("tensor", lambda t, psl=psl, ssl=ssl: t.matmul(
                B.psA[0:128, psl], B.ones_row.ap()[:, :], B.rowa[:, ssl],
                start=True, stop=True),
                deps=[t_mu, t_rstd] + prev_half_applies + CT))
            bts.append(p.do("tensor", lambda t, psl=psl, ssl=ssl: t.matmul(
                B.psA[0:128, 1024 + psl.start:1024 + psl.stop],
                B.ones_row.ap()[:, :], B.rowb[:, ssl],
                start=True, stop=True), deps=[t_rstd] + prev_half_applies))
        for c in range(n_chunks):
            t1 = p.do("vector", lambda v, c=c, hs=hs: v.tensor_sub(
                out=x2[c][:, hs], in0=xb[c][:, hs], in1=B.psA[0:128, 0:1024]),
                deps=bts + [t_sq[c]])
            if gcol is None:
                t2 = p.do("vector", lambda v, c=c, hs=hs: v.tensor_mul(
                    out=xn[c][:, hs], in0=x2[c][:, hs], in1=B.psA[0:128, 1024:2048]),
                    deps=[t1])
            else:
                t1b = p.do("vector", lambda v, c=c, hs=hs: v.tensor_mul(
                    out=x2[c][:, hs], in0=x2[c][:, hs], in1=B.psA[0:128, 1024:2048]),
                    deps=[t1])
                if bcol is not None:
                    t2 = p.do("vector", lambda v, c=c, hs=hs: v.scalar_tensor_tensor(
                        out=xn[c][:, hs], in0=x2[c][:, hs], scalar=gcol(c),
                        in1=bcol(c).rearrange("p one -> p one ()").broadcast_to(
                            [128, 1, 1024]),
                        op0=ALU.mult, op1=ALU.add), deps=[t1b] + CD)
                else:
                    t2 = p.do("vector", lambda v, c=c, hs=hs: v.tensor_scalar(
                        out=xn[c][:, hs], in0=x2[c][:, hs], scalar1=gcol(c),
                        scalar2=None, op0=ALU.mult), deps=[t1b] + CD)
            done[c].append(t2)
            prev_half_applies = prev_half_applies + [t2]
    return [d[-1] for d in done], done


class WStream:

    def __init__(self, pr, dram, Kc, name=""):
        self.pr = pr
        self.dram = dram
        self.Kc = Kc
        if not hasattr(pr, "wb_last_use"):
            pr.wb_last_use = [None, None]
        self.load_tok = {}

    def load(self, mb, deps=()):
        b = mb % 2
        B, p = self.pr.B, self.pr.p
        dep = list(deps)
        if self.pr.wb_last_use[b] is not None:
            dep.append(self.pr.wb_last_use[b])
        tok = p.dma("gpsimd", B.wb[b].ap()[:, 0:self.Kc * 128],
                    self.dram[:, mb * self.Kc * 128:(mb + 1) * self.Kc * 128],
                    deps=dep, sem=f"wb{b}")
        self.load_tok[mb] = tok
        return tok

    def tile(self, mb, kc):
        return self.pr.B.wb[mb % 2].ap()[:, kc * 128:(kc + 1) * 128]

    def used(self, mb, tok):
        self.pr.wb_last_use[mb % 2] = tok


def mm_evac_stream(pr, *, Mb, Kc, ws, rhs, evac, deps_rhs=(), n_total=T,
                   n_tile=1024, psum_banks=(0, 1, 2, 3), extra_mm_deps=None,
                   prefetch=2, region_init=()):
    B, p = pr.B, pr.p
    assert n_tile in (512, 1024)
    if deps_rhs is None:
        deps_rhs = ()
    deps_rhs = list(deps_rhs) if isinstance(deps_rhs, (list, tuple)) else [deps_rhs]
    if n_tile == 1024:
        regions = [B.psA[0:128, 0:1024], B.psA[0:128, 1024:2048]]
    else:
        regions = [B.bank(i) for i in psum_banks]
    region_busy = [list(region_init) or None for _ in regions]
    evac_toks = []
    ri = 0
    for mb in range(min(prefetch, Mb)):
        ws.load(mb, deps=deps_rhs)
    for mb in range(Mb):
        last_mm = None
        for n0 in range(0, n_total, n_tile):
            ps = regions[ri]
            dep0 = [ws.load_tok[mb]] + deps_rhs
            rb = region_busy[ri]
            if rb is not None:
                dep0 += rb if isinstance(rb, list) else [rb]
            for kc in range(Kc):
                dd = list(dep0) if kc == 0 else []
                if extra_mm_deps:
                    dd = dd + extra_mm_deps(mb, kc, n0)
                for q in range(n_tile // NT):
                    a, b_ = n0 + q * NT, n0 + (q + 1) * NT
                    last_mm = p.do("tensor", lambda t, mb=mb, kc=kc, a=a, b_=b_,
                                   ps=ps, n0=n0, kc0=(kc == 0): t.matmul(
                        ps[:, (a - n0):(b_ - n0)],
                        ws.tile(mb, kc), rhs(kc, a, b_),
                        start=kc0, stop=(kc == Kc - 1)), deps=dd)
            tok = evac(mb, n0, ps, last_mm)
            evac_toks.append(tok)
            region_busy[ri] = tok
            ri = (ri + 1) % len(regions)
        ws.used(mb, last_mm)
        if mb + prefetch < Mb:
            ws.load(mb + prefetch)
    return evac_toks


def emit_residual_add(pr, x_dram_in, x_dram_out, delta_src, scale, deps_per_chunk,
                      n_chunks=DC, first_deps=()):
    B, p = pr.B, pr.p
    store = []
    slot_busy = [None, None]
    for c in range(n_chunks):
        sbuf = B.tmpf.ap()[:, (c % 2) * T:(c % 2 + 1) * T]
        dep = [slot_busy[c % 2]] if slot_busy[c % 2] else list(first_deps)
        t_ld = p.dma("sync", sbuf, x_dram_in[c * 128:(c + 1) * 128, :], deps=dep,
                     sem=f"res{c % 2}")
        dap, ddeps = delta_src(c)
        t_add = p.do("vector", lambda v, sbuf=sbuf, dap=dap: v.scalar_tensor_tensor(
            out=sbuf, in0=dap, scalar=float(scale), in1=sbuf,
            op0=ALU.mult, op1=ALU.add), deps=[t_ld] + ddeps + deps_per_chunk(c))
        t_st = p.dma("scalar", x_dram_out[c * 128:(c + 1) * 128, :], sbuf,
                     deps=[t_add], sem=f"xst{c % 2}")
        slot_busy[c % 2] = t_st
        store.append(t_st)
    return store


def emit_l3_ffn(pr, x_in, x_out, lname="L3"):
    B, p = pr.B, pr.p
    H4 = 4096
    Kc_d, Mb_h = DC, H4 // 128
    w1 = pr.din(f"{lname}_w1p", [128, 32 * 8 * 128])
    w2 = pr.din(f"{lname}_w2p", [128, 32 * 8 * 128])
    h1 = pr.din(f"{lname}_h1p", [128, 32 * 32 * 128])
    h2 = pr.din(f"{lname}_h2p", [128, 32 * 32 * 128])
    w3 = pr.din(f"{lname}_w3p", [128, 8 * 32 * 128])
    scol = pr.din(f"{lname}_scols", [128, 136])
    t_sc = p.dma("sync", B.cols.ap()[:, 0:136], scol[:, 0:136], sem="col")

    hb = pr.dscratch(lname + "_hb", [H4, T], BF16)

    t_ln, _ = emit_ln(pr, x_in, xn_slot=0, xb_slot=8, x2_slot=16, deps=())
    xn = [B.slot(c) for c in range(DC)]
    Hsl = [B.slot(8 + c) for c in range(32)]

    ws2 = WStream(pr, w2, Kc_d)

    def rhs_xn(kc, a, b_):
        return xn[kc][:, a:b_]

    def evac_g2(mb, n0, ps, dep):
        return p.do("scalar", lambda s, mb=mb, n0=n0, ps=ps: s.activation(
            out=Hsl[mb][:, n0:n0 + 1024], in_=ps[:, 0:1024], func=AF.Copy,
            scale=B.cols.ap()[:, 32 + mb:33 + mb]), deps=[dep, t_sc])

    ev2 = mm_evac_stream(pr, Mb=32, Kc=Kc_d, ws=ws2, rhs=rhs_xn, evac=evac_g2,
                         deps_rhs=t_ln)

    ws1 = WStream(pr, w1, Kc_d)
    prod_toks = []

    def evac_g1(mb, n0, ps, dep):
        dst = B.stg.ap()[:, (n0 // 1024) * 1024:(n0 // 1024) * 1024 + 1024]
        t_a = p.do("scalar", lambda s, ps=ps, dst=dst, mb=mb: s.activation(
            out=dst, in_=ps[:, 0:1024], func=AF.Silu,
            scale=B.cols.ap()[:, mb:mb + 1]), deps=[dep, t_sc])
        t_m = p.do("vector", lambda v, dst=dst, mb=mb, n0=n0: v.tensor_mul(
            out=Hsl[mb][:, n0:n0 + 1024], in0=Hsl[mb][:, n0:n0 + 1024], in1=dst),
            deps=[t_a, ev2[mb * 2 + n0 // 1024]])
        prod_toks.append(t_m)
        return t_m

    ev1 = mm_evac_stream(pr, Mb=32, Kc=Kc_d, ws=ws1, rhs=rhs_xn, evac=evac_g1,
                         deps_rhs=t_ln, region_init=ev2[-2:])

    h_ready = {}
    for i, tok in enumerate(ev1):
        h_ready.setdefault(i // 2, []).append(tok)

    ws_h1 = WStream(pr, h1, 32)
    hb_w = []

    def rhs_H(kc, a, b_):
        return Hsl[kc][:, a:b_]

    def evac_h1(mb, n0, ps, dep):
        dst = B.stg.ap()[:, (n0 // 1024) * 1024:(n0 // 1024) * 1024 + 1024]
        t_a = p.do("scalar", lambda s, ps=ps, dst=dst, mb=mb: s.activation(
            out=dst, in_=ps[:, 0:1024], func=AF.Gelu,
            scale=B.cols.ap()[:, 64 + mb:65 + mb]), deps=[dep])
        t_w = p.dma("sync", hb[mb * 128:(mb + 1) * 128, n0:n0 + 1024], dst,
                    deps=[t_a], sem="hbw")
        hb_w.append(t_w)
        return t_w

    def mm_deps_h1(mb, kc, n0):
        return h_ready[kc] if kc in h_ready else []

    ev_h1 = mm_evac_stream(pr, Mb=32, Kc=32, ws=ws_h1, rhs=rhs_H, evac=evac_h1,
                           extra_mm_deps=mm_deps_h1, region_init=ev1[-2:])

    t_h1_ld = []
    for c in range(32):
        dep = [ev_h1[c * 2], ev_h1[c * 2 + 1], *h_ready.get(c, [])]
        t_h1_ld.append(p.dma("scalar", Hsl[c], hb[c * 128:(c + 1) * 128, :],
                             deps=list(ev_h1), sem=f"hbr{c % 4}"))

    hb2 = pr.dscratch(lname + "_hb2", [H4, T], BF16)
    ws_h2 = WStream(pr, h2, 32)
    hb2_w = []

    def evac_h2(mb, n0, ps, dep):
        dst = B.stg.ap()[:, (n0 // 1024) * 1024:(n0 // 1024) * 1024 + 1024]
        t_a = p.do("scalar", lambda s, ps=ps, dst=dst, mb=mb: s.activation(
            out=dst, in_=ps[:, 0:1024], func=AF.Gelu,
            scale=B.cols.ap()[:, 96 + mb:97 + mb]), deps=[dep])
        t_w = p.dma("sync", hb2[mb * 128:(mb + 1) * 128, n0:n0 + 1024], dst,
                    deps=[t_a], sem="hb2w")
        hb2_w.append(t_w)
        return t_w

    def mm_deps_h2(mb, kc, n0):
        return [t_h1_ld[kc]]

    ev_h2 = mm_evac_stream(pr, Mb=32, Kc=32, ws=ws_h2, rhs=rhs_H, evac=evac_h2,
                           extra_mm_deps=mm_deps_h2, region_init=ev_h1[-2:])

    t_h2_ld = []
    for c in range(32):
        t_h2_ld.append(p.dma("scalar", Hsl[c], hb2[c * 128:(c + 1) * 128, :],
                             deps=list(ev_h2), sem=f"hb2r{c % 4}"))

    ws3 = WStream(pr, w3, 32)
    delta = [B.slot(c) for c in range(DC)]
    dtoks = {}

    def evac_o(mb, n0, ps, dep):
        tok = p.do("scalar", lambda s, ps=ps, mb=mb, n0=n0: s.activation(
            out=delta[mb][:, n0:n0 + 1024], in_=ps[:, 0:1024], func=AF.Copy,
            scale=B.cols.ap()[:, 128 + mb:129 + mb]), deps=[dep])
        dtoks.setdefault(mb, []).append(tok)
        return tok

    def mm_deps_o(mb, kc, n0):
        return [t_h2_ld[kc]]

    ev_o = mm_evac_stream(pr, Mb=DC, Kc=32, ws=ws3, rhs=rhs_H, evac=evac_o,
                          extra_mm_deps=mm_deps_o, region_init=ev_h2[-2:])

    def delta_src(c):
        return (delta[c], dtoks[c])

    st = emit_residual_add(pr, x_in, x_out, delta_src, 0.5,
                           lambda c: [], n_chunks=DC,
                           first_deps=[t_h2_ld[-1]])
    return st


def host_pack_l3_ffn(ins, pref="L3"):
    out = {}
    out[f"{pref}_w1p"] = pack_lhsT(np.ascontiguousarray(ins[f"{pref}_ffn_w1_w"].T))
    out[f"{pref}_w2p"] = pack_lhsT(np.ascontiguousarray(ins[f"{pref}_ffn_w2_w"].T))
    out[f"{pref}_h1p"] = pack_lhsT(np.ascontiguousarray(ins[f"{pref}_ffn_h1_w"].T))
    out[f"{pref}_h2p"] = pack_lhsT(np.ascontiguousarray(ins[f"{pref}_ffn_h2_w"].T))
    out[f"{pref}_w3p"] = pack_lhsT(np.ascontiguousarray(ins[f"{pref}_ffn_w3_w"].T))
    sc = np.zeros((128, 320), np.float32)
    sc[:, 0:32] = pack_cols(ins[f"{pref}_ffn_w1_s"])
    sc[:, 32:64] = pack_cols(ins[f"{pref}_ffn_w2_s"])
    sc[:, 64:96] = pack_cols(ins[f"{pref}_ffn_h1_s"])
    sc[:, 96:128] = pack_cols(ins[f"{pref}_ffn_h2_s"])
    sc[:, 128:136] = pack_cols(ins[f"{pref}_ffn_w3_s"])
    out[f"{pref}_scols"] = np.ascontiguousarray(sc[:, 0:136])
    return out




def emit_transpose8(pr, src_fn, dst_fn, n_tc, deps_fn, n_c=8, dtype_bf=True,
                    evac_eng="vector"):
    B, p = pr.B, pr.p
    toks = []
    prev_evac = None
    for tc in range(n_tc):
        last = None
        for c in range(n_c):
            dd = list(deps_fn(tc, c)) + list(pr.const_toks)
            if prev_evac is not None:
                dd.append(prev_evac)
            last = p.do("tensor", lambda t, tc=tc, c=c: t.transpose(
                B.psT[0:128, c * 128:(c + 1) * 128], src_fn(tc, c),
                B.ident_bf.ap()[:, :]), deps=dd)
        tok = p.do(evac_eng, lambda e, tc=tc: (
            e.tensor_copy(out=dst_fn(tc), in_=B.psT[0:128, 0:n_c * 128])
            if evac_eng == "vector" else
            e.activation(out=dst_fn(tc), in_=B.psT[0:128, 0:n_c * 128],
                         func=AF.Copy)), deps=[last])
        prev_evac = tok
        toks.append(tok)
    return toks


C_MAGIC = float(1.5 * 2 ** 23)


def emit_mx_quantize_tile(pr, xin, xout, scratch_v, u_buf, q_buf, m1, m2,
                          scales, rinv, deps, width=1024, out_deps=None,
                          a_buf=None):
    B, p = pr.B, pr.p
    nb = width // 32

    def r3(ap, w=width):
        return ap.rearrange("p (b j) -> p b j", b=nb)

    t0 = p.do("vector", lambda v: v.tensor_reduce(
        out=scales.rearrange("p (b one) -> p b one", b=nb),
        in_=r3(xin), axis=mybir.AxisListType.X, op=ALU.max,
        apply_absolute_value=True), deps=deps)
    t1 = p.do("vector", lambda v: v.tensor_scalar(
        out=scales, in0=scales, scalar1=1.0 / 7.5, scalar2=1e-30,
        op0=ALU.mult, op1=ALU.max), deps=[t0])
    t2 = p.do("vector", lambda v: v.reciprocal(out=rinv, in_=scales), deps=[t1])
    tu = p.do("vector", lambda v: v.tensor_tensor(
        out=r3(u_buf), in0=r3(xin),
        in1=rinv.rearrange("p b -> p b ()").broadcast_to([128, nb, 32]),
        op=ALU.mult), deps=[t2])
    t_abs = p.do("scalar", lambda s: s.activation(
        out=a_buf, in_=u_buf, func=AF.Abs), deps=[tu])
    tm1 = p.do("vector", lambda v: v.tensor_scalar(
        out=m1, in0=a_buf, scalar1=1.9375, scalar2=None,
        op0=ALU.is_ge), deps=[t_abs])
    tm2 = p.do("vector", lambda v: v.tensor_scalar(
        out=m2, in0=a_buf, scalar1=3.875, scalar2=None,
        op0=ALU.is_ge), deps=[t_abs])
    tq = p.do("vector", lambda v: v.tensor_scalar(
        out=q_buf, in0=u_buf, scalar1=8.0, scalar2=C_MAGIC,
        op0=ALU.mult, op1=ALU.add), deps=[tu])
    tq = p.do("vector", lambda v: v.tensor_scalar(
        out=q_buf, in0=q_buf, scalar1=C_MAGIC, scalar2=0.125,
        op0=ALU.subtract, op1=ALU.mult), deps=[tq])
    tv = p.do("vector", lambda v: v.tensor_scalar(
        out=scratch_v, in0=u_buf, scalar1=4.0, scalar2=C_MAGIC,
        op0=ALU.mult, op1=ALU.add), deps=[tu])
    tv = p.do("vector", lambda v: v.tensor_scalar(
        out=scratch_v, in0=scratch_v, scalar1=C_MAGIC, scalar2=0.25,
        op0=ALU.subtract, op1=ALU.mult), deps=[tv])
    tq = p.do("vector", lambda v: v.copy_predicated(
        out=q_buf, mask=m1, data=scratch_v), deps=[tq, tv, tm1])
    tv = p.do("vector", lambda v: v.tensor_scalar(
        out=scratch_v, in0=u_buf, scalar1=2.0, scalar2=C_MAGIC,
        op0=ALU.mult, op1=ALU.add), deps=[tv, tq])
    tv = p.do("vector", lambda v: v.tensor_scalar(
        out=scratch_v, in0=scratch_v, scalar1=C_MAGIC, scalar2=0.5,
        op0=ALU.subtract, op1=ALU.mult), deps=[tv])
    tq = p.do("vector", lambda v: v.copy_predicated(
        out=q_buf, mask=m2, data=scratch_v), deps=[tq, tv, tm2])
    tout = p.do("vector", lambda v: v.tensor_tensor(
        out=r3(xout), in0=r3(q_buf),
        in1=scales.rearrange("p b -> p b ()").broadcast_to([128, nb, 32]),
        op=ALU.mult), deps=[tq] + (out_deps or []))
    return tout


def emit_l1_ffn(pr, x_in, x_out, lname="L1f"):
    B, p = pr.B, pr.p
    wdram = pr.din(lname + "_w", [D, D])
    bias = pr.din(lname + "_bcols", [128, 8])
    t_b = p.dma("sync", B.cols.ap()[:, 0:8], bias[:, :], deps=pr.const_toks,
                sem="col")

    t_ln, _ = emit_ln(pr, x_in, xn_slot=0, xb_slot=8, x2_slot=16, deps=())

    def xn_tm(tc):
        return B.arena.ap()[:, 8 * T + tc * 1024: 8 * T + (tc + 1) * 1024]

    tr1 = emit_transpose8(
        pr, lambda tc, c: B.slot(c)[:, tc * 128:(tc + 1) * 128],
        xn_tm, 16, lambda tc, c: [t_ln[c]])

    u_buf = B.tmpf.ap()[:, 0:1024]
    q_buf = B.tmpf.ap()[:, 1024:2048]
    vscr = B.slot_f32(6, 1)[:, 0:1024]
    a_buf = B.slot_f32(6, 1)[:, 1024:2048]
    m1 = B.stg.ap().bitcast(U8)[:, 0:1024]
    m2 = B.stg.ap().bitcast(U8)[:, 1024:2048]
    scales = B.stg.ap()[:, 1024 + 0:1024 + 64].bitcast(F32)
    rinv = B.stg.ap()[:, 1024 + 64:1024 + 128].bitcast(F32)
    q_toks = []
    tq_prev = None
    for tc in range(16):
        dd = [tr1[tc]]
        if tc < 2:
            dd = dd + [tr1[-1]]
        if tq_prev is not None:
            dd.append(tq_prev)
        tq_prev = emit_mx_quantize_tile(
            pr, xn_tm(tc), xn_tm(tc), vscr, u_buf, q_buf, m1, m2, scales,
            rinv, dd, a_buf=a_buf)
        q_toks.append(tq_prev)

    def wq_tile(oc):
        return B.arena.ap()[:, (20 + oc // 2) * T + (oc % 2) * 1024:
                            (20 + oc // 2) * T + (oc % 2 + 1) * 1024]

    t_wld = []
    for oc in range(8):
        t_wld.append(p.dma("gpsimd", wq_tile(oc),
                           wdram[oc * 128:(oc + 1) * 128, :],
                           deps=[t_ln[4 + oc // 2]], sem=f"wq{oc}"))
    for oc in range(8):
        dd = [t_wld[oc], tq_prev]
        tq_prev = emit_mx_quantize_tile(
            pr, wq_tile(oc), wq_tile(oc), vscr, u_buf, q_buf, m1, m2, scales,
            rinv, dd, a_buf=a_buf)
        q_toks.append(tq_prev)

    def wdT_tile(mb, kc):
        off = 16 * T + (mb * 8 + kc) * 128
        return B.arena.ap()[:, off:off + 128]

    trw = []
    prev = None
    for mb in range(8):
        for kc in range(8):
            dd = [q_toks[16 + mb]]
            if prev is not None:
                dd.append(prev)
            tt = p.do("tensor", lambda t, mb=mb, kc=kc: t.transpose(
                B.psT[0:128, kc * 128:(kc + 1) * 128],
                wq_tile(mb)[:, kc * 128:(kc + 1) * 128],
                B.ident_bf.ap()[:, :]), deps=dd)
        tok = p.do("vector", lambda v, mb=mb: v.tensor_copy(
            out=B.arena.ap()[:, 16 * T + mb * 1024:16 * T + (mb + 1) * 1024],
            in_=B.psT[0:128, 0:1024]), deps=[tt])
        prev = tok
        trw.append(tok)

    xd_done = [[] for _ in range(8)]
    prev = trw[-1]
    for tc in range(16):
        tts = []
        for c in range(8):
            dd = [q_toks[tc]]
            if prev is not None:
                dd.append(prev)
            tts.append(p.do("tensor", lambda t, tc=tc, c=c: t.transpose(
                B.psT[0:128, c * 128:(c + 1) * 128],
                xn_tm(tc)[:, c * 128:(c + 1) * 128],
                B.ident_bf.ap()[:, :]), deps=dd))
        ev = p.do("vector", lambda v, tc=tc: v.tensor_copy(
            out=B.stg.ap()[:, 0:1024], in_=B.psT[0:128, 0:1024]),
            deps=[tts[-1], prev])
        for c in range(8):
            tok = p.do("vector", lambda v, tc=tc, c=c: v.tensor_copy(
                out=B.slot(c)[:, tc * 128:(tc + 1) * 128],
                in_=B.stg.ap()[:, c * 128:(c + 1) * 128]), deps=[ev])
            xd_done[c].append(tok)
        prev = tok

    class WS:
        def __init__(self):
            self.load_tok = {}

        def load(self, mb, deps=()):
            self.load_tok[mb] = None
            return None

        def tile(self, mb, kc):
            return wdT_tile(mb, kc)

        def used(self, mb, tok):
            pass

    ws = WS()
    delta = [B.slot(24 + c) for c in range(8)]
    dtoks = {}

    def rhs_xd(kc, a, b_):
        return B.slot(kc)[:, a:b_]

    def evac_o(mb, n0, ps, dep):
        tok = p.do("vector", lambda v, ps=ps, mb=mb, n0=n0: v.tensor_scalar(
            out=delta[mb][:, n0:n0 + 1024], in0=ps[:, 0:1024],
            scalar1=B.cols.ap()[:, mb:mb + 1], scalar2=None, op0=ALU.add),
            deps=[dep, t_b])
        dtoks.setdefault(mb, []).append(tok)
        return tok

    def mm_deps(mb, kc, n0):
        return xd_done[kc] + [trw[mb]]

    ws.load_tok = {mb: trw[mb] for mb in range(8)}
    ev_o = mm_evac_stream(pr, Mb=8, Kc=8, ws=ws, rhs=rhs_xd, evac=evac_o,
                          extra_mm_deps=mm_deps, region_init=t_ln, prefetch=0)

    def delta_src(c):
        return (delta[c], dtoks[c])

    st = emit_residual_add(pr, x_in, x_out, delta_src, 0.5, lambda c: [],
                           first_deps=[q_toks[-1]])
    return st


def host_pack_l1_ffn(ins):
    return {
        "L1f_w": np.ascontiguousarray(ins["L1_ffn_weight"].astype(np.float32)),
        "L1f_bcols": pack_cols(ins["L1_ffn_bias"].astype(np.float32)),
    }


def build_l3_ffn_program():
    pr = Prog()
    pr.load_consts()
    x_in = pr.din("x_cur", [D, T])
    x_out = pr.dout("x_next", [D, T])
    st = emit_l3_ffn(pr, x_in, x_out)
    pr.out_tokens = st
    pr.finish()
    return pr


def run_sublayer_l3ffn(x_full, ins):
    pr = build_l3_ffn_program()
    packed = host_pack_l3_ffn(ins)
    consts = host_consts()
    in_maps = []
    for core in range(8):
        m = dict(consts)
        m.update(packed)
        m["x_cur"] = np.ascontiguousarray(x_full[core].T)
        in_maps.append(m)
    res = run_bass_kernel_spmd(pr.nc, in_maps, core_ids=list(range(8)))
    out = np.stack([res.results[c]["x_next"].T for c in range(8)])
    pr.close()
    return out


def build_l1_ffn_program():
    pr = Prog()
    pr.load_consts()
    x_in = pr.din("x_cur", [D, T])
    x_out = pr.dout("x_next", [D, T])
    st = emit_l1_ffn(pr, x_in, x_out)
    pr.out_tokens = st
    pr.finish()
    return pr


def run_sublayer_l1ffn(x_full, ins):
    pr = build_l1_ffn_program()
    packed = host_pack_l1_ffn(ins)
    consts = host_consts()
    in_maps = []
    for core in range(8):
        m = dict(consts)
        m.update(packed)
        m["x_cur"] = np.ascontiguousarray(x_full[core].T)
        in_maps.append(m)
    res = run_bass_kernel_spmd(pr.nc, in_maps, core_ids=list(range(8)))
    out = np.stack([res.results[c]["x_next"].T for c in range(8)])
    pr.close()
    return out




def emit_l0_ffn(pr, x_in, x_out, lname="L0f"):
    B, p = pr.B, pr.p
    E, HD, I = 16, 128, 512
    NSUB = 16384
    NC = 128
    wm = pr.din(lname + "_wmp", [128, 8 * 8 * 128])
    wo = pr.din(lname + "_wop", [128, 8 * 8 * 128])
    w1 = pr.din(lname + "_w1", [128, E * I])
    w2 = pr.din(lname + "_w2", [128, E * 512])
    emb_d = pr.din(lname + "_emb", [128, E])
    b2_d = pr.din(lname + "_b2", [16, 128])
    colsd = pr.din(lname + "_cols", [128, 80])

    if not hasattr(pr, "_l0_small"):
        st = pr.stack.enter_context
        pr._l0_small = (
            st(pr.nc.sbuf_tensor(lname + "_embsb", [128, E], BF16)),
            st(pr.nc.sbuf_tensor(lname + "_b2sb", [16, 128], BF16)),
        )
    embsb, b2sb = pr._l0_small
    w1sb_v = B.arena.ap()[:, 16 * T:16 * T + E * I]
    w2sb_v = B.arena.ap()[:, 20 * T:20 * T + E * 512]

    t_cols = p.dma("sync", B.cols.ap()[:, 0:80], colsd[:, :], deps=pr.const_toks,
                   sem="col")
    t_emb = p.dma("gpsimd", embsb.ap()[:, :], emb_d[:, :], sem="l0emb")
    t_b2 = p.dma("gpsimd", b2sb.ap()[:, :], b2_d[:, :], sem="l0b2")

    t_ln, _ = emit_ln(pr, x_in, xn_slot=0, xb_slot=8, x2_slot=16)
    t_w1 = p.dma("gpsimd", w1sb_v, w1[:, :], sem="l0w1",
                 deps=[t_ln[0], t_ln[1], t_ln[2], t_ln[3]])
    t_w2 = p.dma("gpsimd", w2sb_v, w2[:, :], sem="l0w2",
                 deps=[t_ln[4], t_ln[5], t_ln[6], t_ln[7]])
    xn = [B.slot(c) for c in range(8)]
    hsl = [B.slot(8 + c) for c in range(8)]

    ws_m = WStream(pr, wm, 8)
    h_toks = {}

    def rhs_xn(kc, a, b_):
        return xn[kc][:, a:b_]

    def evac_h(mb, n0, ps, dep):
        tok = p.do("vector", lambda v, ps=ps, mb=mb, n0=n0: v.tensor_scalar(
            out=hsl[mb][:, n0:n0 + 1024], in0=ps[:, 0:1024],
            scalar1=B.cols.ap()[:, mb:mb + 1], scalar2=None, op0=ALU.add),
            deps=[dep, t_cols])
        h_toks.setdefault(mb, []).append(tok)
        return tok

    ev_h = mm_evac_stream(pr, Mb=8, Kc=8, ws=ws_m, rhs=rhs_xn, evac=evac_h,
                          deps_rhs=t_ln, region_init=t_ln)

    def tsub(nc_):
        return hsl[nc_ // 16][:, (nc_ % 16) * 128:(nc_ % 16 + 1) * 128]

    gf = B.slot_f32(24, 1)
    u_b = B.tmpf.ap()[:, 0:512]
    u2_b = B.tmpf.ap()[:, 512:1024]
    m_b = B.tmpf.ap()[:, 1024:1536]
    s_b = B.tmpf.ap()[:, 1536:2048]
    mx = B.cols.ap()[:, 80:112]
    Ssum = B.cols.ap()[:, 112:144]
    umx = B.cols.ap()[:, 144:176]
    umx2 = B.cols.ap()[:, 176:208]
    rS = B.cols.ap()[:, 208:240]

    def r3(ap):
        return ap.rearrange("p (b j) -> p b j", b=32)

    def bc(colap):
        return colap.rearrange("p b -> p b ()").broadcast_to([128, 32, 16])

    g_done = []
    prev_gate = None
    for bank in range(4):
        ps = B.ps6.ap()[:, :]
        mms = []
        for j in range(32):
            nc_ = bank * 32 + j
            dd = [h_toks[nc_ // 16][(nc_ % 16) // 8], t_emb]
            if prev_gate is not None and j == 0:
                dd.append(prev_gate)
            mms.append(p.do("tensor", lambda t, nc_=nc_, j=j, ps=ps: t.matmul(
                ps[:, j * 16:(j + 1) * 16], tsub(nc_), embsb.ap()[:, :],
                start=True, stop=True), deps=dd))
        t0 = p.do("vector", lambda v, ps=ps: v.tensor_reduce(
            out=mx.rearrange("p (b one) -> p b one", b=32), in_=r3(ps),
            axis=mybir.AxisListType.X, op=ALU.max), deps=[mms[-1]])
        t1 = p.do("vector", lambda v, ps=ps: v.tensor_tensor(
            out=r3(u_b), in0=r3(ps), in1=bc(mx), op=ALU.subtract), deps=[t0])
        t2 = p.do("scalar", lambda s: s.activation(
            out=u_b, in_=u_b, func=AF.Exp), deps=[t1])
        t3 = p.do("vector", lambda v: v.tensor_reduce(
            out=Ssum.rearrange("p (b one) -> p b one", b=32), in_=r3(u_b),
            axis=mybir.AxisListType.X, op=ALU.add), deps=[t2])
        t4 = p.do("vector", lambda v: v.tensor_reduce(
            out=umx.rearrange("p (b one) -> p b one", b=32), in_=r3(u_b),
            axis=mybir.AxisListType.X, op=ALU.max), deps=[t2])
        t5 = p.do("vector", lambda v: v.tensor_tensor(
            out=r3(m_b), in0=r3(u_b), in1=bc(umx), op=ALU.is_ge), deps=[t4])
        t6 = p.do("vector", lambda v: v.scalar_tensor_tensor(
            out=u2_b, in0=m_b, scalar=-1e30, in1=u_b, op0=ALU.mult,
            op1=ALU.add), deps=[t5])
        t7 = p.do("vector", lambda v: v.tensor_reduce(
            out=umx2.rearrange("p (b one) -> p b one", b=32), in_=r3(u2_b),
            axis=mybir.AxisListType.X, op=ALU.max), deps=[t6])
        t8 = p.do("vector", lambda v: v.tensor_tensor(
            out=r3(s_b), in0=r3(u2_b), in1=bc(umx2), op=ALU.is_ge), deps=[t7])
        t9 = p.do("vector", lambda v: v.tensor_add(
            out=m_b, in0=m_b, in1=s_b), deps=[t8])
        t10 = p.do("vector", lambda v: v.reciprocal(out=rS, in_=Ssum), deps=[t3])
        t11 = p.do("vector", lambda v: v.tensor_mul(
            out=u_b, in0=u_b, in1=m_b), deps=[t9])
        t12 = p.do("vector", lambda v, bank=bank: v.tensor_tensor(
            out=r3(gf[:, bank * 512:(bank + 1) * 512]), in0=r3(u_b),
            in1=bc(rS), op=ALU.mult), deps=[t10, t11])
        prev_gate = t12
        g_done.append(t12)

    g_fm = B.arena.ap()[0:16, 29 * T:37 * T]
    gfm_toks = []
    prev = None
    for grp in range(8):
        tts = []
        for j in range(16):
            nc_ = grp * 16 + j
            dd = [g_done[nc_ // 32]]
            if prev is not None:
                dd.append(prev)
            tts.append(p.do("tensor", lambda t, nc_=nc_, j=j: t.transpose(
                B.psA[0:16, j * 128:(j + 1) * 128],
                gf[:, nc_ * 16:(nc_ + 1) * 16],
                B.ident_f32.ap()[:, :]), deps=dd))
        tok = p.do("vector", lambda v, grp=grp: v.tensor_copy(
            out=g_fm[:, grp * T:(grp + 1) * T], in_=B.psA[0:16, 0:T]),
            deps=[tts[-1]])
        prev = tok
        gfm_toks.append(tok)

    comb = [B.slot(c) for c in range(8)]
    hm = [B.slot(s) for s in (26, 27, 28, 37)]
    comb_toks = {}
    init_dep = [ev_h[-1], t_b2, gfm_toks[-1]]
    prev_ev = None
    for grp in range(16):
        ps = B.psA[0:128, (grp % 2) * 1024:(grp % 2) * 1024 + 1024]
        tts = []
        for j in range(8):
            nc_ = grp * 8 + j
            dd = init_dep + [gfm_toks[nc_ // 16]]
            if prev_ev is not None and j == 0:
                dd.append(prev_ev)
            tts.append(p.do("tensor", lambda t, nc_=nc_, j=j, ps=ps: t.matmul(
                ps[:, j * 128:(j + 1) * 128],
                g_fm[:, nc_ * 128:(nc_ + 1) * 128], b2sb.ap()[:, :],
                start=True, stop=True), deps=dd))
        tok = p.do("scalar", lambda s, grp=grp, ps=ps: s.activation(
            out=comb[grp // 2][:, (grp % 2) * 1024:(grp % 2) * 1024 + 1024],
            in_=ps[:, 0:1024], func=AF.Copy), deps=[tts[-1]])
        prev_ev = tok
        for j in range(8):
            comb_toks[grp * 8 + j] = tok

    ph_busy = [prev_ev, prev_ev]
    eo_bank_busy = {0: None, 1: None}
    hm_busy = [None] * 4
    stg_busy = [None, None]

    for nblk in range(8):
        for e in range(E):
            hm_toks = []
            for mb in range(4):
                for half in range(2):
                    ps = B.psA[0:128, half * 1024:half * 1024 + 1024]
                    dd = [t_w1, ph_busy[half]]
                    last = None
                    for q in range(2):
                        nc_ = nblk * 16 + half * 8 + (q * 4)
                        a = half * 1024 + q * 512
                        last = p.do("tensor", lambda t, mb=mb, e=e, ps=ps,
                                    nblk=nblk, a=a, q=q: t.matmul(
                            ps[:, (a % 1024):(a % 1024) + 512],
                            w1sb_v[:, e * I + mb * 128:e * I + (mb + 1) * 128],
                            _l0_sub_rhs(B, hsl, nblk, a),
                            start=True, stop=True), deps=dd)
                    tok = p.do("scalar", lambda s, mb=mb, e=e, ps=ps, half=half:
                               s.activation(
                        out=hm[mb][:, half * 1024:half * 1024 + 1024],
                        in_=ps[:, 0:1024], func=AF.Gelu,
                        bias=B.cols.ap()[:, 8 + e * 4 + mb:9 + e * 4 + mb]),
                        deps=[last, t_cols, hm_busy[mb]])
                    ph_busy[half] = tok
                    hm_toks.append(tok)
                hm_busy[mb] = hm_toks[-1]
            for quad in range(4):
                bank = B.ps4.ap() if quad % 2 == 0 else B.ps5.ap()
                bsel = quad % 2
                last = None
                for j in range(4):
                    nc_ = nblk * 16 + quad * 4 + j
                    dd = [hm_toks[-1], t_w2]
                    if eo_bank_busy[bsel] is not None and j == 0:
                        dd.append(eo_bank_busy[bsel])
                    for kc in range(4):
                        last = p.do("tensor", lambda t, nc_=nc_, j=j, kc=kc,
                                    e=e, bank=bank, nblk=nblk, quad=quad: t.matmul(
                            bank[:, j * 128:(j + 1) * 128],
                            hm[kc][:, (quad * 4 + j) * 128:(quad * 4 + j + 1) * 128],
                            w2sb_v[:, e * 512 + kc * 128:e * 512 + (kc + 1) * 128],
                            start=(kc == 0), stop=(kc == 3)),
                            deps=dd if kc == 0 else [t_w2])
                sb = (nblk * 4 + quad) % 2
                tev = p.do("scalar", lambda s, bank=bank, sb=sb: s.activation(
                    out=B.stg.ap()[:, sb * 1024:sb * 1024 + 512],
                    in_=bank[:, 0:512], func=AF.Copy),
                    deps=[last, stg_busy[sb]])
                eo_bank_busy[bsel] = tev
                for j in range(4):
                    nc_ = nblk * 16 + quad * 4 + j
                    tok = p.do("vector", lambda v, nc_=nc_, j=j, e=e, sb=sb:
                               v.scalar_tensor_tensor(
                        out=_comb_sub(B, comb, nc_),
                        in0=B.stg.ap()[:, sb * 1024 + j * 128:sb * 1024 + (j + 1) * 128],
                        scalar=gf[:, nc_ * 16 + e:nc_ * 16 + e + 1],
                        in1=_comb_sub(B, comb, nc_),
                        op0=ALU.mult, op1=ALU.add),
                        deps=[tev, comb_toks[nc_]])
                    comb_toks[nc_] = tok
                stg_busy[sb] = tok

    def cfm(c):
        return B.slot(8 + c)

    prev = None
    cfm_toks = [[] for _ in range(8)]
    for nc_ in range(NC):
        tts = []
        for dummy in range(1):
            pass
        dd = [comb_toks[nc_]]
        if prev is not None:
            dd.append(prev)
        tt = p.do("tensor", lambda t, nc_=nc_: t.transpose(
            B.psT[0:128, (nc_ % 8) * 128:(nc_ % 8 + 1) * 128],
            _comb_sub(B, comb, nc_), B.ident_bf.ap()[:, :]), deps=dd)
        if nc_ % 8 == 7:
            tok = p.do("vector", lambda v, nc_=nc_: v.tensor_copy(
                out=_cfm_dst(B, nc_), in_=B.psT[0:128, 0:1024]), deps=[tt])
            prev = tok
            c = (nc_ // 16)
            cfm_toks[c].append(tok)
            if (nc_ - 7) // 16 != c:
                cfm_toks[(nc_ - 7) // 16].append(tok)

    ws_o = WStream(pr, wo, 8)
    delta = [B.slot(s) for s in (16, 17, 18, 19, 20, 21, 22, 23)]
    dtoks = {}

    def rhs_cfm(kc, a, b_):
        return cfm(kc)[:, a:b_]

    def evac_o(mb, n0, ps, dep):
        tok = p.do("vector", lambda v, ps=ps, mb=mb, n0=n0: v.tensor_scalar(
            out=delta[mb][:, n0:n0 + 1024], in0=ps[:, 0:1024],
            scalar1=B.cols.ap()[:, 72 + mb:73 + mb], scalar2=None, op0=ALU.add),
            deps=[dep, t_cols])
        dtoks.setdefault(mb, []).append(tok)
        return tok

    def mm_deps_o(mb, kc, n0):
        return cfm_toks[kc] + [t_w1, t_w2]

    ev_o = mm_evac_stream(pr, Mb=8, Kc=8, ws=ws_o, rhs=rhs_cfm, evac=evac_o,
                          extra_mm_deps=mm_deps_o,
                          region_init=[ph_busy[0], ph_busy[1]])

    def delta_src(c):
        return (delta[c], dtoks[c])

    st = emit_residual_add(pr, x_in, x_out, delta_src, 0.5, lambda c: [],
                           first_deps=[ev_o[0]])
    return st


def _l0_sub_rhs(B, hsl, nblk, a):
    base = 8 * T + nblk * 2048 + a
    return B.arena.ap()[:, base:base + 512]


def _comb_sub(B, comb, nc_):
    return B.arena.ap()[:, nc_ * 128:nc_ * 128 + 128]


def _cfm_dst(B, nc_):
    first = nc_ - 7
    c = first // 16
    off = (first % 16) * 128
    return B.slot(8 + c)[:, off:off + 1024]


def host_pack_l0_ffn(ins):
    E, HD, I = 16, 128, 512
    w1 = ins["L0_ffn_W1"].astype(np.float32)
    w2 = ins["L0_ffn_W2"].astype(np.float32)
    w1p = np.ascontiguousarray(w1.transpose(1, 0, 2).reshape(128, E * I))
    w2p = np.ascontiguousarray(
        w2.reshape(E, 4, 128, 128).transpose(2, 0, 1, 3).reshape(128, E * 512))
    cols = np.zeros((128, 80), np.float32)
    cols[:, 0:8] = pack_cols(ins["L0_ffn_bm"].astype(np.float32))
    cols[:, 8:72] = np.concatenate(
        [pack_cols(ins["L0_ffn_b1"][e].astype(np.float32)) for e in range(E)], 1)
    cols[:, 72:80] = pack_cols(ins["L0_ffn_bo"].astype(np.float32))
    return {
        "L0f_wmp": pack_lhsT(ins["L0_ffn_Wm"].astype(np.float32)),
        "L0f_wop": pack_lhsT(ins["L0_ffn_Wo"].astype(np.float32)),
        "L0f_w1": w1p,
        "L0f_w2": w2p,
        "L0f_emb": np.ascontiguousarray(ins["L0_ffn_expert_emb"].astype(np.float32)),
        "L0f_b2": np.ascontiguousarray(ins["L0_ffn_b2"].astype(np.float32)),
        "L0f_cols": cols,
    }


def build_l0_ffn_program():
    pr = Prog()
    pr.load_consts()
    x_in = pr.din("x_cur", [D, T])
    x_out = pr.dout("x_next", [D, T])
    st = emit_l0_ffn(pr, x_in, x_out)
    pr.out_tokens = st
    pr.finish()
    return pr


def run_sublayer_l0ffn(x_full, ins):
    pr = build_l0_ffn_program()
    packed = host_pack_l0_ffn(ins)
    consts = host_consts()
    in_maps = []
    for core in range(8):
        m = dict(consts)
        m.update(packed)
        m["x_cur"] = np.ascontiguousarray(x_full[core].T)
        in_maps.append(m)
    res = run_bass_kernel_spmd(pr.nc, in_maps, core_ids=list(range(8)))
    out = np.stack([res.results[c]["x_next"].T for c in range(8)])
    pr.close()
    return out




def emit_l1_attn(pr, x_in, x_out, lname="L1a"):
    B, p = pr.B, pr.p
    H, HD = 16, 64
    inw = pr.din(lname + "_inwp", [128, 24 * 8 * 128])
    outw = pr.din(lname + "_outwp", [128, 8 * 8 * 128])
    colsd = pr.din(lname + "_cols", [128, 32])
    vecs = pr.din(lname + "_vecs", [16, 2])
    srow = pr.din(lname + "_sloperow", [1, 16])
    selm = pr.din(lname + "_selmask", [16, 128])
    selr = pr.din(lname + "_selrhs", [16, 16])
    iota = pr.din(lname + "_iota", [128, T])
    caus = pr.din(lname + "_caus", [128, 128])

    if not hasattr(pr, "_l1a_small"):
        st = pr.stack.enter_context
        pr._l1a_small = (
            st(pr.nc.sbuf_tensor(lname + "_vec_sb", [16, 2], F32)),
            st(pr.nc.sbuf_tensor(lname + "_srow_sb", [1, 16], F32)),
            st(pr.nc.sbuf_tensor(lname + "_selm_sb", [16, 128], F32)),
            st(pr.nc.sbuf_tensor(lname + "_selr_sb", [16, 16], F32)),
            st(pr.nc.sbuf_tensor(lname + "_caus_sb", [128, 128], F32)),
        )
    vec_sb, srow_sb, selm_sb, selr_sb, caus_sb = pr._l1a_small

    t_cols = p.dma("sync", B.cols.ap()[:, 0:32], colsd[:, :], deps=pr.const_toks,
                   sem="col")
    t_v1 = p.dma("sync", vec_sb.ap()[:, :], vecs[:, :], sem="l1v1")
    t_v2 = p.dma("sync", srow_sb.ap()[:, :], srow[:, :], sem="l1v2")
    t_v3 = p.dma("sync", selm_sb.ap()[:, :], selm[:, :], sem="l1v3")
    t_v4 = p.dma("sync", selr_sb.ap()[:, :], selr[:, :], sem="l1v4")
    t_v5 = p.dma("sync", caus_sb.ap()[:, :], caus[:, :], sem="l1v5")
    tv = [t_v1, t_v2, t_v3, t_v4, t_v5]

    def gc(c):
        return B.cols.ap()[:, c:c + 1]

    def bc_(c):
        return B.cols.ap()[:, 8 + c:9 + c]

    t_ln, _ = emit_ln(pr, x_in, xn_slot=0, xb_slot=8, x2_slot=16,
                      gcol=gc, bcol=bc_, var_scale=1.0 + 1e-5,
                      var_bias=1e-5 + 1e-10, cols_dep=[t_cols])
    xn = [B.slot(c) for c in range(8)]

    t_sig = p.do("scalar", lambda s: s.activation(
        out=vec_sb.ap()[0:16, 1:2], in_=vec_sb.ap()[0:16, 1:2],
        func=AF.Sigmoid), deps=tv)
    t_sd = p.do("vector", lambda v: v.tensor_scalar(
        out=selm_sb.ap()[:, :], in0=selm_sb.ap()[:, :],
        scalar1=vec_sb.ap()[0:16, 1:2], scalar2=None, op0=ALU.mult),
        deps=[t_sig, t_v3])
    t_mm1 = p.do("tensor", lambda t: t.matmul(
        B.ps6[0:128, 0:16], selm_sb.ap()[:, :], selr_sb.ap()[:, :],
        start=True, stop=True), deps=[t_sd, t_v4] + t_ln)
    t_sc = p.do("vector", lambda v: v.tensor_copy(
        out=B.cols.ap()[:, 32:48], in_=B.ps6[0:128, 0:16]), deps=[t_mm1])
    t_1ms = p.do("vector", lambda v: v.tensor_scalar(
        out=B.cols.ap()[:, 48:64], in0=B.cols.ap()[:, 32:48],
        scalar1=-1.0, scalar2=1.0, op0=ALU.mult, op1=ALU.add), deps=[t_sc])
    t_mm2 = p.do("tensor", lambda t: t.matmul(
        B.ps6[0:128, 16:32], B.ones_row.ap()[:, :], srow_sb.ap()[:, :],
        start=True, stop=True), deps=[t_v2, t_sc] + list(pr.const_toks))
    t_slc = p.do("vector", lambda v: v.tensor_copy(
        out=B.cols.ap()[:, 64:80], in_=B.ps6[0:128, 16:32]), deps=[t_mm2])

    ws = WStream(pr, inw, 8)
    q_toks, k_toks, v_toks = {}, {}, {}

    def rhs_xn(kc, a, b_):
        return xn[kc][:, a:b_]

    def evac_qkv(mb, n0, ps, dep):
        if mb < 8:
            tok = p.do("vector", lambda v, ps=ps, mb=mb, n0=n0: v.tensor_scalar(
                out=B.slot(8 + mb)[:, n0:n0 + 1024], in0=ps[:, 0:1024],
                scalar1=0.125, scalar2=None, op0=ALU.mult), deps=[dep])
            q_toks.setdefault(mb, []).append(tok)
        elif mb < 16:
            tok = p.do("vector", lambda v, ps=ps, mb=mb, n0=n0: v.tensor_copy(
                out=B.slot(16 + mb - 8)[:, n0:n0 + 1024], in_=ps[:, 0:1024]),
                deps=[dep])
            k_toks.setdefault(mb - 8, []).append(tok)
        else:
            tok = p.do("scalar", lambda s, ps=ps, mb=mb, n0=n0: s.activation(
                out=B.slot(24 + mb - 16)[:, n0:n0 + 1024], in_=ps[:, 0:1024],
                func=AF.Copy), deps=[dep])
            v_toks.setdefault(mb - 16, []).append(tok)
        return tok

    ev_qkv = mm_evac_stream(pr, Mb=24, Kc=8, ws=ws, rhs=rhs_xn, evac=evac_qkv,
                            deps_rhs=t_ln, region_init=t_ln)

    ksm_toks = {}
    prev_sm = None
    for c in range(8):
        ksl = B.slot(16 + c)
        dd = k_toks[c] + [t_sc, t_1ms]
        if prev_sm is not None:
            dd.append(prev_sm)
        ta = p.do("vector", lambda v, ksl=ksl, c=c: v.tensor_scalar(
            out=B.stg.ap()[:, 0:T], in0=ksl, scalar1=B.cols.ap()[:, 48 + c:49 + c],
            scalar2=None, op0=ALU.mult), deps=dd)
        tb = p.do("vector", lambda v, ksl=ksl, c=c: v.tensor_scalar(
            out=B.tmpf.ap()[:, 0:T].bitcast(BF16)[:, 0:T], in0=ksl,
            scalar1=B.cols.ap()[:, 32 + c:33 + c],
            scalar2=None, op0=ALU.mult), deps=dd)
        t0 = p.do("vector", lambda v, ksl=ksl: v.tensor_copy(
            out=ksl[:, 0:1], in_=B.stg.ap()[:, 0:1]), deps=[ta, tb])
        t1 = p.do("vector", lambda v, ksl=ksl: v.tensor_add(
            out=ksl[:, 1:T], in0=B.stg.ap()[:, 1:T],
            in1=B.tmpf.ap()[:, 0:T].bitcast(BF16)[:, 0:T - 1]), deps=[t0])
        ksm_toks[c] = t1
        prev_sm = t1

    def vtm(h, tkc):
        off = (32 + tkc // 2) * T + (tkc % 2) * 1024 + h * 64
        return B.arena.ap()[:, off:off + 64]

    vt_toks = {}
    prev_t = None
    for h in range(H):
        for tkc in range(16):
            dd = v_toks[(2 * h) // 2 if False else h // 2] + [prev_t]
            tt = p.do("tensor", lambda t, h=h, tkc=tkc: t.transpose(
                B.psT[0:128, (tkc % 8) * 128:(tkc % 8) * 128 + 64],
                B.slot(24 + h // 2)[(h % 2) * 64:(h % 2) * 64 + 64,
                                    tkc * 128:(tkc + 1) * 128],
                B.ident64.ap()[(h % 2) * 64:(h % 2) * 64 + 64, :]),
                deps=[d for d in dd if d])
            if tkc % 8 == 7:
                toks = []
                for j in range(8):
                    tkc0 = tkc - 7 + j
                    toks.append(p.do("vector", lambda v, h=h, tkc0=tkc0, j=j:
                                     v.tensor_copy(
                        out=vtm(h, tkc0), in_=B.psT[0:128, j * 128:j * 128 + 64]),
                        deps=[tt]))
                prev_t = toks[-1]
                vt_toks.setdefault(h, []).append(toks[-1])

    t_iota = p.dma("sync", B.tmpf.ap()[:, T:2 * T], iota[:, :],
                   deps=[prev_sm], sem="l1io")

    Sf = B.tmpf.ap()[:, 0:T]
    y_toks = {}
    psA_busy = None
    psT_busy = prev_t
    pt_busy = [None, None]
    av_busy = [None, None]
    it = 0
    units1 = [(h, tc) for h in range(H) for tc in range(16)]
    states1 = {}

    def a1_stageA(i):
        nonlocal psA_busy
        h, tc = units1[i]
        qsl = B.slot(8 + h // 2)
        W = (tc + 1) * 128
        sb = i % 2
        dd = q_toks[h // 2] + [ksm_toks[h // 2]]
        if psA_busy is not None:
            dd.append(psA_busy)
        last = None
        for n0 in range(0, W, 512):
            nn = min(512, W - n0)
            last = p.do("tensor", lambda t, h=h, tc=tc, n0=n0, nn=nn,
                        qsl=qsl: t.matmul(
                B.psA[0:128, n0:n0 + nn],
                qsl[(h % 2) * 64:(h % 2) * 64 + 64, tc * 128:(tc + 1) * 128],
                B.slot(16 + h // 2)[(h % 2) * 64:(h % 2) * 64 + 64, n0:n0 + nn],
                start=True, stop=True), deps=dd)
        t_ai = p.do("vector", lambda v, h=h, W=W: v.scalar_tensor_tensor(
            out=B.psA[0:128, 0:W], in0=B.tmpf.ap()[:, T:T + W],
            scalar=B.cols.ap()[:, 64 + h:65 + h], in1=B.psA[0:128, 0:W],
            op0=ALU.mult, op1=ALU.add), deps=[last, t_iota, t_slc])
        t_cz = p.do("vector", lambda v, tc=tc: v.tensor_add(
            out=B.psA[0:128, tc * 128:tc * 128 + 128],
            in0=B.psA[0:128, tc * 128:tc * 128 + 128], in1=caus_sb.ap()[:, :]),
            deps=[t_ai, t_v5])
        t_mx = p.do("vector", lambda v, W=W: v.tensor_reduce(
            out=B.cols.ap()[:, 88:89].rearrange("p one -> p one ()"),
            in_=B.psA[0:128, 0:W].rearrange("p w -> p () w"),
            axis=mybir.AxisListType.X, op=ALU.max, negate=True),
            deps=[t_cz])
        t_ex = p.do("scalar", lambda s, W=W, sb=sb: s.activation(
            out=B.slot(26 + sb)[:, 0:W],
            in_=B.psA[0:128, 0:W], func=AF.Exp, bias=B.cols.ap()[:, 88:89],
            accum_out=B.cols.ap()[:, 89:90]), deps=[t_mx, pt_busy[sb]])
        psA_busy = t_ex
        t_rd = p.do("vector", lambda v: v.reciprocal(
            out=B.cols.ap()[:, 90:91], in_=B.cols.ap()[:, 89:90]),
            deps=[t_ex])
        t_pn = p.do("vector", lambda v, W=W, sb=sb: v.tensor_scalar(
            out=B.slot(26 + sb)[:, 0:W], in0=B.slot(26 + sb)[:, 0:W],
            scalar1=B.cols.ap()[:, 90:91], scalar2=None, op0=ALU.mult),
            deps=[t_rd])
        states1[i] = dict(h=h, tc=tc, sb=sb, t_pn=t_pn)

    def a1_stageC(i):
        nonlocal psT_busy
        st_ = states1[i]
        h, tc, sb, t_pn = st_["h"], st_["tc"], st_["sb"], st_["t_pn"]
        ptsl = 24 + sb
        tts = []
        tev = None
        for j in range(tc + 1):
            dd2 = [t_pn]
            if j == 0:
                dd2.append(psT_busy)
                if pt_busy[sb] is not None:
                    dd2.append(pt_busy[sb])
            if j % 8 == 0 and tev is not None:
                dd2.append(tev)
            tts.append(p.do("tensor", lambda t, j=j, sb=sb: t.transpose(
                B.psT[0:128, (j % 8) * 128:(j % 8 + 1) * 128],
                B.slot(26 + sb)[:, j * 128:(j + 1) * 128],
                B.ident_bf.ap()[:, :]), deps=dd2))
            if j % 8 == 7 or j == tc:
                lo = (j // 8) * 8
                tev = p.do("vector", lambda v, lo=lo, j=j, ptsl=ptsl: v.tensor_copy(
                    out=B.slot(ptsl)[:, lo * 128:(j + 1) * 128],
                    in_=B.psT[0:128, 0:((j % 8) + 1) * 128]),
                    deps=[tts[-1]])
        psT_busy = tev
        pt_busy[sb] = tev
        st_["tev"] = tev

    def a1_stageD(i):
        st_ = states1[i]
        h, tc, sb, tev = st_["h"], st_["tc"], st_["sb"], st_["tev"]
        avb = B.ps4.ap() if sb == 0 else B.ps5.ap()
        dd3 = [tev, vt_toks[h][0], vt_toks[h][1]]
        if av_busy[sb] is not None:
            dd3.append(av_busy[sb])
        lastav = None
        for j in range(tc + 1):
            lastav = p.do("tensor", lambda t, h=h, j=j, avb=avb, sb=sb,
                          tc=tc: t.matmul(
                avb[0:64, 0:128], vtm(h, j),
                B.slot(24 + sb)[:, j * 128:(j + 1) * 128],
                start=(j == 0), stop=(j == tc)),
                deps=dd3 if j == 0 else [])
        t_ye = p.do("scalar", lambda s, h=h, tc=tc, avb=avb: s.activation(
            out=B.slot(h // 2)[(h % 2) * 64:(h % 2) * 64 + 64,
                               tc * 128:(tc + 1) * 128],
            in_=avb[0:64, 0:128], func=AF.Copy),
            deps=[lastav, ev_qkv[-1]])
        av_busy[sb] = t_ye
        y_toks.setdefault(h // 2, []).append(t_ye)
        del states1[i]

    NU1 = len(units1)
    for i in range(NU1 + 1):
        if i < NU1:
            a1_stageA(i)
        if i >= 1:
            a1_stageC(i - 1)
            a1_stageD(i - 1)
    it += NU1

    ws_o = WStream(pr, outw, 8)
    o_toks = {}

    def rhs_y(kc, a, b_):
        return B.slot(kc)[:, a:b_]

    def evac_o(mb, n0, ps, dep):
        tok = p.do("scalar", lambda s, ps=ps, mb=mb, n0=n0: s.activation(
            out=B.slot(8 + mb)[:, n0:n0 + 1024], in_=ps[:, 0:1024],
            func=AF.Copy), deps=[dep])
        o_toks.setdefault(mb, []).append(tok)
        return tok

    def mm_deps_o(mb, kc, n0):
        return y_toks[kc]

    ev_o = mm_evac_stream(pr, Mb=8, Kc=8, ws=ws_o, rhs=rhs_y, evac=evac_o,
                          extra_mm_deps=mm_deps_o, region_init=[psA_busy])

    def gc2(c):
        return B.cols.ap()[:, 16 + c:17 + c]

    def bc2(c):
        return B.cols.ap()[:, 24 + c:25 + c]

    o_sl = [B.slot(8 + c) for c in range(8)]
    t_ln2, _ = emit_ln(pr, None, xn_slot=16, xb_slot=None, x2_slot=24,
                       gcol=gc2, bcol=bc2, src_sbuf=o_sl,
                       src_deps=lambda c: ev_o[-1], cols_dep=[t_cols])
    vout = [B.slot(16 + c) for c in range(8)]

    def delta_src(c):
        return (vout[c], [t_ln2[c]])

    st = emit_residual_add(pr, x_in, x_out, delta_src, 1.0, lambda c: [],
                           first_deps=[t_ln2[-1]])
    return st


def host_pack_l1_attn(ins):
    H = 16
    inw = ins["L1_attn_in_w"].astype(np.float32)
    outw = ins["L1_attn_out_w"].astype(np.float32)
    cols = np.zeros((128, 32), np.float32)
    cols[:, 0:8] = pack_cols(ins["L1_attn_ln1_g"].astype(np.float32))
    cols[:, 8:16] = pack_cols(ins["L1_attn_ln1_b"].astype(np.float32))
    cols[:, 16:24] = pack_cols(ins["L1_attn_ln2_g"].astype(np.float32))
    cols[:, 24:32] = pack_cols(ins["L1_attn_ln2_b"].astype(np.float32))
    vecs = np.zeros((16, 2), np.float32)
    vecs[:, 0] = ins["L1_attn_slopes"]
    vecs[:, 1] = ins["L1_attn_smear"]
    selm = np.zeros((16, 128), np.float32)
    for k in range(16):
        selm[k, (k % 2) * 64:(k % 2) * 64 + 64] = 1.0
    selr = np.zeros((16, 16), np.float32)
    for k in range(16):
        selr[k, k // 2] = 1.0
    iota = np.tile(np.arange(T, dtype=np.float32)[None, :], (128, 1))
    caus = np.triu(np.full((128, 128), -1e9, np.float32), 1)
    return {
        "L1a_inwp": pack_lhsT(np.ascontiguousarray(inw.T)),
        "L1a_outwp": pack_lhsT(np.ascontiguousarray(outw.T)),
        "L1a_cols": cols,
        "L1a_vecs": vecs,
        "L1a_sloperow": np.ascontiguousarray(ins["L1_attn_slopes"].astype(np.float32)[None, :]),
        "L1a_selmask": selm,
        "L1a_selrhs": selr,
        "L1a_iota": iota,
        "L1a_caus": caus,
    }


def build_l1_attn_program():
    pr = Prog()
    pr.load_consts()
    x_in = pr.din("x_cur", [D, T])
    x_out = pr.dout("x_next", [D, T])
    st = emit_l1_attn(pr, x_in, x_out)
    pr.out_tokens = st
    pr.finish()
    return pr


def run_sublayer_l1attn(x_full, ins):
    pr = build_l1_attn_program()
    packed = host_pack_l1_attn(ins)
    consts = host_consts()
    in_maps = []
    for core in range(8):
        m = dict(consts)
        m.update(packed)
        m["x_cur"] = np.ascontiguousarray(x_full[core].T)
        in_maps.append(m)
    res = run_bass_kernel_spmd(pr.nc, in_maps, core_ids=list(range(8)))
    out = np.stack([res.results[c]["x_next"].T for c in range(8)])
    pr.close()
    return out




def emit_l0_attn(pr, x_in, x_out, lname="L0a"):
    B, p = pr.B, pr.p
    wq = pr.din(lname + "_wqp", [128, 8 * 8 * 128])
    wkv = pr.din(lname + "_wkvp", [128, 1 * 8 * 128])
    wup = pr.din(lname + "_wup", [64, 256])
    wo = pr.din(lname + "_wop", [128, 8 * 8 * 128])
    caus = pr.din("L1a_caus", [128, 128])

    if not hasattr(pr, "_l0a_small"):
        st = pr.stack.enter_context
        pr._l0a_small = (
            st(pr.nc.sbuf_tensor(lname + "_wup_sb", [64, 256], BF16)),
            st(pr.nc.sbuf_tensor(lname + "_caus_sb", [128, 128], F32)),
        )
    wup_sb, caus_sb = pr._l0a_small
    t_wu = p.dma("gpsimd", wup_sb.ap()[:, :], wup[:, :], sem="l0wu")
    t_cz = p.dma("sync", caus_sb.ap()[:, :], caus[:, :], sem="l0cz")

    t_ln, _ = emit_ln(pr, x_in, xn_slot=0, xb_slot=8, x2_slot=16)
    xn = [B.slot(c) for c in range(8)]

    ws_q = WStream(pr, wq, 8)
    q_toks = {}
    QSC = float(128 ** -0.5)

    def rhs_xn(kc, a, b_):
        return xn[kc][:, a:b_]

    def evac_q(mb, n0, ps, dep):
        tok = p.do("vector", lambda v, ps=ps, mb=mb, n0=n0: v.tensor_scalar(
            out=B.slot(8 + mb)[:, n0:n0 + 1024], in0=ps[:, 0:1024],
            scalar1=QSC, scalar2=None, op0=ALU.mult), deps=[dep])
        q_toks.setdefault(mb, []).append(tok)
        return tok

    ev_q = mm_evac_stream(pr, Mb=8, Kc=8, ws=ws_q, rhs=rhs_xn, evac=evac_q,
                          deps_rhs=t_ln, region_init=t_ln)

    ws_kv = WStream(pr, wkv, 8)
    kv_toks = []

    def evac_kv(mb, n0, ps, dep):
        p.do("scalar", lambda s, ps=ps, n0=n0: s.activation(
            out=B.slot(22)[0:64, n0:n0 + 1024], in_=ps[0:64, 0:1024],
            func=AF.Copy), deps=[dep])
        tok = p.do("scalar", lambda s, ps=ps, n0=n0: s.activation(
            out=B.slot(23)[0:64, n0:n0 + 1024], in_=ps[64:128, 0:1024],
            func=AF.Copy), deps=[dep])
        kv_toks.append(tok)
        return tok

    ev_kv = mm_evac_stream(pr, Mb=1, Kc=8, ws=ws_kv, rhs=rhs_xn, evac=evac_kv,
                           region_init=[ev_q[-1]])

    kf_toks, vf_toks = {}, {}
    prev = ev_kv[-1]
    for kv in range(2):
        for what in range(2):
            dst = B.slot(16 + 2 * what + kv)
            for n0 in range(0, T, 512):
                tt = p.do("tensor", lambda t, kv=kv, what=what, n0=n0: t.matmul(
                    B.ps6[0:128, 0:512],
                    wup_sb.ap()[kv * 32:kv * 32 + 32,
                                what * 128:what * 128 + 128],
                    B.slot(22 + what)[kv * 32:kv * 32 + 32, n0:n0 + 512],
                    start=True, stop=True),
                    deps=[t_wu, kv_toks[-1], prev])
                tok = p.do("vector", lambda v, dst=dst, n0=n0: v.tensor_copy(
                    out=dst[:, n0:n0 + 512], in_=B.ps6[0:128, 0:512]),
                    deps=[tt])
                prev = tok
                (kf_toks if what == 0 else vf_toks).setdefault(kv, []).append(tok)

    vt_toks = {}
    prev_t = None
    for kv in range(2):
        for tkc in range(16):
            dd = vf_toks[kv] + ([prev_t] if prev_t else [])
            tt = p.do("tensor", lambda t, kv=kv, tkc=tkc: t.transpose(
                B.psT[0:128, (tkc % 8) * 128:(tkc % 8 + 1) * 128],
                B.slot(18 + kv)[:, tkc * 128:(tkc + 1) * 128],
                B.ident_bf.ap()[:, :]), deps=dd)
            if tkc % 8 == 7:
                tok = p.do("vector", lambda v, kv=kv, tkc=tkc: v.tensor_copy(
                    out=B.slot(20 + kv)[:, (tkc - 7) * 128:(tkc + 1) * 128],
                    in_=B.psT[0:128, 0:1024]), deps=[tt])
                prev_t = tok
                vt_toks.setdefault(kv, []).append(tok)

    y_toks = {}
    psA_busy = ev_q[-1]
    psT_busy = prev_t
    pt_busy = [None, None]
    av_busy = [None, None]
    it = 0
    units0 = [(c, tc) for c in range(8) for tc in range(16)]
    states0 = {}

    def a0_stageA(i):
        c, tc = units0[i]
        kv = c // 4
        W = (tc + 1) * 128
        nonlocal psA_busy
        dd = q_toks[c] + kf_toks[kv] + [psA_busy]
        last = None
        for n0 in range(0, W, 512):
            nn = min(512, W - n0)
            last = p.do("tensor", lambda t, c=c, kv=kv, tc=tc, n0=n0,
                        nn=nn: t.matmul(
                B.psA[0:128, n0:n0 + nn],
                B.slot(8 + c)[:, tc * 128:(tc + 1) * 128],
                B.slot(16 + kv)[:, n0:n0 + nn],
                start=True, stop=True), deps=dd)
        t_cm = p.do("vector", lambda v, tc=tc: v.tensor_add(
            out=B.psA[0:128, tc * 128:tc * 128 + 128],
            in0=B.psA[0:128, tc * 128:tc * 128 + 128],
            in1=caus_sb.ap()[:, :]), deps=[last, t_cz])
        t_mx = p.do("vector", lambda v, W=W: v.tensor_reduce(
            out=B.cols.ap()[:, 88:89].rearrange("p one -> p one ()"),
            in_=B.psA[0:128, 0:W].rearrange("p w -> p () w"),
            axis=mybir.AxisListType.X, op=ALU.max, negate=True),
            deps=[t_cm])
        states0[i] = dict(c=c, tc=tc, kv=kv, W=W, t_mx=t_mx)

    def a0_stageB(i):
        nonlocal psA_busy
        st_ = states0[i]
        W, tc = st_["W"], st_["tc"]
        sb = i % 2
        t_ex = p.do("scalar", lambda s, W=W, sb=sb: s.activation(
            out=B.slot(26 + sb)[:, 0:W], in_=B.psA[0:128, 0:W], func=AF.Exp,
            bias=B.cols.ap()[:, 88:89], accum_out=B.cols.ap()[:, 89:90]),
            deps=[st_["t_mx"], pt_busy[sb]])
        psA_busy = t_ex
        t_rd = p.do("vector", lambda v: v.reciprocal(
            out=B.cols.ap()[:, 90:91], in_=B.cols.ap()[:, 89:90]),
            deps=[t_ex])
        t_pn = p.do("vector", lambda v, W=W, sb=sb: v.tensor_scalar(
            out=B.slot(26 + sb)[:, 0:W], in0=B.slot(26 + sb)[:, 0:W],
            scalar1=B.cols.ap()[:, 90:91], scalar2=None, op0=ALU.mult),
            deps=[t_rd])
        st_["t_pn"] = t_pn
        st_["sb"] = sb

    def a0_stageC(i):
        nonlocal psT_busy
        st_ = states0[i]
        tc, sb, t_pn = st_["tc"], st_["sb"], st_["t_pn"]
        
        ptsl = 24 + sb
        tts = []
        tev = None
        for j in range(tc + 1):
            dd2 = [t_pn]
            if j == 0:
                dd2.append(psT_busy)
                if pt_busy[sb] is not None:
                    dd2.append(pt_busy[sb])
            if j % 8 == 0 and tev is not None:
                dd2.append(tev)
            tts.append(p.do("tensor", lambda t, j=j, sb=sb: t.transpose(
                B.psT[0:128, (j % 8) * 128:(j % 8 + 1) * 128],
                B.slot(26 + sb)[:, j * 128:(j + 1) * 128],
                B.ident_bf.ap()[:, :]), deps=dd2))
            if j % 8 == 7 or j == tc:
                lo = (j // 8) * 8
                tev = p.do("scalar", lambda s, lo=lo, j=j, ptsl=ptsl:
                           s.activation(
                    out=B.slot(ptsl)[:, lo * 128:(j + 1) * 128],
                    in_=B.psT[0:128, 0:((j % 8) + 1) * 128], func=AF.Copy),
                    deps=[tts[-1]])
        psT_busy = tev
        pt_busy[sb] = tev
        st_["tev"] = tev

    def a0_stageD(i):
        st_ = states0[i]
        c, tc, kv, sb, tev = st_["c"], st_["tc"], st_["kv"], st_["sb"], st_["tev"]
        avb = B.ps4.ap() if sb == 0 else B.ps5.ap()
        dd3 = [tev] + vt_toks[kv]
        if av_busy[sb] is not None:
            dd3.append(av_busy[sb])
        lastav = None
        for j in range(tc + 1):
            lastav = p.do("tensor", lambda t, kv=kv, j=j, avb=avb, sb=sb,
                          tc=tc: t.matmul(
                avb[0:128, 0:128],
                B.slot(20 + kv)[:, j * 128:(j + 1) * 128],
                B.slot(24 + sb)[:, j * 128:(j + 1) * 128],
                start=(j == 0), stop=(j == tc)),
                deps=dd3 if j == 0 else [])
        t_ye = p.do("scalar", lambda s, c=c, tc=tc, avb=avb: s.activation(
            out=B.slot(c)[:, tc * 128:(tc + 1) * 128],
            in_=avb[0:128, 0:128], func=AF.Copy),
            deps=[lastav, ev_q[-1], ev_kv[-1]])
        av_busy[sb] = t_ye
        y_toks.setdefault(c, []).append(t_ye)
        del states0[i]

    NU0 = len(units0)
    for i in range(NU0 + 1):
        if i < NU0:
            a0_stageA(i)
            a0_stageB(i)
        if i >= 1:
            a0_stageC(i - 1)
            a0_stageD(i - 1)
    it += NU0

    ws_o = WStream(pr, wo, 8)
    dtoks = {}

    def rhs_y(kc, a, b_):
        return B.slot(kc)[:, a:b_]

    def evac_o(mb, n0, ps, dep):
        tok = p.do("scalar", lambda s, ps=ps, mb=mb, n0=n0: s.activation(
            out=B.slot(8 + mb)[:, n0:n0 + 1024], in_=ps[:, 0:1024],
            func=AF.Copy), deps=[dep])
        dtoks.setdefault(mb, []).append(tok)
        return tok

    def mm_deps_o(mb, kc, n0):
        return y_toks[kc]

    ev_o = mm_evac_stream(pr, Mb=8, Kc=8, ws=ws_o, rhs=rhs_y, evac=evac_o,
                          extra_mm_deps=mm_deps_o, region_init=[psA_busy])

    def delta_src(c):
        return (B.slot(8 + c), dtoks[c])

    st = emit_residual_add(pr, x_in, x_out, delta_src, 1.0, lambda c: [],
                           first_deps=[ev_o[-1]])
    return st


def host_pack_l0_attn(ins):
    wkv = np.concatenate([ins["L0_attn_wk_down"], ins["L0_attn_wv_down"]],
                         1).astype(np.float32)
    wup = np.zeros((64, 256), np.float32)
    wup[0:32, 0:128] = ins["L0_attn_wk_up"][:, :]
    wup[32:64, 0:128] = ins["L0_attn_wk_up"][:, :]
    wup[0:32, 128:256] = ins["L0_attn_wv_up"][:, :]
    wup[32:64, 128:256] = ins["L0_attn_wv_up"][:, :]
    return {
        "L0a_wqp": pack_lhsT(ins["L0_attn_wq"].astype(np.float32)),
        "L0a_wkvp": pack_lhsT(wkv),
        "L0a_wup": wup,
        "L0a_wop": pack_lhsT(ins["L0_attn_wo"].astype(np.float32)),
    }


def build_l0_attn_program():
    pr = Prog()
    pr.load_consts()
    x_in = pr.din("x_cur", [D, T])
    x_out = pr.dout("x_next", [D, T])
    st = emit_l0_attn(pr, x_in, x_out)
    pr.out_tokens = st
    pr.finish()
    return pr


def run_sublayer_l0attn(x_full, ins):
    pr = build_l0_attn_program()
    packed = host_pack_l0_attn(ins)
    packed["L1a_caus"] = np.triu(np.full((128, 128), -1e9, np.float32), 1)
    consts = host_consts()
    in_maps = []
    for core in range(8):
        m = dict(consts)
        m.update(packed)
        m["x_cur"] = np.ascontiguousarray(x_full[core].T)
        in_maps.append(m)
    res = run_bass_kernel_spmd(pr.nc, in_maps, core_ids=list(range(8)))
    out = np.stack([res.results[c]["x_next"].T for c in range(8)])
    pr.close()
    return out




def emit_l3_attn(pr, x_in, x_out, lname="L3a", stage=99):
    B, p = pr.B, pr.p
    H, HD = 16, 64
    wqkv = pr.din(lname + "_wqkvp", [128, 24 * 8 * 128])
    wg = pr.din(lname + "_wgp", [128, 8 * 8 * 128])
    wpr = pr.din(lname + "_wprp", [128, 8 * 8 * 128])
    ropes = pr.din(lname + "_rope", [128, 4 * T], BF16)
    dec = pr.din(lname + "_dec", [128, 16 * 256], BF16)
    colsd = pr.din(lname + "_cols", [128, 16])
    gsc = host_gamma_scalars()

    t_cols = p.dma("sync", B.cols.ap()[:, 0:16], colsd[:, :], deps=pr.const_toks,
                   sem="col")

    t_ln, _ = emit_ln(pr, x_in, xn_slot=0, xb_slot=8, x2_slot=16)
    xn = [B.slot(c) for c in range(8)]

    t_rope = p.dma("gpsimd", B.slot(32, 4), ropes[:, :], sem="l3rope",
                   deps=t_ln)

    ws = WStream(pr, wqkv, 8)
    q_toks, k_toks, v_toks = {}, {}, {}

    def rhs_xn(kc, a, b_):
        return xn[kc][:, a:b_]

    def evac_qkv(mb, n0, ps, dep):
        if mb < 8:
            tok = p.do("vector", lambda v, ps=ps, mb=mb, n0=n0: v.tensor_scalar(
                out=B.slot(8 + mb)[:, n0:n0 + 1024], in0=ps[:, 0:1024],
                scalar1=1.0, scalar2=None, op0=ALU.mult), deps=[dep])
            q_toks.setdefault(mb, []).append(tok)
        elif mb < 16:
            tok = p.do("vector", lambda v, ps=ps, mb=mb, n0=n0: v.tensor_copy(
                out=B.slot(16 + mb - 8)[:, n0:n0 + 1024], in_=ps[:, 0:1024]),
                deps=[dep])
            k_toks.setdefault(mb - 8, []).append(tok)
        else:
            tok = p.do("scalar", lambda s, ps=ps, mb=mb, n0=n0: s.activation(
                out=B.slot(24 + mb - 16)[:, n0:n0 + 1024], in_=ps[:, 0:1024],
                func=AF.Copy), deps=[dep])
            v_toks.setdefault(mb - 16, []).append(tok)
        return tok

    ev_qkv = mm_evac_stream(pr, Mb=24, Kc=8, ws=ws, rhs=rhs_xn, evac=evac_qkv,
                            deps_rhs=t_ln, region_init=t_ln)

    if stage <= 1:
        def delta_src(c):
            return (B.slot(8 + c), q_toks[c])
        return emit_residual_add(pr, x_in, x_out, delta_src, 1.0,
                                 lambda c: [], first_deps=[ev_qkv[-1]])

    rope_done = {}
    ps_busy = ev_qkv[-1]
    for what in range(2):
        base_slot = 8 if what == 0 else 16
        cs = B.slot(32 + 2 * what)
        ss = B.slot(33 + 2 * what)
        for h in range(H):
            r0 = (h % 2) * 64
            sl = B.slot(base_slot + h // 2)
            for half in range(2):
                hs = slice(half * 1024, (half + 1) * 1024)
                dd0 = (q_toks if what == 0 else k_toks)[h // 2] + [t_rope,
                                                                   ps_busy]
                mm2 = []
                for qq in range(2):
                    n0 = half * 1024 + qq * 512
                    mm2.append(p.do("tensor", lambda t, h=h, r0=r0, sl=sl,
                                    n0=n0, qq=qq: t.matmul(
                        (B.ps6 if qq == 0 else B.ps5)[r0:r0 + 64, 0:512],
                        B.rot64.ap()[r0:r0 + 64, :],
                        sl[r0:r0 + 64, n0:n0 + 512],
                        start=True, stop=True), deps=dd0))
                t1 = p.do("vector", lambda v, sl=sl, r0=r0, hs=hs, cs=cs:
                          v.tensor_mul(
                    out=B.stg.ap()[r0:r0 + 64, 0:1024],
                    in0=sl[r0:r0 + 64, hs], in1=cs[r0:r0 + 64, hs]),
                    deps=[t_rope] + dd0)
                t2 = p.do("vector", lambda v, sl=sl, r0=r0, half=half, ss=ss:
                          v.tensor_mul(
                    out=sl[r0:r0 + 64, half * 1024:half * 1024 + 512],
                    in0=B.ps6[r0:r0 + 64, 0:512],
                    in1=ss[r0:r0 + 64, half * 1024:half * 1024 + 512]),
                    deps=[mm2[0], t1])
                t2b = p.do("vector", lambda v, sl=sl, r0=r0, half=half, ss=ss:
                           v.tensor_mul(
                    out=sl[r0:r0 + 64, half * 1024 + 512:half * 1024 + 1024],
                    in0=B.ps5[r0:r0 + 64, 0:512],
                    in1=ss[r0:r0 + 64, half * 1024 + 512:half * 1024 + 1024]),
                    deps=[mm2[1], t2])
                t3 = p.do("vector", lambda v, sl=sl, r0=r0, hs=hs: v.tensor_add(
                    out=sl[r0:r0 + 64, hs], in0=sl[r0:r0 + 64, hs],
                    in1=B.stg.ap()[r0:r0 + 64, 0:1024]), deps=[t2b])
                ps_busy = t3
                rope_done.setdefault((what, h), []).append(t3)

    if stage <= 2:
        def delta_src(c):
            return (B.slot(8 + c), [ps_busy])
        return emit_residual_add(pr, x_in, x_out, delta_src, 1.0,
                                 lambda c: [], first_deps=[ps_busy])

    def vtm(h, tkc):
        off = (32 + tkc // 2) * T + (tkc % 2) * 1024 + h * 64
        return B.arena.ap()[:, off:off + 64]

    vt_toks = {}
    prev_t = None
    for h in range(H):
        for tkc in range(16):
            dd = v_toks[h // 2] + [ps_busy] + ([prev_t] if prev_t else [])
            tt = p.do("tensor", lambda t, h=h, tkc=tkc: t.transpose(
                B.psT[0:128, (tkc % 8) * 128:(tkc % 8) * 128 + 64],
                B.slot(24 + h // 2)[(h % 2) * 64:(h % 2) * 64 + 64,
                                    tkc * 128:(tkc + 1) * 128],
                B.ident64.ap()[(h % 2) * 64:(h % 2) * 64 + 64, :]),
                deps=[d for d in dd if d])
            if tkc % 8 == 7:
                toks = []
                for j in range(8):
                    tkc0 = tkc - 7 + j
                    toks.append(p.do("vector", lambda v, h=h, tkc0=tkc0, j=j:
                                     v.tensor_copy(
                        out=vtm(h, tkc0), in_=B.psT[0:128, j * 128:j * 128 + 64]),
                        deps=[tt]))
                prev_t = toks[-1]
                vt_toks.setdefault(h, []).append(toks[-1])

    t_dec = p.dma("gpsimd", B.slot(26, 2), dec[:, :], sem="l3dec",
                  deps=[prev_t])

    y_toks = {}
    psA_busy = ps_busy
    pt_busy = [None, None]
    av_busy = [None, None]
    it = 0
    for h in range(H):
        ksl = B.slot(16 + h // 2)
        qsl = B.slot(8 + h // 2)
        r0 = (h % 2) * 64
        for tc in range(16):
            sb = it % 2
            stt_toks = []
            last = None
            for kc in range(tc + 1):
                dd = rope_done[(0, h)] + rope_done[(1, h)]
                if kc == 0:
                    dd = dd + [psA_busy]
                last = p.do("tensor", lambda t, h=h, r0=r0, tc=tc, kc=kc,
                            ksl=ksl, qsl=qsl: t.matmul(
                    B.psA[0:128, kc * 128:(kc + 1) * 128],
                    ksl[r0:r0 + 64, kc * 128:(kc + 1) * 128],
                    qsl[r0:r0 + 64, tc * 128:(tc + 1) * 128],
                    start=True, stop=True), deps=dd)
                stt_toks.append(last)
            ev_toks = []
            for kc in range(tc + 1):
                dtile = B.slot(26, 2)[:, h * 256 + (128 if tc != kc else 0):
                                      h * 256 + (256 if tc != kc else 128)]
                imm = float(gsc[h] ** (128 * (tc - kc)) * 0.125)
                dd2 = [stt_toks[-1], t_dec]
                if pt_busy[sb] is not None and kc == 0:
                    dd2.append(pt_busy[sb])
                ev_toks.append(p.do("vector", lambda v, kc=kc, sb=sb,
                                    dtile=dtile, imm=imm: v.scalar_tensor_tensor(
                    out=B.slot(24 + sb)[:, kc * 128:(kc + 1) * 128],
                    in0=B.psA[0:128, kc * 128:(kc + 1) * 128], scalar=imm,
                    in1=dtile, op0=ALU.mult, op1=ALU.mult), deps=dd2))
            psA_busy = ev_toks[-1]
            pt_busy[sb] = ev_toks[-1]
            avb = B.ps4.ap() if sb == 0 else B.ps5.ap()
            dd3 = [ev_toks[-1]] + vt_toks[h]
            if av_busy[sb] is not None:
                dd3.append(av_busy[sb])
            lastav = None
            for j in range(tc + 1):
                lastav = p.do("tensor", lambda t, h=h, j=j, avb=avb, sb=sb,
                              tc=tc: t.matmul(
                    avb[0:64, 0:128], vtm(h, j),
                    B.slot(24 + sb)[:, j * 128:(j + 1) * 128],
                    start=(j == 0), stop=(j == tc)),
                    deps=dd3 if j == 0 else [])
            t_ye = p.do("scalar", lambda s, h=h, tc=tc, avb=avb: s.activation(
                out=B.slot(h // 2)[(h % 2) * 64:(h % 2) * 64 + 64,
                                   tc * 128:(tc + 1) * 128],
                in_=avb[0:64, 0:128], func=AF.Copy),
                deps=[lastav, ev_qkv[-1]])
            av_busy[sb] = t_ye
            y_toks.setdefault(h, []).append(t_ye)
            it += 1

    if stage <= 4:
        def delta_src(c):
            return (B.slot(c), y_toks[2 * c] + y_toks[2 * c + 1])
        return emit_residual_add(pr, x_in, x_out, delta_src, 1.0,
                                 lambda c: [], first_deps=[psA_busy])

    gn_ab = {}
    prev_gn = None
    for h in range(H):
        r0 = (h % 2) * 64
        ysl = B.slot(h // 2)
        dd = y_toks[h] + ([prev_gn] if prev_gn else [])
        t_sq = p.do("scalar", lambda s, ysl=ysl, r0=r0: s.activation(
            out=B.stg.ap()[r0:r0 + 64, 0:T], in_=ysl[r0:r0 + 64, :],
            func=AF.Square), deps=dd)
        mm = None
        for nt in range(4):
            mm = p.do("tensor", lambda t, ysl=ysl, r0=r0, nt=nt: t.matmul(
                B.ps6[0:1, 0:512], B.onescol_bf.ap()[r0:r0 + 64, :],
                ysl[r0:r0 + 64, nt * 512:(nt + 1) * 512],
                start=(nt == 0), stop=(nt == 3)), deps=dd + list(pr.const_toks))
        for nt in range(4):
            mm = p.do("tensor", lambda t, r0=r0, nt=nt: t.matmul(
                B.ps6[32:33, 0:512], B.onescol_bf.ap()[r0:r0 + 64, :],
                B.stg.ap()[r0:r0 + 64, nt * 512:(nt + 1) * 512],
                start=(nt == 0), stop=(nt == 3)), deps=[t_sq])
        t_r1 = p.do("vector", lambda v: v.tensor_reduce(
            out=B.cols.ap()[0:1, 96:97].rearrange("p one -> p one ()"),
            in_=B.ps6[0:1, 0:512].rearrange("p w -> p () w"),
            axis=mybir.AxisListType.X, op=ALU.add), deps=[mm])
        t_r2 = p.do("vector", lambda v: v.tensor_reduce(
            out=B.cols.ap()[0:1, 97:98].rearrange("p one -> p one ()"),
            in_=B.ps6[32:33, 0:512].rearrange("p w -> p () w"),
            axis=mybir.AxisListType.X, op=ALU.add), deps=[mm])
        NN = 64.0 * T
        t_mu = p.do("vector", lambda v: v.tensor_scalar(
            out=B.cols.ap()[0:1, 98:99], in0=B.cols.ap()[0:1, 96:97],
            scalar1=1.0 / NN, scalar2=None, op0=ALU.mult), deps=[t_r1])
        t_m2 = p.do("vector", lambda v: v.tensor_scalar(
            out=B.cols.ap()[0:1, 99:100], in0=B.cols.ap()[0:1, 97:98],
            scalar1=1.0 / NN, scalar2=None, op0=ALU.mult), deps=[t_r2])
        t_mu2 = p.do("scalar", lambda s: s.activation(
            out=B.cols.ap()[0:1, 100:101], in_=B.cols.ap()[0:1, 98:99],
            func=AF.Square), deps=[t_mu])
        t_var = p.do("vector", lambda v: v.tensor_sub(
            out=B.cols.ap()[0:1, 99:100], in0=B.cols.ap()[0:1, 99:100],
            in1=B.cols.ap()[0:1, 100:101]), deps=[t_mu2, t_m2])
        t_lnv = p.do("scalar", lambda s: s.activation(
            out=B.cols.ap()[0:1, 100:101], in_=B.cols.ap()[0:1, 99:100],
            func=AF.Ln, bias=B.cp(1e-5)[0:1, :]), deps=[t_var])
        t_rs = p.do("scalar", lambda s: s.activation(
            out=B.cols.ap()[0:1, 101:102], in_=B.cols.ap()[0:1, 100:101],
            func=AF.Exp, scale=-0.5), deps=[t_lnv])
        t_bc = p.do("tensor", lambda t: t.matmul(
            B.ps6[0:128, 510:511], B.ones_row.ap()[:, :],
            B.cols.ap()[0:1, 98:99], start=True, stop=True),
            deps=[t_rs] + list(pr.const_toks))
        t_bc2 = p.do("tensor", lambda t: t.matmul(
            B.ps6[0:128, 508:509], B.ones_row.ap()[:, :],
            B.cols.ap()[0:1, 101:102], start=True, stop=True), deps=[t_rs, t_bc])
        ci = 104 + (h % 2) * 2
        t_a = p.do("vector", lambda v, h=h, ci=ci: v.tensor_mul(
            out=B.cols.ap()[:, ci:ci + 1], in0=B.ps6[0:128, 508:509],
            in1=B.cols.ap()[:, h // 2:h // 2 + 1]), deps=[t_bc2, t_cols,
                                                          prev_gn or t_bc2])
        t_b = p.do("vector", lambda v, h=h, ci=ci: v.scalar_tensor_tensor(
            out=B.cols.ap()[:, ci + 1:ci + 2], in0=B.ps6[0:128, 510:511],
            scalar=B.cols.ap()[:, ci:ci + 1], in1=B.cols.ap()[:, 8 + h // 2:
                                                              9 + h // 2],
            op0=ALU.mult, op1=ALU.subtract), deps=[t_a])
        t_bf = p.do("vector", lambda v, ci=ci: v.tensor_scalar(
            out=B.cols.ap()[:, ci + 1:ci + 2], in0=B.cols.ap()[:, ci + 1:ci + 2],
            scalar1=-1.0, scalar2=None, op0=ALU.mult), deps=[t_b])
        t_ap = p.do("vector", lambda v, ysl=ysl, r0=r0, ci=ci: v.tensor_scalar(
            out=ysl[r0:r0 + 64, :], in0=ysl[r0:r0 + 64, :],
            scalar1=B.cols.ap()[r0:r0 + 64, ci:ci + 1] if False else
            B.cols.ap()[:, ci:ci + 1][r0:r0 + 64, :],
            scalar2=B.cols.ap()[:, ci + 1:ci + 2][r0:r0 + 64, :],
            op0=ALU.mult, op1=ALU.add), deps=[t_bf])
        prev_gn = t_ap
        gn_ab[h] = t_ap

    if stage <= 5:
        def delta_src(c):
            return (B.slot(c), [gn_ab[2 * c], gn_ab[2 * c + 1]])
        return emit_residual_add(pr, x_in, x_out, delta_src, 1.0,
                                 lambda c: [], first_deps=[prev_gn])

    t_ln2, _ = emit_ln(pr, x_in, xn_slot=24, xb_slot=8, x2_slot=16,
                       deps=[prev_gn, psA_busy])
    xn2 = [B.slot(24 + c) for c in range(8)]

    ws_g = WStream(pr, wg, 8)
    g_toks = {}

    def rhs_xn2(kc, a, b_):
        return xn2[kc][:, a:b_]

    def evac_g(mb, n0, ps, dep):
        tok = p.do("scalar", lambda s, ps=ps, mb=mb, n0=n0: s.activation(
            out=B.slot(8 + mb)[:, n0:n0 + 1024], in_=ps[:, 0:1024],
            func=AF.Silu), deps=[dep])
        g_toks.setdefault(mb, []).append(tok)
        return tok

    ev_g = mm_evac_stream(pr, Mb=8, Kc=8, ws=ws_g, rhs=rhs_xn2, evac=evac_g,
                          deps_rhs=t_ln2, region_init=t_ln2)
    yg_toks = {}
    for c in range(8):
        tok = p.do("vector", lambda v, c=c: v.tensor_mul(
            out=B.slot(c), in0=B.slot(c), in1=B.slot(8 + c)),
            deps=g_toks[c] + [gn_ab[2 * c], gn_ab[2 * c + 1]])
        yg_toks[c] = tok

    ws_o = WStream(pr, wpr, 8)
    dtoks = {}

    def rhs_yg(kc, a, b_):
        return B.slot(kc)[:, a:b_]

    def evac_o(mb, n0, ps, dep):
        tok = p.do("scalar", lambda s, ps=ps, mb=mb, n0=n0: s.activation(
            out=B.slot(16 + mb)[:, n0:n0 + 1024], in_=ps[:, 0:1024],
            func=AF.Copy), deps=[dep])
        dtoks.setdefault(mb, []).append(tok)
        return tok

    def mm_deps_o(mb, kc, n0):
        return [yg_toks[kc]]

    ev_o = mm_evac_stream(pr, Mb=8, Kc=8, ws=ws_o, rhs=rhs_yg, evac=evac_o,
                          extra_mm_deps=mm_deps_o, region_init=ev_g[-2:])

    def delta_src(c):
        return (B.slot(16 + c), dtoks[c])

    st = emit_residual_add(pr, x_in, x_out, delta_src, 1.0, lambda c: [],
                           first_deps=[ev_o[-1]])
    return st


def _cat_ps(B):
    return None


def host_gamma_scalars():
    return [float(1.0 - 2.0 ** (-5.0 - h)) for h in range(16)]


def host_pack_l3_attn(ins):
    H, DH, TT_ = 16, 64, T
    half = np.arange(0, DH, 2, dtype=np.float64)
    inv_freq = 1.0 / (10000.0 ** (half / DH))
    t = np.arange(TT_, dtype=np.float64)
    freqs = np.repeat(t[:, None] * inv_freq[None, :], 2, axis=-1)
    scale_vec = (half + 0.4 * DH) / (1.4 * DH)
    power = (t - TT_ // 2) / 512.0
    scale = np.repeat(scale_vec[None, :] ** power[:, None], 2, axis=-1)
    cos, sin = np.cos(freqs), np.sin(freqs)
    CS = (cos * scale).T
    SS = (sin * scale).T
    CSi = (cos / scale).T
    SSi = (sin / scale).T
    rope = np.concatenate(
        [np.tile(a, (2, 1))[None] for a in (CS, SS, CSi, SSi)], 0)
    rope = rope.transpose(1, 0, 2).reshape(128, 4 * TT_).astype(BF)
    qq = np.arange(128)
    diff = qq[None, :] - qq[:, None]
    dec = np.zeros((128, 16 * 256), np.float64)
    g = host_gamma_scalars()
    for h in range(16):
        d = g[h] ** diff
        dec[:, h * 256:h * 256 + 128] = np.where(diff >= 0, d, 0.0)
        dec[:, h * 256 + 128:h * 256 + 256] = d
    cols = np.zeros((128, 16), np.float32)
    cols[:, 0:8] = pack_cols(ins["L3_attn_gn_weight"].astype(np.float32))
    cols[:, 8:16] = pack_cols(ins["L3_attn_gn_bias"].astype(np.float32))
    return {
        "L3a_wqkvp": pack_lhsT(np.ascontiguousarray(
            ins["L3_attn_w_qkv"].astype(np.float32).T)),
        "L3a_wgp": pack_lhsT(np.ascontiguousarray(
            ins["L3_attn_w_gated"].astype(np.float32).T)),
        "L3a_wprp": pack_lhsT(np.ascontiguousarray(
            ins["L3_attn_w_proj"].astype(np.float32).T)),
        "L3a_rope": np.ascontiguousarray(rope),
        "L3a_dec": dec.astype(BF),
        "L3a_cols": cols,
    }


def build_l3_attn_program(stage=99):
    pr = Prog()
    pr.load_consts()
    x_in = pr.din("x_cur", [D, T])
    x_out = pr.dout("x_next", [D, T])
    st = emit_l3_attn(pr, x_in, x_out, stage=stage)
    pr.out_tokens = st
    pr.finish()
    return pr


def run_sublayer_l3attn(x_full, ins, stage=99):
    pr = build_l3_attn_program(stage)
    packed = host_pack_l3_attn(ins)
    consts = host_consts()
    in_maps = []
    for core in range(8):
        m = dict(consts)
        m.update(packed)
        m["x_cur"] = np.ascontiguousarray(x_full[core].T)
        in_maps.append(m)
    res = run_bass_kernel_spmd(pr.nc, in_maps, core_ids=list(range(8)))
    out = np.stack([res.results[c]["x_next"].T for c in range(8)])
    pr.close()
    return out




def emit_l2_attn(pr, x_in, x_out, lname="L2a"):
    B, p = pr.B, pr.p
    win = pr.din(lname + "_winp", [128, 32 * 8 * 128])
    wx = pr.din(lname + "_wxp", [128, 9 * 16 * 128])
    wdt = pr.din(lname + "_wdtp", [128, 16 * 8 * 128])
    wout = pr.din(lname + "_woutp", [128, 8 * 16 * 128])
    ascan = pr.din(lname + "_ascan", [128, 256])
    reps = pr.din(lname + "_reps", [128, 4096], BF16)
    colsd = pr.din(lname + "_cols", [128, 96])
    selbc = pr.din(lname + "_selbc", [32, 256], BF16)

    if not hasattr(pr, "_l2a_small"):
        st = pr.stack.enter_context
        pr._l2a_small = (
            st(pr.nc.sbuf_tensor(lname + "_asc_sb", [128, 256], F32)),
            st(pr.nc.sbuf_tensor(lname + "_selbc_sb", [32, 256], BF16)),
        )
    asc_sb, selbc_sb = pr._l2a_small

    zdram = pr.dscratch(lname + "_z", [2048, T], BF16)
    ydram = pr.dscratch(lname + "_y", [2048, T], BF16)

    t_cols = p.dma("sync", B.cols.ap()[:, 0:96], colsd[:, :], deps=pr.const_toks,
                   sem="col")
    t_asc = p.dma("sync", asc_sb.ap()[:, :], ascan[:, :], sem="l2asc")
    t_sbc = p.dma("sync", selbc_sb.ap()[:, :], selbc[:, :], sem="l2sbc")

    t_ln, _ = emit_ln(pr, x_in, xn_slot=0, xb_slot=8, x2_slot=16)
    xn = [B.slot(c) for c in range(8)]

    t_a1 = p.do("scalar", lambda s: s.activation(
        out=asc_sb.ap()[:, :], in_=asc_sb.ap()[:, :], func=AF.Exp),
        deps=[t_asc])
    t_a2 = p.do("vector", lambda v: v.tensor_scalar(
        out=asc_sb.ap()[:, :], in0=asc_sb.ap()[:, :], scalar1=-1.0,
        scalar2=None, op0=ALU.mult), deps=[t_a1])

    def rep8(gg):
        return B.slot(6, 2)[:, gg * 128:(gg + 1) * 128]

    def selp(gg):
        return B.slot(6, 2)[:, 2048 + gg * 128:2048 + (gg + 1) * 128]

    ws_in = WStream(pr, win, 8)
    upre_toks = {}
    z_w_toks = []
    stg_busy = {5: None, 5.5: None}

    def evac_in(mb, n0, ps, dep):
        if mb < 16:
            tok = p.do("scalar", lambda s, ps=ps, mb=mb, n0=n0: s.activation(
                out=B.slot(8 + mb)[:, n0:n0 + 1024], in_=ps[:, 0:1024],
                func=AF.Copy), deps=[dep])
            upre_toks.setdefault(mb, []).append(tok)
            return tok
        key = 5 if (n0 == 0) else 5.5
        tok = p.do("scalar", lambda s, ps=ps, n0=n0: s.activation(
            out=B.slot(5)[:, n0:n0 + 1024], in_=ps[:, 0:1024],
            func=AF.Copy), deps=[dep, stg_busy[key]])
        if n0 == 1024:
            tok = p.dma("sync", zdram[(mb - 16) * 128:(mb - 15) * 128, :],
                        B.slot(5), deps=[tok], sem="l2zw")
            z_w_toks.append(tok)
            stg_busy[5] = tok
            stg_busy[5.5] = tok
        return tok

    ev_in = mm_evac_stream(pr, Mb=32, Kc=8, ws=ws_in, rhs=lambda kc, a, b_:
                           xn[kc][:, a:b_], evac=evac_in, deps_rhs=t_ln,
                           region_init=t_ln)

    conv_toks = {}
    prev_cv = None
    for c in range(16):
        usl = B.slot(24 + c)
        psl = B.slot(8 + c)
        wc = [B.cols.ap()[:, j * 16 + c:j * 16 + c + 1] for j in range(4)]
        bcc = B.cols.ap()[:, 64 + c:65 + c]
        dd = upre_toks[c] + [t_cols] + ([prev_cv] if prev_cv else [])
        acc = B.stg.ap().bitcast(F32)[:, 0:1024]
        half_prev = None
        for half in range(2):
            h0 = half * 1024
            lo = 3 if half == 0 else 0
            width = 1024 - lo
            t1 = p.do("vector", lambda v, psl=psl, wc=wc, h0=h0, lo=lo,
                      width=width, acc=acc: v.tensor_scalar(
                out=acc[:, lo:lo + width], in0=psl[:, h0 + lo - 3:h0 + lo - 3 + width],
                scalar1=wc[0], scalar2=None, op0=ALU.mult),
                deps=dd + ([half_prev] if half_prev else []))
            for j in range(1, 4):
                t1 = p.do("vector", lambda v, psl=psl, wc=wc, h0=h0, lo=lo,
                          width=width, j=j, acc=acc: v.scalar_tensor_tensor(
                    out=acc[:, lo:lo + width],
                    in0=psl[:, h0 + lo - 3 + j:h0 + lo - 3 + j + width],
                    scalar=wc[j], in1=acc[:, lo:lo + width],
                    op0=ALU.mult, op1=ALU.add), deps=[t1])
            if half == 0:
                for tcol in range(3):
                    te = p.do("vector", lambda v, psl=psl, wc=wc, tcol=tcol,
                              acc=acc: v.tensor_scalar(
                        out=acc[:, tcol:tcol + 1], in0=psl[:, 0:1],
                        scalar1=wc[3 - tcol], scalar2=None, op0=ALU.mult),
                        deps=[t1])
                    for j in range(4 - tcol, 4):
                        te = p.do("vector", lambda v, psl=psl, wc=wc, tcol=tcol,
                                  j=j, acc=acc: v.scalar_tensor_tensor(
                            out=acc[:, tcol:tcol + 1],
                            in0=psl[:, tcol - 3 + j:tcol - 2 + j],
                            scalar=wc[j], in1=acc[:, tcol:tcol + 1],
                            op0=ALU.mult, op1=ALU.add), deps=[te])
                    t1 = te
            tok = p.do("scalar", lambda s, usl=usl, h0=h0, bcc=bcc, acc=acc:
                       s.activation(
                out=usl[:, h0:h0 + 1024], in_=acc, func=AF.Silu, bias=bcc),
                deps=[t1])
            conv_toks.setdefault(c, []).append(tok)
            prev_cv = tok
            half_prev = tok

    ws_x = WStream(pr, wx, 16)
    xdbl_toks = {}

    def evac_x(mb, n0, ps, dep):
        tok = p.do("scalar", lambda s, ps=ps, mb=mb, n0=n0: s.activation(
            out=B.slot(mb)[:, n0:n0 + 1024], in_=ps[:, 0:1024],
            func=AF.Copy), deps=[dep, t_ln[mb] if mb < 8 else t_ln[-1]])
        xdbl_toks.setdefault(mb, []).append(tok)
        return tok

    def mm_deps_x(mb, kc, n0):
        return conv_toks[kc]

    ev_x = mm_evac_stream(pr, Mb=9, Kc=16, ws=ws_x, rhs=lambda kc, a, b_:
                          B.slot(24 + kc)[:, a:b_], evac=evac_x,
                          extra_mm_deps=mm_deps_x, region_init=[ev_in[-1]])

    ws_dt = WStream(pr, wdt, 8)
    d_toks = {}
    sfp_prev = [None]

    def dslot(mb):
        return B.slot(9 + mb) if mb < 15 else B.slot(0)

    def evac_dt(mb, n0, ps, dep):
        sf = B.stg.ap().bitcast(F32)[:, 0:1024]
        tf = B.tmpf.ap()[:, 0:1024]
        dd = [dep, t_cols, sfp_prev[0]]
        t0 = p.do("vector", lambda v, ps=ps, mb=mb, sf=sf: v.tensor_scalar(
            out=sf, in0=ps[:, 0:1024],
            scalar1=B.cols.ap()[:, 80 + mb:81 + mb], scalar2=None, op0=ALU.add),
            deps=dd)
        ta = p.do("scalar", lambda s, sf=sf, tf=tf: s.activation(
            out=tf, in_=sf, func=AF.Abs), deps=[t0])
        te = p.do("scalar", lambda s, tf=tf: s.activation(
            out=tf, in_=tf, func=AF.Exp, scale=-1.0), deps=[ta])
        tl = p.do("scalar", lambda s, tf=tf: s.activation(
            out=tf, in_=tf, func=AF.Ln, bias=B.cp(1.0)), deps=[te])
        tr = p.do("vector", lambda v, sf=sf: v.tensor_scalar(
            out=sf, in0=sf, scalar1=0.0, scalar2=None, op0=ALU.max), deps=[tl])
        tok = p.do("vector", lambda v, mb=mb, n0=n0, sf=sf, tf=tf: v.tensor_add(
            out=dslot(mb)[:, n0:n0 + 1024], in0=sf, in1=tf), deps=[tr])
        d_toks.setdefault(mb, []).append(tok)
        sfp_prev[0] = tok
        return tok

    def mm_deps_dt(mb, kc, n0):
        return xdbl_toks[kc]

    ev_dt = mm_evac_stream(pr, Mb=16, Kc=8, ws=ws_dt, rhs=lambda kc, a, b_:
                           B.slot(kc)[:, a:b_], evac=evac_dt,
                           extra_mm_deps=mm_deps_dt, region_init=[ev_x[-1]])


    bc_toks = []
    prev_bc = ev_dt[-1]
    for what in range(2):
        for nt in range(4):
            tt = p.do("tensor", lambda t, what=what, nt=nt: t.matmul(
                B.ps6[0:128, 0:512],
                selbc_sb.ap()[:, what * 128:what * 128 + 128],
                B.slot(8)[0:32, nt * 512:(nt + 1) * 512],
                start=True, stop=True),
                deps=[t_sbc, xdbl_toks[8][nt // 2], prev_bc, ev_dt[-1]])
            tok = p.do("vector", lambda v, what=what, nt=nt: v.tensor_copy(
                out=B.slot(2 + what)[:, nt * 512:(nt + 1) * 512],
                in_=B.ps6[0:128, 0:512]), deps=[tt])
            prev_bc = tok
            bc_toks.append(tok)

    t_rep = p.dma("gpsimd", B.slot(6, 2), reps[:, :], sem="l2rep",
                  deps=[ev_dt[-1]])

    yq_banks = [B.ps4.ap()[:, 0:512], B.ps5.ap()[:, 0:512],
                B.ps6.ap()[:, 0:512], B.psA[0:128, 1024:1536]]

    def scr(par):
        slot_i = 4 if par == 0 else 8
        base = slot_i * T
        return (B.arena.ap()[:, base:base + 512],
                B.arena.ap()[:, base + 512:base + 1024],
                B.arena.ap()[:, base + 1024:base + 1536],
                B.arena.ap()[:, base + 1536:base + 2048],
                B.arena.ap()[:, base + 1535:base + 1536])

    y_ev_toks = []
    tW = None
    m2_last = None
    rb_busy = [ev_dt[-1], ev_dt[-1]]
    yb_busy = [None] * 4
    scr_busy = [{"dA": None, "dB": None, "X": None, "Xc": None},
                {"dA": bc_toks[-1], "dB": bc_toks[-1], "X": bc_toks[-1],
                 "Xc": bc_toks[-1]}]
    prev_yev = None
    quarters = [(c, gg, q) for c in range(16) for gg in range(16)
                for q in range(4)]
    tW_by_c = {}
    states = {}

    def q_stageA(i):
        c, gg, q = quarters[i]
        par = i % 2
        dAq, dBq, Xq, Xcq, scol = scr(par)
        if c not in tW_by_c:
            prevW = tW_by_c.get(c - 1)
            tok = p.do("vector", lambda v, c=c: v.tensor_mul(
                out=B.slot(1), in0=dslot(c), in1=B.slot(24 + c)),
                deps=[d_toks[c][-1], conv_toks[c][-1]] +
                ([prevW] if prevW else []) +
                ([m2_by_c[c - 1]] if (c - 1) in m2_by_c else []))
            tW_by_c[c] = tok
        sl = slice(q * 512, (q + 1) * 512)
        m1 = p.do("tensor", lambda t, c=c, gg=gg, sl=sl: t.matmul(
            B.psA[0:128, 0:512], rep8(gg), dslot(c)[:, sl],
            start=True, stop=True),
            deps=[rb_busy[0], t_rep, d_toks[c][-1]])
        m2 = p.do("tensor", lambda t, c=c, gg=gg, sl=sl: t.matmul(
            B.psA[0:128, 512:1024], rep8(gg), B.slot(1)[:, sl],
            start=True, stop=True), deps=[rb_busy[1], tW_by_c[c]])
        m2_by_c[c] = m2
        g = c * 16 + gg
        tdA = p.do("scalar", lambda s, g=g, dAq=dAq: s.activation(
            out=dAq, in_=B.psA[0:128, 0:512], func=AF.Exp,
            scale=asc_sb.ap()[:, g:g + 1]),
            deps=[m1, t_a2, scr_busy[par]["dA"]])
        rb_busy[0] = tdA
        tdB = p.do("vector", lambda v, q=q, dBq=dBq: v.tensor_tensor(
            out=dBq, in0=B.psA[0:128, 512:1024],
            in1=B.slot(2)[:, q * 512:(q + 1) * 512], op=ALU.mult),
            deps=[m2, bc_toks[3], scr_busy[par]["dB"]])
        rb_busy[1] = tdB
        states[i] = dict(par=par, c=c, gg=gg, q=q, tdA=tdA, tdB=tdB)

    def q_stageB(i):
        st_ = states[i]
        par, c, gg, q = st_["par"], st_["c"], st_["gg"], st_["q"]
        dAq, dBq, Xq, Xcq, scol = scr(par)
        prev_scol = scr(1 - par)[4]
        tX = p.do("vector", lambda v, q=q, Xq=Xq, dAq=dAq, dBq=dBq,
                  prev_scol=prev_scol: v.tensor_tensor_scan(
            out=Xq, data0=dAq, data1=dBq,
            initial=(0.0 if q == 0 else prev_scol),
            op0=ALU.mult, op1=ALU.add),
            deps=[st_["tdA"], st_["tdB"], scr_busy[par]["X"],
                  scr_busy[par]["Xc"],
                  states[i - 1]["tX"] if q != 0 and (i - 1) in states else None])
        st_["tX"] = tX
        scr_busy[par]["dA"] = tX
        scr_busy[par]["dB"] = tX
        tXc = p.do("gpsimd", lambda gp, q=q, Xq=Xq, Xcq=Xcq: gp.tensor_tensor(
            out=Xcq, in0=Xq, in1=B.slot(3)[:, q * 512:(q + 1) * 512],
            op=ALU.mult), deps=[tX, bc_toks[7], scr_busy[par]["Xc"]])
        scr_busy[par]["X"] = tXc
        st_["tXc"] = tXc

    def q_stageC(i):
        nonlocal prev_yev
        st_ = states[i]
        par, c, gg, q = st_["par"], st_["c"], st_["gg"], st_["q"]
        Xcq = scr(par)[3]
        my = p.do("tensor", lambda t, gg=gg, q=q, Xcq=Xcq: t.matmul(
            yq_banks[q], selp(gg), Xcq,
            start=(gg == 0), stop=(gg == 15)),
            deps=[st_["tXc"], yb_busy[q] if gg == 0 else None, t_rep])
        scr_busy[par]["Xc"] = my
        if gg == 15:
            tok = p.do("scalar", lambda s, q=q, c=c: s.activation(
                out=B.slot(5)[:, q * 512:(q + 1) * 512],
                in_=yq_banks[q], func=AF.Copy),
                deps=[my, prev_yev])
            yb_busy[q] = tok
            prev_yev = tok
            if q == 3:
                tok = p.dma("sync", ydram[c * 128:(c + 1) * 128, :], B.slot(5),
                            deps=[prev_yev], sem="l2yw")
                prev_yev = tok
                y_ev_toks.append(tok)
        del states[i]

    m2_by_c = {}
    NQ = len(quarters)
    for i in range(NQ + 2):
        if i < NQ:
            q_stageA(i)
        if 1 <= i <= NQ:
            q_stageB(i - 1)
        if i >= 2:
            q_stageC(i - 2)

    g_toks = {}
    prev_g = y_ev_toks[-1]
    for c in range(16):
        gsl = dslot(c)
        t_zl = p.dma("sync", B.slot(4), zdram[c * 128:(c + 1) * 128, :],
                     deps=[prev_g, z_w_toks[c]], sem="l2zr")
        t_yl = p.dma("scalar", B.slot(5), ydram[c * 128:(c + 1) * 128, :],
                     deps=[prev_g, y_ev_toks[c]], sem="l2yr")
        t_sz = p.do("scalar", lambda s: s.activation(
            out=B.slot(4), in_=B.slot(4), func=AF.Silu), deps=[t_zl])
        t_ad = p.do("vector", lambda v, c=c: v.tensor_add(
            out=B.slot(5), in0=B.slot(5), in1=B.slot(24 + c)),
            deps=[t_yl, conv_toks[c][-1]])
        tok = p.do("vector", lambda v, gsl=gsl: v.tensor_mul(
            out=gsl, in0=B.slot(5), in1=B.slot(4)), deps=[t_sz, t_ad])
        g_toks[c] = tok
        prev_g = tok

    out_sl = [B.slot(24 + c) for c in range(8)]
    ws_o = WStream(pr, wout, 16)
    dtoks = {}

    def evac_o(mb, n0, ps, dep):
        tok = p.do("scalar", lambda s, ps=ps, mb=mb, n0=n0: s.activation(
            out=out_sl[mb][:, n0:n0 + 1024], in_=ps[:, 0:1024],
            func=AF.Copy), deps=[dep, g_toks[mb] if mb < 16 else None])
        dtoks.setdefault(mb, []).append(tok)
        return tok

    def mm_deps_o(mb, kc, n0):
        return [g_toks[kc]]

    ev_o = mm_evac_stream(pr, Mb=8, Kc=16, ws=ws_o, rhs=lambda kc, a, b_:
                          dslot(kc)[:, a:b_], evac=evac_o,
                          extra_mm_deps=mm_deps_o,
                          region_init=[prev_g, scr_busy[0]["Xc"],
                                       scr_busy[1]["Xc"]])

    def delta_src(c):
        return (out_sl[c], dtoks[c])

    st = emit_residual_add(pr, x_in, x_out, delta_src, 1.0, lambda c: [],
                           first_deps=[ev_o[-1]])
    return st


def host_pack_l2_attn(ins):
    DI = 2048
    wx = pad_to(ins["L2_attn_W_x"].astype(np.float32), DI, 1152)
    A_log = ins["L2_attn_A_log"].astype(np.float32)
    A_scan = np.ascontiguousarray(
        A_log.reshape(256, 8, 16).transpose(2, 1, 0).reshape(128, 256))
    rep = np.zeros((128, 2048), np.float32)
    selp = np.zeros((128, 2048), np.float32)
    for gg in range(16):
        for pp in range(128):
            rep[gg * 8 + (pp % 8), gg * 128 + pp] = 1.0
            selp[pp, gg * 128 + gg * 8 + (pp % 8)] = 1.0
    reps = np.concatenate([rep, selp], 1).astype(BF)
    selbc = np.zeros((32, 256), np.float32)
    for pp in range(128):
        selbc[pp // 8, pp] = 1.0
        selbc[16 + pp // 8, 128 + pp] = 1.0
    cols = np.zeros((128, 96), np.float32)
    convw = ins["L2_attn_conv_w"].astype(np.float32)[:, 0, :]
    for j in range(4):
        cols[:, j * 16:(j + 1) * 16] = np.ascontiguousarray(
            convw[j].reshape(16, 128).T)
    cols[:, 64:80] = pack_cols(ins["L2_attn_conv_b"].astype(np.float32))
    cols[:, 80:96] = pack_cols(ins["L2_attn_b_dt"].astype(np.float32))
    return {
        "L2a_winp": pack_lhsT(ins["L2_attn_W_in"].astype(np.float32)),
        "L2a_wxp": pack_lhsT(wx),
        "L2a_wdtp": pack_lhsT(ins["L2_attn_W_dt"].astype(np.float32)),
        "L2a_woutp": pack_lhsT(ins["L2_attn_W_out"].astype(np.float32)),
        "L2a_ascan": A_scan,
        "L2a_reps": reps,
        "L2a_cols": cols,
        "L2a_selbc": selbc.astype(BF),
    }


def build_l2_attn_program():
    pr = Prog()
    pr.load_consts()
    x_in = pr.din("x_cur", [D, T])
    x_out = pr.dout("x_next", [D, T])
    st = emit_l2_attn(pr, x_in, x_out)
    pr.out_tokens = st
    pr.finish()
    return pr


def run_sublayer_l2attn(x_full, ins):
    pr = build_l2_attn_program()
    packed = host_pack_l2_attn(ins)
    consts = host_consts()
    in_maps = []
    for core in range(8):
        m = dict(consts)
        m.update(packed)
        m["x_cur"] = np.ascontiguousarray(x_full[core].T)
        in_maps.append(m)
    res = run_bass_kernel_spmd(pr.nc, in_maps, core_ids=list(range(8)))
    out = np.stack([res.results[c]["x_next"].T for c in range(8)])
    pr.close()
    return out




def emit_l2_ffn(pr, x_in, x_out, lname="L2f", n_sub=6, macaron_scale=0.5,
                halves=2):
    B, p = pr.B, pr.p
    NSUB = n_sub
    wqkv = pr.din(lname + "_wqkvp", [128, 6 * 24 * 8 * 128])
    wo = pr.din(lname + "_wop", [128, 6 * 8 * 8 * 128])
    w1 = pr.din(lname + "_w1p", [128, 6 * 44 * 8 * 128])
    w2 = pr.din(lname + "_w2p", [128, 6 * 8 * 22 * 128])
    ropes = pr.din(lname + "_rope", [128, 4 * T], BF16)
    colsd = pr.din(lname + "_cols", [128, 6 * 112])
    lms = pr.din(lname + "_lm", [128, 2 * 512 + 2 * 256], BF16)

    if not hasattr(pr, "_l2f_small"):
        st = pr.stack.enter_context
        pr._l2f_small = (
            st(pr.nc.sbuf_tensor(lname + "_lm_sb", [128, 1536], BF16)),
            st(pr.nc.sbuf_tensor(lname + "_cols2", [128, 112], F32)),
        )
    lm_sb, cols2 = pr._l2f_small
    t_lm = p.dma("sync", lm_sb.ap()[:, :], lms[:, :], sem="l2flm")

    x2dram = pr.dscratch(lname + "_x2", [D, T], F32)
    ydram2 = pr.dscratch(lname + "_yat", [D, T], BF16)

    CT = list(pr.const_toks)
    x2 = [B.slot_f32(2 * c, 1) for c in range(8)]

    t_ln, _ = emit_ln(pr, x_in, xn_slot=16, xb_slot=24, x2_slot=32)
    x2_tok = {}
    for c in range(8):
        tok = p.do("vector", lambda v, c=c: v.tensor_copy(
            out=x2[c], in_=B.slot(16 + c)), deps=[t_ln[c]])
        x2_tok[c] = [tok]


    def ccol(i, j):
        return cols2.ap()[:, j:j + 1]

    prev_cols = None
    psA_busy = [t_ln[-1], t_ln[-1]]

    rope_war = [t_ln[-1]]
    for i in range(NSUB):
        t_cols = p.dma("sync", cols2.ap()[:, :], colsd[:, i * 112:(i + 1) * 112],
                       deps=[prev_cols] if prev_cols else CT, sem="col")
        t_rope = p.dma("gpsimd", B.slot(32, 4), ropes[:, :], sem="l2frope",
                       deps=rope_war)

        ax_toks = {}
        for c in range(8):
            if c < 4:
                tok = p.do("vector", lambda v, c=c: v.tensor_copy(
                    out=B.slot(16 + c), in_=x2[c]), deps=x2_tok[c])
            else:
                t0 = p.do("vector", lambda v, c=c: v.memset(
                    B.slot(16 + c)[:, 0:1], 0.0), deps=x2_tok[c])
                tok = p.do("vector", lambda v, c=c: v.tensor_copy(
                    out=B.slot(16 + c)[:, 1:T], in_=x2[c][:, 0:T - 1]),
                    deps=[t0])
            ax_toks[c] = tok
        ax = [B.slot(16 + c) for c in range(8)]
        psT_busy = [None, None]

        y_pair_toks = []
        pt_busy = [None, None]
        av_busy = [None, None]
        vt_busy = [None, None]
        qkv_busy = [None, None]
        it = 0
        for pj in range(8):
            pb = pj % 2
            qsl, ksl, vsl = (B.slot(24 + 3 * pb), B.slot(25 + 3 * pb),
                             B.slot(26 + 3 * pb))
            ws_dep = [t for t in [qkv_busy[pb]] if t]
            ld = p.dma("gpsimd", B.wb[pb].ap()[:, 0:3 * 8 * 128],
                       wqkv[:, (i * 24 + 3 * pj) * 8 * 128:
                            (i * 24 + 3 * pj + 3) * 8 * 128],
                       deps=[pr.wb_last_use[pb]] if pr.wb_last_use[pb] else [],
                       sem=f"wb{pb}")
            pair_toks = []
            last_mm = None
            for mi, dst in enumerate((qsl, ksl, vsl)):
                for n0 in (0, 1024):
                    last_mm = None
                    for kc in range(8):
                        dd = ([ld, psA_busy[0], psA_busy[1]] + ws_dep +
                              [ax_toks[kc]]) if kc == 0 else [ax_toks[kc]]
                        for qq in range(2):
                            a = n0 + qq * 512
                            last_mm = p.do("tensor", lambda t, mi=mi, kc=kc,
                                           a=a, n0=n0, pb=pb: t.matmul(
                                B.psA[0:128, (a - n0):(a - n0) + 512],
                                B.wb[pb].ap()[:, (mi * 8 + kc) * 128:
                                              (mi * 8 + kc + 1) * 128],
                                ax[kc][:, a:a + 512],
                                start=(kc == 0), stop=(kc == 7)), deps=dd)
                    tok = p.do("scalar", lambda s, dst=dst, n0=n0: s.activation(
                        out=dst[:, n0:n0 + 1024], in_=B.psA[0:128, 0:1024],
                        func=AF.Copy), deps=[last_mm])
                    psA_busy[0] = tok
                    psA_busy[1] = tok
                    pair_toks.append(tok)
            pr.wb_last_use[pb] = last_mm
            qkv_done = pair_toks[-1]

            rope_done = []
            sq_prev = None
            for what in range(2):
                sl = (qsl, ksl)[what]
                t_sq = p.do("scalar", lambda s, sl=sl: s.activation(
                    out=B.slot(39), in_=sl, func=AF.Square),
                    deps=[qkv_done, vt_busy[pb] or qkv_done,
                          sq_prev])
                mm = None
                for hh in range(2):
                    r0 = hh * 64
                    ro = hh * 32
                    for nt in range(4):
                        mm = p.do("tensor", lambda t, r0=r0, ro=ro, nt=nt:
                                  t.matmul(
                            B.psA[ro:ro + 1, nt * 512:(nt + 1) * 512],
                            B.onescol_bf.ap()[r0:r0 + 64, :],
                            B.slot(39)[r0:r0 + 64, nt * 512:(nt + 1) * 512],
                            start=True, stop=True),
                            deps=[t_sq, psA_busy[0], psA_busy[1]])
                sq_prev = mm
                t_ns = p.do("vector", lambda v: v.tensor_scalar(
                    out=B.tmpf.ap()[0:64, 0:T], in0=B.psA[0:64, 0:T],
                    scalar1=1e-24, scalar2=None, op0=ALU.max), deps=[mm])
                psA_busy[0] = t_ns
                psA_busy[1] = t_ns
                t_l = p.do("scalar", lambda s: s.activation(
                    out=B.tmpf.ap()[0:64, 0:T], in_=B.tmpf.ap()[0:64, 0:T],
                    func=AF.Ln), deps=[t_ns])
                t_r = p.do("scalar", lambda s: s.activation(
                    out=B.tmpf.ap()[0:64, 0:T], in_=B.tmpf.ap()[0:64, 0:T],
                    func=AF.Exp, scale=-0.5), deps=[t_l])
                bc_mm = []
                for nt in range(4):
                    bc_mm.append(p.do("tensor", lambda t, nt=nt: t.matmul(
                        B.psA[0:128, nt * 512:(nt + 1) * 512],
                        B.selrow.ap()[0:64, :],
                        B.tmpf.ap()[0:64, nt * 512:(nt + 1) * 512],
                        start=True, stop=True), deps=[t_r] + CT))
                t_ap = p.do("vector", lambda v, sl=sl, i=i: v.scalar_tensor_tensor(
                    out=sl, in0=sl, scalar=ccol(i, 0), in1=B.psA[0:128, 0:T],
                    op0=ALU.mult, op1=ALU.mult), deps=bc_mm + [t_cols])
                psA_busy[0] = t_ap
                psA_busy[1] = t_ap
                cs = B.slot(32 + 2 * what)
                ss = B.slot(33 + 2 * what)
                rp = t_ap
                for half in range(2):
                    h0 = half * 1024
                    m1 = p.do("tensor", lambda t, sl=sl, h0=h0: t.matmul(
                        B.ps6[0:128, 0:512], B.rot128.ap()[:, :],
                        sl[:, h0:h0 + 512], start=True, stop=True),
                        deps=[rp] + CT)
                    m2 = p.do("tensor", lambda t, sl=sl, h0=h0: t.matmul(
                        B.ps5[0:128, 0:512], B.rot128.ap()[:, :],
                        sl[:, h0 + 512:h0 + 1024], start=True, stop=True),
                        deps=[rp, av_busy[1], av_busy[0]] + CT)
                    t1 = p.do("vector", lambda v, sl=sl, h0=h0, cs=cs:
                              v.tensor_mul(
                        out=B.stg.ap()[:, 0:1024], in0=sl[:, h0:h0 + 1024],
                        in1=cs[:, h0:h0 + 1024]), deps=[rp, t_rope, m2])
                    t2 = p.do("vector", lambda v, sl=sl, h0=h0, ss=ss:
                              v.tensor_mul(
                        out=sl[:, h0:h0 + 512], in0=B.ps6[0:128, 0:512],
                        in1=ss[:, h0:h0 + 512]), deps=[m1, t1])
                    t2b = p.do("vector", lambda v, sl=sl, h0=h0, ss=ss:
                               v.tensor_mul(
                        out=sl[:, h0 + 512:h0 + 1024], in0=B.ps5[0:128, 0:512],
                        in1=ss[:, h0 + 512:h0 + 1024]), deps=[m2, t2])
                    t3 = p.do("vector", lambda v, sl=sl, h0=h0: v.tensor_add(
                        out=sl[:, h0:h0 + 1024], in0=sl[:, h0:h0 + 1024],
                        in1=B.stg.ap()[:, 0:1024]), deps=[t2b])
                    rp = t3
                rope_done.append(rp)

            def vtm2(pb, hh, tkc):
                off = (30 + pb) * T + tkc * 128 + hh * 64
                return B.arena.ap()[:, off:off + 64]

            vt_prev = psT_busy[0] if isinstance(psT_busy, list) else psT_busy
            vts = []
            for hh in range(2):
                for tkc in range(16):
                    dd = [qkv_done] + ([vt_prev] if vt_prev else []) +                          ([vt_busy[pb]] if vt_busy[pb] and tkc == 0 else [])
                    tt = p.do("tensor", lambda t, hh=hh, tkc=tkc, vsl=vsl:
                              t.transpose(
                        B.psT[0:128, (tkc % 8) * 128:(tkc % 8) * 128 + 64],
                        vsl[hh * 64:hh * 64 + 64, tkc * 128:(tkc + 1) * 128],
                        B.ident64.ap()[hh * 64:hh * 64 + 64, :]),
                        deps=[d for d in dd if d])
                    if tkc % 8 == 7:
                        toks = []
                        for j in range(8):
                            toks.append(p.do("vector", lambda v, hh=hh, j=j,
                                             tkc=tkc, pb=pb: v.tensor_copy(
                                out=vtm2(pb, hh, tkc - 7 + j),
                                in_=B.psT[0:128, j * 128:j * 128 + 64]),
                                deps=[tt]))
                        vt_prev = toks[-1]
                        vts.append(toks[-1])
            psT_busy = [vt_prev, vt_prev]
            vt_busy[pb] = vt_prev

            units = [(hh, w, r) for hh in range(2) for w in range(8)
                     for r in range(2)]

            def stage_A(idx):
                hh, w, r = units[idx]
                sb = (it + idx) % 2
                sA = sb * 512
                c0_ = 240 + sb * 3
                r0 = hh * 64
                tq0 = w * 256 + r * 128
                if w == 0:
                    k0 = 0
                    Wk = 256 if r == 1 else 128
                    moff = 1024 + r * 256
                else:
                    k0 = (w - 1) * 256 + (0 if r == 0 else 128)
                    Wk = 384
                    moff = r * 512 + r * 128
                dd = rope_done + [psA_busy[sb]]
                last = p.do("tensor", lambda t, r0=r0, tq0=tq0, k0=k0, Wk=Wk,
                            sA=sA, qsl=qsl, ksl=ksl: t.matmul(
                    B.psA[0:128, sA:sA + Wk],
                    qsl[r0:r0 + 64, tq0:tq0 + 128],
                    ksl[r0:r0 + 64, k0:k0 + Wk],
                    start=True, stop=True), deps=dd)
                t_mk = p.do("vector", lambda v, Wk=Wk, moff=moff, sA=sA:
                            v.tensor_tensor(
                    out=B.psA[0:128, sA:sA + Wk],
                    in0=B.psA[0:128, sA:sA + Wk],
                    in1=lm_sb.ap()[:, moff:moff + Wk], op=ALU.add),
                    deps=[last, t_lm])
                t_mx = p.do("vector", lambda v, Wk=Wk, sA=sA, c0_=c0_:
                            v.tensor_reduce(
                    out=B.cols.ap()[:, c0_:c0_ + 1].rearrange(
                        "p one -> p one ()"),
                    in_=B.psA[0:128, sA:sA + Wk].rearrange("p w -> p () w"),
                    axis=mybir.AxisListType.X, op=ALU.max, negate=True),
                    deps=[t_mk])
                return dict(sb=sb, sA=sA, c0_=c0_, hh=hh, w=w, r=r, r0=r0,
                            tq0=tq0, k0=k0, Wk=Wk, t_mx=t_mx)

            def stage_B(st_):
                sb, sA, c0_, Wk = st_["sb"], st_["sA"], st_["c0_"], st_["Wk"]
                sG = sb * 1024
                t_ex = p.do("scalar", lambda s, Wk=Wk, sA=sA, sG=sG, c0_=c0_:
                            s.activation(
                    out=B.stg.ap()[:, sG:sG + Wk],
                    in_=B.psA[0:128, sA:sA + Wk],
                    func=AF.Exp, bias=B.cols.ap()[:, c0_:c0_ + 1],
                    accum_out=B.cols.ap()[:, c0_ + 1:c0_ + 2]),
                    deps=[st_["t_mx"], pt_busy[sb]])
                psA_busy[sb] = t_ex
                t_rd = p.do("vector", lambda v, c0_=c0_: v.reciprocal(
                    out=B.cols.ap()[:, c0_ + 2:c0_ + 3],
                    in_=B.cols.ap()[:, c0_ + 1:c0_ + 2]), deps=[t_ex])
                t_pn = p.do("vector", lambda v, Wk=Wk, sG=sG, c0_=c0_:
                            v.tensor_scalar(
                    out=B.stg.ap()[:, sG:sG + Wk],
                    in0=B.stg.ap()[:, sG:sG + Wk],
                    scalar1=B.cols.ap()[:, c0_ + 2:c0_ + 3],
                    scalar2=None, op0=ALU.mult), deps=[t_rd])
                st_["t_pn"] = t_pn
                st_["sG"] = sG
                return st_

            def stage_C(st_):
                sb, sG, Wk = st_["sb"], st_["sG"], st_["Wk"]
                sP = sb * 512
                nkc = (Wk + 127) // 128
                tts = []
                for j in range(nkc):
                    dd2 = [st_["t_pn"]]
                    if j == 0 and psT_busy[sb] is not None:
                        dd2.append(psT_busy[sb])
                    tts.append(p.do("tensor", lambda t, j=j, sP=sP, sG=sG:
                                    t.transpose(
                        B.psT[0:128, sP + j * 128:sP + (j + 1) * 128],
                        B.stg.ap()[:, sG + j * 128:sG + (j + 1) * 128],
                        B.ident_bf.ap()[:, :]), deps=dd2))
                tev = p.do("scalar", lambda s, nkc=nkc, sb=sb, sP=sP:
                           s.activation(
                    out=B.slot(36 + sb)[:, 0:nkc * 128],
                    in_=B.psT[0:128, sP:sP + nkc * 128], func=AF.Copy),
                    deps=[tts[-1]])
                psT_busy[sb] = tev
                pt_busy[sb] = tev
                st_["tev"] = tev
                st_["nkc"] = nkc
                return st_

            def stage_D(st_, first_unit):
                sb, nkc, hh = st_["sb"], st_["nkc"], st_["hh"]
                k0, tq0 = st_["k0"], st_["tq0"]
                avb = B.ps4.ap() if sb == 0 else B.ps5.ap()
                dd3 = [st_["tev"], vts[-1]]
                if av_busy[sb] is not None:
                    dd3.append(av_busy[sb])
                lastav = None
                for j in range(nkc):
                    tkc = k0 // 128 + j
                    lastav = p.do("tensor", lambda t, hh=hh, j=j, avb=avb,
                                  nkc=nkc, tkc=tkc, pb=pb, sb=sb: t.matmul(
                        avb[0:64, 0:128], vtm2(pb, hh, tkc),
                        B.slot(36 + sb)[:, j * 128:(j + 1) * 128],
                        start=(j == 0), stop=(j == nkc - 1)),
                        deps=dd3 if j == 0 else [])
                t_ye = p.do("scalar", lambda s, hh=hh, tq0=tq0, avb=avb:
                            s.activation(
                    out=B.slot(38)[hh * 64:hh * 64 + 64, tq0:tq0 + 128],
                    in_=avb[0:64, 0:128], func=AF.Copy),
                    deps=[lastav, y_pair_toks[-1] if first_unit else None])
                av_busy[sb] = t_ye

            states = {}
            NU = len(units)
            for idx in range(NU + 2):
                if idx < NU:
                    states[idx] = stage_A(idx)
                if idx >= 1 and idx - 1 < NU:
                    stage_B(states[idx - 1])
                    stage_C(states[idx - 1])
                if idx >= 2:
                    stage_D(states[idx - 2],
                            first_unit=(idx - 2 == 0 and it >= 1))
                    del states[idx - 2]
            it += NU
            tok = p.dma("sync", ydram2[pj * 128:(pj + 1) * 128, :], B.slot(38),
                        deps=[av_busy[0], av_busy[1]], sem="l2fy")
            y_pair_toks.append(tok)
            qkv_busy[pb] = av_busy[0]

        y_ld = []
        for c in range(8):
            y_ld.append(p.dma("scalar", B.slot(16 + c),
                              ydram2[c * 128:(c + 1) * 128, :],
                              deps=[y_pair_toks[c], psA_busy[0], psA_busy[1]],
                              sem=f"lnc{c}"))

        ws_o = WStream(pr, wo[:, i * 8 * 8 * 128:(i + 1) * 8 * 8 * 128], 8)
        ao_toks = {}

        def evac_ao(mb, n0, ps, dep, _x2t=x2_tok):
            tok = p.do("vector", lambda v, ps=ps, mb=mb, n0=n0: v.tensor_add(
                out=x2[mb][:, n0:n0 + 1024], in0=ps[:, 0:1024],
                in1=x2[mb][:, n0:n0 + 1024]), deps=[dep] + _x2t[mb])
            ao_toks.setdefault(mb, []).append(tok)
            return tok

        def mm_deps_ao(mb, kc, n0, _yld=y_ld):
            return [_yld[kc]]

        ev_ao = mm_evac_stream(pr, Mb=8, Kc=8, ws=ws_o,
                               rhs=lambda kc, a, b_: B.slot(16 + kc)[:, a:b_],
                               evac=evac_ao, extra_mm_deps=mm_deps_ao,
                               region_init=[psA_busy[0], psA_busy[1]])
        for c in range(8):
            x2_tok[c] = [ao_toks[c][-1]]

        if halves == 1:
            break
        sh_toks = {}
        for c in range(8):
            if c < 4:
                tok = p.do("vector", lambda v, c=c: v.tensor_copy(
                    out=B.slot(24 + c), in_=x2[c]), deps=x2_tok[c] +
                    [vt_busy[0], vt_busy[1]])
            else:
                t0 = p.do("vector", lambda v, c=c: v.memset(
                    B.slot(24 + c)[:, 0:1], 0.0), deps=x2_tok[c] +
                    [vt_busy[0], vt_busy[1]])
                tok = p.do("vector", lambda v, c=c: v.tensor_copy(
                    out=B.slot(24 + c)[:, 1:T], in_=x2[c][:, 0:T - 1]),
                    deps=[t0])
            sh_toks[c] = tok
        vt_busy = [None, None]

        t_ln3, _ = emit_ln(pr, None, xn_slot=16, xb_slot=None, x2_slot=32,
                           gcol=lambda c: ccol(i, 1 + c),
                           bcol=lambda c: ccol(i, 9 + c),
                           src_sbuf=[B.slot(24 + c) for c in range(8)],
                           src_deps=lambda c: sh_toks[c], cols_dep=[t_cols])
        xn3 = [B.slot(16 + c) for c in range(8)]

        x2_sp = []
        for c in range(8):
            x2_sp.append(p.dma("sync", x2dram[c * 128:(c + 1) * 128, :],
                               x2[c], deps=x2_tok[c] + [t_ln3[c]],
                               sem=f"sp{c}"))

        def hslot(mb):
            return B.slot(24 + mb) if mb < 16 else B.slot(mb - 16)

        ws_1 = WStream(pr, w1[:, i * 44 * 8 * 128:(i + 1) * 44 * 8 * 128], 8)
        h_toks = {}
        stg_tok = [None, None]

        def evac_w1(mb, n0, ps, dep, _i=i):
            hmb = mb // 2
            sbi = (n0 // 1024)
            if mb % 2 == 0:
                tok = p.do("vector", lambda v, ps=ps, hmb=hmb, n0=n0:
                           v.tensor_scalar(
                    out=hslot(hmb)[:, n0:n0 + 1024], in0=ps[:, 0:1024],
                    scalar1=ccol(_i, 17 + hmb), scalar2=None, op0=ALU.add),
                    deps=[dep, t_cols] + ([h_toks[hmb][-1]] if hmb in h_toks
                                          else []) +
                    (x2_sp if hmb >= 16 else []))
                h_toks.setdefault(hmb, []).append(tok)
                return tok
            t_g = p.do("scalar", lambda s, ps=ps, hmb=hmb, n0=n0, sbi=sbi:
                       s.activation(
                out=B.stg.ap()[:, sbi * 1024:sbi * 1024 + 1024],
                in_=ps[:, 0:1024], func=AF.Gelu, bias=ccol(_i, 39 + hmb)),
                deps=[dep, t_cols, stg_tok[sbi]])
            tok = p.do("vector", lambda v, hmb=hmb, n0=n0, sbi=sbi: v.tensor_mul(
                out=hslot(hmb)[:, n0:n0 + 1024],
                in0=hslot(hmb)[:, n0:n0 + 1024],
                in1=B.stg.ap()[:, sbi * 1024:sbi * 1024 + 1024]),
                deps=[t_g, h_toks[hmb][-1]])
            h_toks[hmb].append(tok)
            stg_tok[sbi] = tok
            return tok

        def mm_deps_w1(mb, kc, n0):
            return [t_ln3[kc]]

        ev_w1 = mm_evac_stream(pr, Mb=44, Kc=8, ws=ws_1,
                               rhs=lambda kc, a, b_: xn3[kc][:, a:b_],
                               evac=evac_w1, extra_mm_deps=mm_deps_w1,
                               region_init=[t_ln3[-1], ev_ao[-1]])

        ws_2 = WStream(pr, w2[:, i * 8 * 22 * 128:(i + 1) * 8 * 22 * 128], 22)
        d_ev = {}

        def evac_w2(mb, n0, ps, dep):
            tok = p.do("scalar", lambda s, ps=ps, mb=mb, n0=n0: s.activation(
                out=B.slot(16 + mb)[:, n0:n0 + 1024], in_=ps[:, 0:1024],
                func=AF.Copy), deps=[dep, t_ln3[-1]])
            d_ev.setdefault(mb, []).append(tok)
            return tok

        def mm_deps_w2b(mb, kc, n0):
            return [h_toks[kc][-1]]

        ev_w2 = mm_evac_stream(pr, Mb=8, Kc=22, ws=ws_2,
                               rhs=lambda kc, a, b_: hslot(kc)[:, a:b_],
                               evac=evac_w2, extra_mm_deps=mm_deps_w2b,
                               region_init=[ev_w1[-1]])
        f_toks = {}
        res_ld = [None, None]
        for mb in range(8):
            for n0 in (0, 1024):
                sbi = (mb * 2 + n0 // 1024) % 2
                lt = p.dma("sync", B.tmpf.ap()[:, sbi * 1024:sbi * 1024 + 1024],
                           x2dram[mb * 128:(mb + 1) * 128, n0:n0 + 1024],
                           deps=[x2_sp[mb], ev_w2[-1]] +
                           ([res_ld[sbi]] if res_ld[sbi] else []),
                           sem=f"res{sbi}")
                tok = p.do("vector", lambda v, mb=mb, n0=n0, sbi=sbi, _i=i:
                           v.scalar_tensor_tensor(
                    out=x2[mb][:, n0:n0 + 1024],
                    in0=B.slot(16 + mb)[:, n0:n0 + 1024],
                    scalar=ccol(_i, 61 + mb),
                    in1=B.tmpf.ap()[:, sbi * 1024:sbi * 1024 + 1024],
                    op0=ALU.add, op1=ALU.add),
                    deps=[lt, d_ev[mb][-1], t_cols])
                f_toks.setdefault(mb, []).append(tok)
                res_ld[sbi] = tok
        for c in range(8):
            x2_tok[c] = [f_toks[c][-1]]
        psA_busy = [ev_w2[-1], ev_w2[-1]]
        prev_cols = f_toks[7][-1]
        rope_war = [f_toks[7][-1]]

    def delta_src(c):
        return (x2[c], x2_tok[c])

    st = emit_residual_add(pr, x_in, x_out, delta_src, macaron_scale,
                           lambda c: [],
                           first_deps=psA_busy +
                           [x2_tok[c][-1] for c in range(8)])
    return st


def host_pack_l2_ffn(ins):
    DEP, DIM, INNER = 6, 1024, 1024
    DI2 = 2816
    wqkv = ins["L2_ffn_Wqkv"].astype(np.float32)
    perm = []
    for j in range(8):
        for part in range(3):
            base = part * 1024
            perm += list(range(base + j * 128, base + j * 128 + 64 + 0))
    perm = []
    for j in range(8):
        for part in range(3):
            base = part * 1024
            perm += list(range(base + (2 * j) * 64, base + (2 * j) * 64 + 64))
            perm += list(range(base + (2 * j + 1) * 64, base + (2 * j + 1) * 64 + 64))
    wq_all = np.concatenate([pack_lhsT(np.ascontiguousarray(wqkv[d][:, perm]))
                             for d in range(DEP)], 1)
    wo_all = np.concatenate([pack_lhsT(ins["L2_ffn_Wo"][d].astype(np.float32))
                             for d in range(DEP)], 1)
    w1l, w2l = [], []
    for d in range(DEP):
        W1 = ins["L2_ffn_W1"][d].astype(np.float32)
        a = pad_to(W1[:, 0:2730], 1024, DI2)
        g = pad_to(W1[:, 2730:5460], 1024, DI2)
        cols = []
        for mb in range(22):
            cols.append(a[:, mb * 128:(mb + 1) * 128])
            cols.append(g[:, mb * 128:(mb + 1) * 128])
        w1l.append(pack_lhsT(np.concatenate(cols, 1)))
        w2l.append(pack_lhsT(pad_to(ins["L2_ffn_W2"][d].astype(np.float32),
                                    DI2, 1024)))
    colsd = np.zeros((128, DEP * 112), np.float32)
    for d in range(DEP):
        o = d * 112
        qs = ins["L2_ffn_q_scale"][d].astype(np.float32)
        ks = ins["L2_ffn_k_scale"][d].astype(np.float32)
        colsd[:, o:o + 1] = np.tile(qs * ks, 2)[:, None]
        colsd[:, o + 1:o + 9] = pack_cols(ins["L2_ffn_ln_g"][d].astype(np.float32))
        colsd[:, o + 9:o + 17] = pack_cols(ins["L2_ffn_ln_b"][d].astype(np.float32))
        b1 = ins["L2_ffn_b1"][d].astype(np.float32)
        b1a = pad_to(b1[0:2730], DI2)
        b1g = pad_to(b1[2730:5460], DI2)
        colsd[:, o + 17:o + 39] = pack_cols(b1a)
        colsd[:, o + 39:o + 61] = pack_cols(b1g)
        colsd[:, o + 61:o + 69] = pack_cols(ins["L2_ffn_b2"][d].astype(np.float32))
    ii = np.arange(128)
    lm = np.zeros((128, 1536), np.float32)
    NEG = -1e9
    for r in range(2):
        m = np.zeros((128, 512), np.float32)
        for c in range(4):
            k = r - c
            if k == 1 or k == -3:
                m[:, c * 128:(c + 1) * 128] = NEG
            elif k == 0:
                m[:, c * 128:(c + 1) * 128] = np.where(
                    ii[:, None] > ii[None, :], NEG, 0.0)
            elif k == -2:
                m[:, c * 128:(c + 1) * 128] = np.where(
                    ii[:, None] < ii[None, :], NEG, 0.0)
        lm[:, r * 512:(r + 1) * 512] = m
    lm[:, 1024:1152] = np.where(ii[:, None] < ii[None, :], NEG, 0.0)
    lm[:, 1280:1408] = 0.0
    lm[:, 1408:1536] = np.where(ii[:, None] < ii[None, :], NEG, 0.0)
    DH = 64
    half = np.arange(0, DH, 2, dtype=np.float64)
    inv_freq = 1.0 / (10000.0 ** (half / DH))
    t = np.arange(T, dtype=np.float64)
    freqs = np.concatenate([t[:, None] * inv_freq[None, :]] * 2, 1)
    base_scale = (half + 0.4 * DH) / (1.4 * DH)
    power = (t - T // 2) / 128.0
    scale = np.concatenate([base_scale[None, :] ** power[:, None]] * 2, 1)
    cos, sin = np.cos(freqs), np.sin(freqs)
    CS = (cos * scale).T
    SS = (sin * scale).T
    CSi = (cos / scale).T
    SSi = (sin / scale).T
    rope = np.concatenate([np.tile(a, (2, 1))[None] for a in
                           (CS, SS, CSi, SSi)], 0)
    rope = rope.transpose(1, 0, 2).reshape(128, 4 * T).astype(BF)
    return {
        "L2f_wqkvp": wq_all,
        "L2f_wop": wo_all,
        "L2f_w1p": np.concatenate(w1l, 1),
        "L2f_w2p": np.concatenate(w2l, 1),
        "L2f_rope": np.ascontiguousarray(rope),
        "L2f_cols": colsd,
        "L2f_lm": lm.astype(BF),
    }


def build_l2_ffn_program(n_sub=6, halves=2):
    pr = Prog()
    pr.load_consts()
    x_in = pr.din("x_cur", [D, T])
    x_out = pr.dout("x_next", [D, T])
    st = emit_l2_ffn(pr, x_in, x_out, n_sub=n_sub, halves=halves)
    pr.out_tokens = st
    pr.finish()
    return pr


def run_sublayer_l2ffn(x_full, ins, n_sub=6):
    pr = build_l2_ffn_program(n_sub)
    packed = host_pack_l2_ffn(ins)
    consts = host_consts()
    in_maps = []
    for core in range(8):
        m = dict(consts)
        m.update(packed)
        m["x_cur"] = np.ascontiguousarray(x_full[core].T)
        in_maps.append(m)
    res = run_bass_kernel_spmd(pr.nc, in_maps, core_ids=list(range(8)))
    out = np.stack([res.results[c]["x_next"].T for c in range(8)])
    pr.close()
    return out


def build_megakernel(n_sublayers=12):
    pr = Prog()
    pr.load_consts()
    x_in = pr.din("x_cur", [D, T])
    x_out = pr.dout("x_next", [D, T])
    xA = pr.dscratch("xA", [D, T], F32)
    xB = pr.dscratch("xB", [D, T], F32)
    seq = [
        (emit_l0_ffn, {}), (emit_l0_attn, {}), (emit_l0_ffn, {}),
        (emit_l1_ffn, {}), (emit_l1_attn, {}), (emit_l1_ffn, {}),
        (emit_l2_ffn, {}), (emit_l2_attn, {}), (emit_l2_ffn, {}),
        (emit_l3_ffn, {}), (emit_l3_attn, {}), (emit_l3_ffn, {}),
    ][:n_sublayers]
    cur = x_in
    nxt = xA
    st = None
    for idx, (fn, kw) in enumerate(seq):
        last = (idx == len(seq) - 1)
        dst = x_out if last else nxt
        st = fn(pr, cur, dst, **kw)
        pr.p.barrier()
        cur = dst
        nxt = xB if nxt is xA else xA
    pr.out_tokens = st
    pr.finish()
    return pr


_PROGS = {}


def _get_prog(name, builder):
    if name not in _PROGS:
        _PROGS[name] = builder()
    return _PROGS[name]


def _host_all_packs(ins):
    packs = {}
    packs.update(host_pack_l0_ffn(ins))
    packs.update(host_pack_l0_attn(ins))
    packs.update(host_pack_l1_ffn(ins))
    packs.update(host_pack_l1_attn(ins))
    packs.update(host_pack_l2_ffn(ins))
    packs.update(host_pack_l2_attn(ins))
    packs.update(host_pack_l3_ffn(ins))
    packs.update(host_pack_l3_attn(ins))
    packs["L1a_caus"] = np.triu(np.full((128, 128), -1e9, np.float32), 1)
    return packs


def kernel(**inputs):
    consts = host_consts()
    packs = _host_all_packs(inputs)
    pr = _get_prog("mega", build_megakernel)
    x = inputs["L0_attn_x"].astype(np.float32)
    base = dict(consts)
    base.update(packs)
    in_maps = []
    for c in range(8):
        m = dict(base)
        m["x_cur"] = np.ascontiguousarray(x[c].T)
        in_maps.append(m)
    res = run_bass_kernel_spmd(pr.nc, in_maps, core_ids=list(range(8)))
    return np.stack([res.results[c]["x_next"].T for c in range(8)])


def kernel_unfused(**inputs):
    ins = inputs
    consts = host_consts()
    packs = {}
    packs.update(host_pack_l0_ffn(ins))
    packs.update(host_pack_l0_attn(ins))
    packs.update(host_pack_l1_ffn(ins))
    packs.update(host_pack_l1_attn(ins))
    packs.update(host_pack_l2_ffn(ins))
    packs.update(host_pack_l2_attn(ins))
    packs.update(host_pack_l3_ffn(ins))
    packs.update(host_pack_l3_attn(ins))
    packs["L1a_caus"] = np.triu(np.full((128, 128), -1e9, np.float32), 1)

    progs = [
        ("l0f", build_l0_ffn_program),
        ("l0a", build_l0_attn_program),
        ("l1f", build_l1_ffn_program),
        ("l1a", build_l1_attn_program),
        ("l2f", build_l2_ffn_program),
        ("l2a", build_l2_attn_program),
        ("l3f", build_l3_ffn_program),
        ("l3a", build_l3_attn_program),
    ]
    seq = ["l0f", "l0a", "l0f", "l1f", "l1a", "l1f",
           "l2f", "l2a", "l2f", "l3f", "l3a", "l3f"]
    builders = dict(progs)

    x = inputs["L0_attn_x"].astype(np.float32)
    xs = [np.ascontiguousarray(x[c].T) for c in range(8)]
    base = dict(consts)
    base.update(packs)
    for name in seq:
        pr = _get_prog(name, builders[name])
        in_maps = []
        for c in range(8):
            m = dict(base)
            m["x_cur"] = xs[c]
            in_maps.append(m)
        res = run_bass_kernel_spmd(pr.nc, in_maps, core_ids=list(range(8)))
        xs = [np.ascontiguousarray(res.results[c]["x_next"]) for c in range(8)]
    out = np.stack([xs[c].T for c in range(8)])
    return out
